# version 05
# baseline (speedup 1.0000x reference)
from contextlib import ExitStack

import numpy as np
import ml_dtypes

import concourse.bass as bass
import concourse.mybir as mybir
from concourse.bass_utils import run_bass_kernel_spmd

BFnp = ml_dtypes.bfloat16
F32 = mybir.dt.float32
BF16 = mybir.dt.bfloat16
ALU = mybir.AluOpType
AF = mybir.ActivationFunctionType
AX = mybir.AxisListType

NCORES = 8
CONV_ENG = "vector"
D = 2048
CC = D // 128
T = 8192
TLOC = T // NCORES
EPS = 1e-5

L0_HQ, L0_HK, L0_DH = 32, 8, 64
L1_H, L1_DH = 16, 128



def _rope0_tables():
    pos = np.arange(T, dtype=np.float64)
    inv = 1.0 / (10000.0 ** (np.arange(0, L0_DH, 2, dtype=np.float64) / L0_DH))
    fr = pos[:, None] * inv[None, :]
    c = np.cos(fr).T
    s = np.sin(fr).T
    return (np.concatenate([c, c], 0).astype(BFnp),
            np.concatenate([s, s], 0).astype(BFnp))


def _rope1_tables():
    half = L1_DH // 2
    pos = np.arange(T, dtype=np.float64)
    th = 10000.0 ** (-np.arange(half, dtype=np.float64) / half)
    ang = pos[:, None] * th[None, :]
    c = np.cos(ang).T
    s = np.sin(ang).T
    return (np.concatenate([c, c], 0).astype(BFnp),
            np.concatenate([s, s], 0).astype(BFnp))


def _causal_masks():
    m = np.zeros((4, 128, 512), dtype=BFnp)
    for i in range(4):
        d = i * 128
        k = np.arange(128)[:, None]
        q = np.arange(512)[None, :]
        m[i] = (k <= (q - d)).astype(BFnp)
    return m


_DEINT = np.concatenate([np.arange(0, 128, 2), np.arange(1, 128, 2)])


def _cc(a):
    return np.ascontiguousarray(a)


def prep_host(inp):
    x = np.asarray(inp["L0_attn_x"])
    wq = np.asarray(inp["L0_attn_Wq"])
    wkv = np.asarray(inp["L0_attn_Wkv"])
    wo = np.asarray(inp["L0_attn_Wo"])
    gw = np.asarray(inp["L0_ffn_gate_w"])
    gb = np.asarray(inp["L0_ffn_gate_bias"])
    sgw = np.asarray(inp["L0_ffn_shared_gw"])
    sdw = np.asarray(inp["L0_ffn_shared_dw"])
    upw = np.asarray(inp["L0_ffn_up_w"])
    dnw = np.asarray(inp["L0_ffn_down_w"])
    wqkv1 = np.asarray(inp["L1_attn_w_qkv"])
    bqkv1 = np.asarray(inp["L1_attn_b_qkv"])
    wo1 = np.asarray(inp["L1_attn_w_out"])
    bo1 = np.asarray(inp["L1_attn_b_out"])
    wgu1 = np.asarray(inp["L1_ffn_w_gate_up"])
    wdn1 = np.asarray(inp["L1_ffn_w_down"])

    cos0, sin0 = _rope0_tables()
    cos1, sin1 = _rope1_tables()
    masks = _causal_masks()
    ones_col = np.ones((128, 1), dtype=BFnp)
    ones_row = np.ones((1, 128), dtype=BFnp)
    idn_bf = np.eye(128, dtype=BFnp)
    idn_f32 = np.eye(128, dtype=np.float32)

    wqkv1T = wqkv1.T

    def _rot(w, half):
        return np.concatenate([-w[:, half:], w[:, :half]], axis=1)

    per_core = []
    for r in range(NCORES):
        d = {}
        d["xin"] = _cc(x[0, TLOC * r:TLOC * (r + 1), :].T)
        wq_r = wq[:, 256 * r:256 * (r + 1)]
        wk_r = wkv[:, 64 * r:64 * r + 64]
        wv_r = wkv[:, 512 + 64 * r:512 + 64 * r + 64]
        qrot = np.concatenate([_rot(wq_r[:, 64 * i:64 * (i + 1)], 32)
                               for i in range(4)], axis=1)
        krot = _rot(wk_r, 32)
        d["wall0"] = _cc(np.concatenate(
            [wq_r, wk_r, wv_r, qrot, krot], axis=1).astype(np.float32))
        d["wo0"] = _cc(wo)
        d["gw"] = _cc(gw.T)
        d["gbbc"] = _cc(np.broadcast_to(gb, (128, 8)))
        d["sgw"] = _cc(sgw.T)
        d["sdw"] = _cc(sdw.T)
        d["upw"] = _cc(upw.transpose(0, 2, 1))
        d["dnw"] = _cc(dnw.transpose(0, 2, 1))
        cols, bias = [], []
        rcols, rbias = [], []
        for base, perm in ((0, True), (D, True), (2 * D, False)):
            for h in (2 * r, 2 * r + 1):
                c0 = base + 128 * h
                blk = wqkv1T[:, c0:c0 + 128]
                bb = bqkv1[c0:c0 + 128]
                if perm:
                    blk = blk[:, _DEINT]
                    bb = bb[_DEINT]
                    rcols.append(_rot(blk, 64))
                    rbias.append(np.concatenate([-bb[64:], bb[:64]]))
                cols.append(blk)
                bias.append(bb)
        d["wall1"] = _cc(np.concatenate(cols + rcols, axis=1).astype(np.float32))
        bias12 = np.stack(bias + rbias + [np.zeros(128)] * 2, axis=1)
        d["bqkv1"] = _cc(bias12.astype(np.float32))
        d["wo1"] = _cc(wo1.T)
        d["bo1"] = _cc(bo1[:, None])
        d["wgu1"] = _cc(wgu1.T)
        d["wdn1"] = _cc(wdn1.T)
        d["cos0"], d["sin0"] = cos0, sin0
        d["cos1"], d["sin1"] = cos1, sin1
        d["masks"] = masks
        d["ones_col"] = ones_col
        d["ones_row"] = ones_row
        d["idn_bf"] = idn_bf
        d["idn_f32"] = idn_f32
        per_core.append(d)
    return per_core



class Ctr:
    def __init__(self):
        self.v = {}

    def bump(self, name, n=1):
        self.v[name] = self.v.get(name, 0) + n
        return self.v[name]

    def get(self, name):
        return self.v.get(name, 0)



class Sched:

    ENGINES = ("sync", "tensor", "vector", "scalar", "gpsimd")

    def __init__(self):
        self.ops = {e: [] for e in self.ENGINES}
        self.cnt = {}

    def get(self, sem):
        return self.cnt.get(sem, 0)

    def emit(self, eng, fn, waits=(), inc=None):
        w = [(s, v) for (s, v) in waits if v > 0]
        self.ops[eng].append((fn, w, inc))
        if inc is not None:
            self.cnt[inc[0]] = self.cnt.get(inc[0], 0) + inc[1]
            return self.cnt[inc[0]]
        return None

    def flush(self, blk, sems):
        for eng in self.ENGINES:
            ops = self.ops[eng]
            if not ops:
                continue

            def make(ops, eng=eng):
                def body(e):
                    waited = {}
                    pe = e if eng in ("tensor", "sync") else _EngProxy(e)
                    for fn, waits, inc in ops:
                        for s, v in waits:
                            if waited.get(s, 0) < v:
                                e.wait_ge(sems[s], v)
                                waited[s] = v
                        inst = fn(pe)
                        if inc is not None:
                            inst.then_inc(sems[inc[0]], inc[1])
                return body

            getattr(blk, eng)(make(ops))
        self.ops = {e: [] for e in self.ENGINES}


class _EngProxy:

    _PASS = ("wait_ge", "nop", "drain", "dma_start", "then_inc")

    def __init__(self, eng):
        self._e = eng
        self._written = []
        self._read = []

    @staticmethod
    def _iv(ap):
        lo = ap.offset
        hi = lo + 1
        for step, count in ap.ap:
            hi += (count - 1) * step
        return (ap.tensor.name, lo, hi)

    @staticmethod
    def _hits(iv, lst):
        n, lo, hi = iv
        for (n2, lo2, hi2) in lst:
            if n2 == n and lo < hi2 and lo2 < hi:
                return True
        return False

    def __getattr__(self, name):
        meth = getattr(self._e, name)
        if name in self._PASS or name.startswith("_"):
            return meth

        def call(*args, **kwargs):
            aps = [a for a in list(args) + list(kwargs.values())
                   if isinstance(a, bass.AP)]
            if aps:
                wr = self._iv(aps[0])
                rds = [self._iv(a) for a in aps[1:]]
                hazard = (self._hits(wr, self._written)
                          or self._hits(wr, self._read)
                          or any(self._hits(r, self._written) for r in rds))
                if hazard or len(self._written) + len(self._read) > 96:
                    self._e.drain()
                    self._written.clear()
                    self._read.clear()
                self._written.append(wr)
                self._read.extend(rds)
            return meth(*args, **kwargs)
        return call






def ln_emit(S, x_sb, out_sb, xsq, t1, srow, rowbf, abc, bbc,
            ps_stats, ps_bc, oc_sb, orw_sb, sd_consts, wx=()):
    CH = 1024
    m_off, tmp_off, a_off, b_off = 0, 1024, 2048, 3072
    v0 = S.get("sv")
    for c in range(CC):
        S.emit("vector", (lambda e, c=c: e.tensor_copy(
               out_sb[:, CH * c:CH * (c + 1)], x_sb[:, CH * c:CH * (c + 1)])),
               waits=list(wx))
        S.emit("vector", (lambda e, c=c: e.tensor_tensor(
               xsq[:, CH * c:CH * (c + 1)], out_sb[:, CH * c:CH * (c + 1)],
               out_sb[:, CH * c:CH * (c + 1)], ALU.mult)), inc=("sv", 1))
    for c in range(CC):
        for t in range(2):
            last = (c == CC - 1 and t == 1)
            S.emit("tensor", (lambda e, c=c, t=t: e.matmul(
                   ps_stats[0:1, 512 * t:512 * (t + 1)], oc_sb[:, 0:1],
                   out_sb[:, CH * c + 512 * t:CH * c + 512 * (t + 1)],
                   start=(c == 0), stop=(c == CC - 1))),
                   waits=[("sv", v0 + c + 1), ("sd", sd_consts)])
            S.emit("tensor", (lambda e, c=c, t=t: e.matmul(
                   ps_stats[0:1, 1024 + 512 * t:1536 + 512 * t], oc_sb[:, 0:1],
                   xsq[:, CH * c + 512 * t:CH * c + 512 * (t + 1)],
                   start=(c == 0), stop=(c == CC - 1))),
                   inc=("st", 1) if last else None)
    t_stats = S.get("st")
    for t in range(2):
        o = 512 * t
        S.emit("vector", (lambda e, o=o: e.tensor_scalar(
               srow[0:1, m_off + o:m_off + o + 512], ps_stats[0:1, o:o + 512],
               1.0 / D, None, ALU.mult)), waits=[("st", t_stats)])
        S.emit("vector", (lambda e, o=o: e.tensor_scalar(
               srow[0:1, tmp_off + o:tmp_off + o + 512],
               ps_stats[0:1, 1024 + o:1536 + o], 1.0 / D, EPS,
               ALU.mult, ALU.add)))
        S.emit("vector", (lambda e, o=o: e.tensor_tensor(
               srow[0:1, a_off + o:a_off + o + 512],
               srow[0:1, m_off + o:m_off + o + 512],
               srow[0:1, m_off + o:m_off + o + 512], ALU.mult)))
        S.emit("vector", (lambda e, o=o: e.tensor_tensor(
               srow[0:1, b_off + o:b_off + o + 512],
               srow[0:1, tmp_off + o:tmp_off + o + 512],
               srow[0:1, a_off + o:a_off + o + 512], ALU.subtract)))
        S.emit("vector", (lambda e, o=o: e.tensor_copy(
               srow[0:1, tmp_off + o:tmp_off + o + 512],
               srow[0:1, b_off + o:b_off + o + 512])), inc=("sv", 1))
    v_recip = S.get("sv")
    for t in range(2):
        o = 512 * t
        S.emit("scalar", (lambda e, o=o: e.activation(
               srow[0:1, b_off + o:b_off + o + 512],
               srow[0:1, tmp_off + o:tmp_off + o + 512], AF.Ln)),
               waits=[("sv", v_recip)])
        S.emit("scalar", (lambda e, o=o: e.activation(
               srow[0:1, a_off + o:a_off + o + 512],
               srow[0:1, b_off + o:b_off + o + 512], AF.Exp, scale=-0.5)),
               inc=("sc", 1))
    a_sqrt = S.get("sc")
    for t in range(2):
        o = 512 * t
        S.emit("vector", (lambda e, o=o: e.scalar_tensor_tensor(
               srow[0:1, b_off + o:b_off + o + 512],
               srow[0:1, m_off + o:m_off + o + 512], -1.0,
               srow[0:1, a_off + o:a_off + o + 512], ALU.mult, ALU.mult)),
               waits=[("sc", a_sqrt)])
        S.emit("vector", (lambda e, o=o: e.tensor_copy(
               rowbf[0:1, o:o + 512], srow[0:1, a_off + o:a_off + o + 512])))
        S.emit("vector", (lambda e, o=o: e.tensor_copy(
               rowbf[0:1, 1024 + o:1536 + o],
               srow[0:1, b_off + o:b_off + o + 512])), inc=("sv", 1))
    v_rows = S.get("sv")
    for i in range(4):
        o = 512 * i
        S.emit("tensor", (lambda e, o=o: e.matmul(
               ps_bc[:, o:o + 512], orw_sb[0:1, :], rowbf[0:1, o:o + 512],
               start=True, stop=True)),
               waits=[("sv", v_rows), ("sd", sd_consts)],
               inc=("st", 1) if i == 3 else None)
    t_bc = S.get("st")
    S.emit("vector", lambda e: e.tensor_copy(abc[:, :], ps_bc[:, 0:1024]),
           waits=[("st", t_bc)])
    S.emit("vector", lambda e: e.tensor_copy(bbc[:, :], ps_bc[:, 1024:2048]))
    for c in range(CC):
        S.emit("vector", (lambda e, c=c: e.tensor_tensor(
               t1[:, CH * (c % 2):CH * (c % 2 + 1)],
               out_sb[:, CH * c:CH * (c + 1)], abc[:, :], ALU.mult)))
        S.emit("vector", (lambda e, c=c: e.tensor_tensor(
               out_sb[:, CH * c:CH * (c + 1)],
               t1[:, CH * (c % 2):CH * (c % 2 + 1)], bbc[:, :], ALU.add)),
               inc=("sv", 1))
    return S.get("sv")


def ln_bufs(nc, ex, pref=""):
    b = {}
    b["xsq"] = ex.enter_context(nc.sbuf_tensor(pref + "xsq", [128, CC * 1024], BF16))
    b["t1"] = ex.enter_context(nc.sbuf_tensor(pref + "t1", [128, 2 * 1024], BF16))
    b["srow"] = ex.enter_context(nc.sbuf_tensor(pref + "srow", [1, 4096], F32))
    b["rowbf"] = ex.enter_context(nc.sbuf_tensor(pref + "rowbf", [1, 2048], BF16))
    b["abc"] = ex.enter_context(nc.sbuf_tensor(pref + "abc", [128, 1024], BF16))
    b["bbc"] = ex.enter_context(nc.sbuf_tensor(pref + "bbc", [128, 1024], BF16))
    return b



class WStream:

    def __init__(self, S, nc, ex, name, rows_chunks, maxw):
        self.S = S
        self.rc = rows_chunks
        self.maxw = maxw
        self.stg = ex.enter_context(
            nc.sbuf_tensor(name + "_stg", [128, 2 * rows_chunks * maxw], F32))
        self.bf = ex.enter_context(
            nc.sbuf_tensor(name + "_bf", [128, 2 * rows_chunks * maxw], BF16))
        self.n = 0
        self.last_use = {0: 0, 1: 0}
        self.sems = ("swa", "swb")

    def load(self, dram_slice_3d, ncols, rows_chunks=None):
        rc = rows_chunks if rows_chunks is not None else self.rc
        slot = self.n % 2
        self.n += 1
        sem = self.sems[slot]
        S = self.S
        W = self.maxw
        stg, bf = self.stg, self.bf
        base = slot * self.rc * W
        S.emit("sync", (lambda e: e.dma_start(
               out=stg[:, base:base + rc * ncols]
                   .rearrange("p (c f) -> p c f", f=ncols),
               in_=dram_slice_3d)),
               waits=[("st", self.last_use[slot])], inc=(sem, 16))
        d = S.get(sem)
        S.emit(CONV_ENG, (lambda e: e.tensor_copy(
               bf[:, base:base + rc * ncols], stg[:, base:base + rc * ncols])),
               waits=[(sem, d)], inc=("sp", 1))
        return (slot, S.get("sp"), ncols, rc)

    def tile(self, h, c, coloff, M):
        slot, _, ncols, rc = h
        base = slot * self.rc * self.maxw
        return self.bf[:, base + c * ncols + coloff:base + c * ncols + coloff + M]

    def conv_count(self, h):
        return h[1]

    CONV_SEM = "sp"

    def done(self, h):
        slot = h[0]
        self.last_use[slot] = self.S.get("st")



def build_attn(layer, Tt=T):
    nc = bass.Bass("TRN2", target_bir_lowering=False)
    HROWS = (4 * 64) if layer == 0 else (2 * 128)
    FW = 704 if layer == 0 else 1280
    TBL_P = 64 if layer == 0 else 128
    lnf = nc.dram_tensor("lnf", [D, Tt], BF16, kind="ExternalInput")
    ot = nc.dram_tensor("ot", [HROWS, Tt], BF16, kind="ExternalOutput")
    cos_d = nc.dram_tensor("cos", [TBL_P, Tt], BF16, kind="ExternalInput")
    sin_d = nc.dram_tensor("sin", [TBL_P, Tt], BF16, kind="ExternalInput")
    masks_d = nc.dram_tensor("masks", [4, 128, 512], BF16, kind="ExternalInput")
    idn_d = nc.dram_tensor("idn_bf", [128, 128], BF16, kind="ExternalInput")
    orw_d = nc.dram_tensor("ones_row", [1, 128], BF16, kind="ExternalInput")
    oc_d = nc.dram_tensor("ones_col", [128, 1], BF16, kind="ExternalInput")
    wall_d = nc.dram_tensor("wall", [D, FW], F32, kind="ExternalInput")
    bqkv_d = nc.dram_tensor("bqkv1", [128, 12], F32, kind="ExternalInput") \
        if layer == 1 else None
    lnf3 = lnf.ap().rearrange("(c p) t -> p c t", p=128)
    DH = 64 if layer == 0 else 128
    with ExitStack() as ex:
        sems = {n: ex.enter_context(nc.semaphore(n)) for n in
                ("sd", "sdw0", "sdw1", "sda0", "sda1", "sdo0", "sdo1",
                 "st", "sv", "sc", "sp")}
        blk = ex.enter_context(nc.Block())
        S = Sched()
        emit_attn(nc, S, blk, sems, layer, Tt, "a",
                  (lambda tw: lnf3[:, :, 512 * tw:512 * (tw + 1)]),
                  (lambda h, qc: ot[DH * h:DH * (h + 1),
                                    512 * qc:512 * (qc + 1)]),
                  cos_d, sin_d, masks_d, idn_d, orw_d, oc_d, wall_d, bqkv_d)
    return nc


def emit_attn(nc, S, blk, sems, layer, Tt, pref, lnf_tile, ot_dst,
              cos_d, sin_d, masks_d, idn_d, orw_d, oc_d, wall_d, bqkv_d,
              src_ready=()):
    if layer == 0:
        NH, DH = 4, 64
        scale = DH ** -0.5
        MCH = [("q", 0, 64, 0, 384), ("q", 1, 64, 64, 448),
               ("q", 2, 64, 128, 512), ("q", 3, 64, 192, 576),
               ("k", 0, 64, 256, 640), ("v", 0, 64, 320, None)]
        FW = 704
        TBL_P = 64
    else:
        NH, DH = 2, 128
        scale = DH ** -0.5
        MCH = [("q", 0, 128, 0, 768), ("q", 1, 128, 128, 896),
               ("k", 0, 128, 256, 1024), ("k", 1, 128, 384, 1152),
               ("v", 0, 128, 512, None), ("v", 1, 128, 640, None)]
        FW = 1280
        TBL_P = 128
    HROWS = NH * DH
    NKV = 1 if layer == 0 else NH
    QP = 64 if layer == 0 else 128
    NW = Tt // 512
    KT = Tt // 128
    VW = DH + 1

    with ExitStack() as ex:
        q_sb = ex.enter_context(nc.sbuf_tensor(pref + "q_sb", [QP, NH * Tt], BF16))
        k_sb = ex.enter_context(nc.sbuf_tensor(pref + "k_sb", [QP, NKV * Tt], BF16))
        v_sb = ex.enter_context(nc.sbuf_tensor(pref + "v_sb", [128, NKV * KT * VW], BF16))
        lnst = ex.enter_context(nc.sbuf_tensor(pref + "lnst", [128, 2 * CC * 512], BF16))
        cs_sb = ex.enter_context(nc.sbuf_tensor(pref + "cs_sb", [TBL_P, 2 * 2 * 512], BF16))
        masks_sb = ex.enter_context(nc.sbuf_tensor(pref + "masks_sb", [128, 4 * 512], BF16))
        w_bf = ex.enter_context(nc.sbuf_tensor(pref + "w_bf", [128, CC * FW], BF16))
        wstg = ex.enter_context(nc.sbuf_tensor(pref + "wstg", [128, 2 * FW], F32))
        b_sb = ex.enter_context(nc.sbuf_tensor(pref + "b_sb", [128, 16], F32))
        ropeA = ex.enter_context(nc.sbuf_tensor(pref + "ropeA", [128, 512], F32))
        ropeB = ex.enter_context(nc.sbuf_tensor(pref + "ropeB", [128, 512], F32))
        vstg = ex.enter_context(nc.sbuf_tensor(pref + "vstg", [128, 512], BF16))
        p_sb = ex.enter_context(nc.sbuf_tensor(pref + "p_sb", [128, 2 * 512], BF16))
        otst = ex.enter_context(nc.sbuf_tensor(pref + "otst", [128, 2 * 512], BF16))
        qsq = ex.enter_context(nc.sbuf_tensor(pref + "qsq", [128, 512], BF16))
        nrow = ex.enter_context(nc.sbuf_tensor(pref + "nrow", [1, 64], F32))
        nsc = ex.enter_context(nc.sbuf_tensor(pref + "nsc", [1, 64], F32))
        nbf = ex.enter_context(nc.sbuf_tensor(pref + "nbf", [1, 8], BF16))
        rrow = ex.enter_context(nc.sbuf_tensor(pref + "rrow", [1, 512], F32))
        rbf = ex.enter_context(nc.sbuf_tensor(pref + "rbf", [1, 512], BF16))
        rbc = ex.enter_context(nc.sbuf_tensor(pref + "rbc", [128, 512], F32))
        idn_sb = ex.enter_context(nc.sbuf_tensor(pref + "idn_sb", [128, 128], BF16))
        orw_sb = ex.enter_context(nc.sbuf_tensor(pref + "orw_sb", [1, 128], BF16))
        oc_sb = ex.enter_context(nc.sbuf_tensor(pref + "oc_sb", [128, 1], BF16))
        S.emit("sync", lambda e: e.dma_start(
               out=masks_sb[:, :].rearrange("p (m t) -> p m t", t=512),
               in_=masks_d.ap().rearrange("m p t -> p m t")), inc=("sd", 16))
        S.emit("sync", lambda e: e.dma_start(out=idn_sb[:, :], in_=idn_d[:, :]),
               inc=("sd", 16))
        S.emit("sync", lambda e: e.dma_start(out=orw_sb[:, :], in_=orw_d[:, :]),
               inc=("sd", 16))
        S.emit("sync", lambda e: e.dma_start(out=oc_sb[:, :], in_=oc_d[:, :]),
               inc=("sd", 16))
        if layer == 1:
            S.emit("sync", lambda e: e.dma_start(
                out=b_sb[:, 0:12], in_=bqkv_d[:, :]), inc=("sd", 16))
        sd_consts = S.get("sd")
        wconv = {}
        for c in range(CC):
            war = [("sv", wconv.get(c - 2, 0))]
            wsem = "sdw%d" % (c % 2)
            S.emit("sync", (lambda e, c=c:
                            e.dma_start(out=wstg[:, (c % 2) * FW:(c % 2 + 1) * FW],
                                        in_=wall_d[128 * c:128 * (c + 1), :])),
                   waits=war, inc=(wsem, 16))
            S.emit("vector", (lambda e, c=c:
                              e.tensor_copy(w_bf[:, FW * c:FW * (c + 1)],
                                            wstg[:, (c % 2) * FW:(c % 2 + 1) * FW])),
                   waits=[(wsem, S.get(wsem))], inc=("sv", 1))
            wconv[c] = S.get("sv")
        S.emit("vector", lambda e: e.memset(v_sb[:, :], 1.0), inc=("sv", 1))
        wconv_done_v = S.get("sv")

        with nc.psum_tensor(pref + "ps_qkv", [128, 4 * 512], F32) as ps_qkv, \
             nc.psum_tensor(pref + "ps_tr", [128, 512], BF16) as ps_tr:
            evac_cnt = {}
            win_mm = {}
            vstg_free = {}
            ucount = 0
            for tw in range(NW):
                asem = "sda%d" % (tw % 2)
                warw = ([("st", win_mm.get(tw - 2, 0))] if tw >= 2 else []) \
                    + list(src_ready)
                S.emit("gpsimd", (lambda e, tw=tw: e.dma_start(
                    out=lnst[:, (tw % 2) * CC * 512:(tw % 2 + 1) * CC * 512]
                        .rearrange("p (c t) -> p c t", t=512),
                    in_=lnf_tile(tw))),
                    waits=warw, inc=(asem, 16))
                S.emit("gpsimd", (lambda e, tw=tw: e.dma_start(
                    out=cs_sb[:, (tw % 2) * 1024:(tw % 2) * 1024 + 512],
                    in_=cos_d[:, 512 * tw:512 * (tw + 1)])),
                    waits=warw, inc=(asem, 16))
                S.emit("gpsimd", (lambda e, tw=tw: e.dma_start(
                    out=cs_sb[:, (tw % 2) * 1024 + 512:(tw % 2) * 1024 + 1024],
                    in_=sin_d[:, 512 * tw:512 * (tw + 1)])),
                    waits=warw, inc=(asem, 16))
                dma_done = S.get(asem)

                for mi, (kind, h, M, col, colrot) in enumerate(MCH):
                    sp = 2 * (ucount % 2)
                    ucount += 1
                    ps_m = ps_qkv[0:M, 512 * sp:512 * sp + 512]
                    ps_r = ps_qkv[0:M, 512 * (sp + 1):512 * (sp + 1) + 512]
                    chains = [(ps_m, col)] + ([(ps_r, colrot)]
                                              if colrot is not None else [])
                    for ps_o, coff in chains:
                        for c in range(CC):
                            w = []
                            if c == 0:
                                w = [(asem, dma_done), ("sv", wconv_done_v),
                                     ("sv", evac_cnt.get(sp, 0)),
                                     ("sd", sd_consts)]
                            S.emit("tensor", (lambda e, ps_o=ps_o, coff=coff, c=c,
                                              M=M, tw=tw: e.matmul(
                                ps_o,
                                w_bf[:, FW * c + coff:FW * c + coff + M],
                                lnst[:, (tw % 2) * CC * 512 + 512 * c:
                                     (tw % 2) * CC * 512 + 512 * (c + 1)],
                                start=(c == 0), stop=(c == CC - 1))),
                                waits=w,
                                inc=("st", 1) if c == CC - 1 else None)
                    mm_done = S.get("st")
                    win_mm[tw] = mm_done

                    cosap = cs_sb[0:M, (tw % 2) * 1024:(tw % 2) * 1024 + 512]
                    sinap = cs_sb[0:M, (tw % 2) * 1024 + 512:(tw % 2) * 1024 + 1024]
                    if kind in ("q", "k"):
                        dst = q_sb if kind == "q" else k_sb
                        base = (h * Tt + 512 * tw)
                        if layer == 1:
                            S.emit("vector", (lambda e, ps_m=ps_m, cosap=cosap,
                                              mi=mi, M=M: e.scalar_tensor_tensor(
                                   ropeA[0:M, :], ps_m, b_sb[:, mi:mi + 1], cosap,
                                   ALU.add, ALU.mult)),
                                   waits=[("st", mm_done), (asem, dma_done),
                                          ("sd", sd_consts)])
                            S.emit("vector", (lambda e, ps_r=ps_r, sinap=sinap,
                                              mi=mi, M=M: e.scalar_tensor_tensor(
                                   ropeB[0:M, :], ps_r, b_sb[:, 6 + mi:7 + mi],
                                   sinap, ALU.add, ALU.mult)))
                        else:
                            S.emit("vector", (lambda e, ps_m=ps_m, cosap=cosap,
                                              M=M: e.tensor_tensor(
                                   ropeA[0:M, :], ps_m, cosap, ALU.mult)),
                                   waits=[("st", mm_done), (asem, dma_done),
                                          ("sd", sd_consts)])
                            S.emit("vector", (lambda e, ps_r=ps_r, sinap=sinap,
                                              M=M: e.tensor_tensor(
                                   ropeB[0:M, :], ps_r, sinap, ALU.mult)))
                        S.emit("vector", (lambda e, dst=dst, base=base, M=M:
                               e.tensor_tensor(dst[0:M, base:base + 512],
                                               ropeA[0:M, :], ropeB[0:M, :],
                                               ALU.add)), inc=("sv", 1))
                        evac_cnt[sp] = S.get("sv")
                    else:
                        vw = [("st", mm_done), ("st", vstg_free.get(h, 0)),
                              ("sd", sd_consts)]
                        if layer == 1:
                            S.emit("vector", (lambda e, ps_m=ps_m, M=M, mi=mi:
                                   e.tensor_scalar(vstg[0:M, :], ps_m,
                                                   b_sb[:, mi:mi + 1], None,
                                                   ALU.add)),
                                   waits=vw, inc=("sv", 1))
                        else:
                            S.emit("vector", (lambda e, ps_m=ps_m, M=M:
                                   e.tensor_copy(vstg[0:M, :], ps_m)),
                                   waits=vw, inc=("sv", 1))
                        evac_cnt[sp] = S.get("sv")
                        vdone = S.get("sv")
                        cp_prev = 0
                        for j in range(4):
                            kt = 4 * tw + j
                            S.emit("tensor", (lambda e, j=j, M=M:
                                   e.transpose(ps_tr[0:128, 0:M],
                                               vstg[0:M, 128 * j:128 * (j + 1)],
                                               idn_sb[0:M, 0:M])),
                                   waits=[("sv", vdone), ("sv", cp_prev),
                                          ("sd", sd_consts)],
                                   inc=("st", 1))
                            trd = S.get("st")
                            vb = h * KT * VW
                            if layer == 0:
                                S.emit("vector", (lambda e, kt=kt, vb=vb:
                                       e.tensor_copy(
                                           v_sb[:, vb + VW * kt:vb + VW * kt + 64],
                                           ps_tr[0:128, 0:64])),
                                       waits=[("st", trd)], inc=("sv", 1))
                            else:
                                S.emit("vector", (lambda e, kt=kt, vb=vb:
                                       e.tensor_copy(
                                           v_sb[:, vb + VW * kt:vb + VW * kt + 64],
                                           ps_tr[0:128, 0:64])),
                                       waits=[("st", trd)])
                                S.emit("vector", (lambda e, kt=kt, vb=vb:
                                       e.tensor_copy(
                                           v_sb[:, vb + VW * kt + 65:
                                                vb + VW * kt + 129],
                                           ps_tr[0:128, 64:128])),
                                       inc=("sv", 1))
                            cp_prev = S.get("sv")
                        vstg_free[h] = S.get("st")
            S.flush(blk, sems)

        qkv_done_v = S.get("sv")
        with nc.psum_tensor(pref + "ps_n", [1, 512], F32) as ps_n, \
             nc.psum_tensor(pref + "ps_b1", [128, 512], F32) as ps_b1:
            units = [("q", h) for h in range(NH)] + [("k", h) for h in range(NKV)]
            for ui, (kind, h) in enumerate(units):
                src_sb = q_sb if kind == "q" else k_sb
                for tc in range(NW):
                    base = h * Tt + 512 * tc
                    S.emit("vector", (lambda e, src_sb=src_sb, base=base:
                           e.tensor_tensor(qsq[0:QP, :],
                                           src_sb[0:QP, base:base + 512],
                                           src_sb[0:QP, base:base + 512],
                                           ALU.mult)),
                           waits=[("sv", qkv_done_v)], inc=("sv", 1))
                    sq_done = S.get("sv")
                    S.emit("tensor", (lambda e: e.matmul(
                           ps_n[0:1, :], oc_sb[0:QP, 0:1], qsq[0:QP, :],
                           start=True, stop=True)),
                           waits=[("sv", sq_done), ("sd", sd_consts)],
                           inc=("st", 1))
                    nm_done = S.get("st")
                    S.emit("vector", (lambda e, tc=tc: e.tensor_reduce(
                           nrow[0:1, tc:tc + 1], ps_n[0:1, :], axis=AX.X,
                           op=ALU.max)),
                           waits=[("st", nm_done)], inc=("sv", 1))
                S.emit("vector", (lambda e, ui=ui: e.tensor_reduce(
                       nsc[0:1, ui:ui + 1], nrow[0:1, 0:NW], axis=AX.X,
                       op=ALU.max)), inc=("sv", 1))
            bcol0 = 12
            for h in range(NH):
                kvi = NH + (0 if layer == 0 else h)
                S.emit("vector", (lambda e, h=h, kvi=kvi: e.tensor_tensor(
                       nsc[0:1, 8 + h:9 + h], nsc[0:1, h:h + 1],
                       nsc[0:1, kvi:kvi + 1], ALU.mult)), inc=("sv", 1))
                sv_m = S.get("sv")
                S.emit("scalar", (lambda e, h=h: e.activation(
                       nsc[0:1, 16 + h:17 + h], nsc[0:1, 8 + h:9 + h], AF.Sqrt)),
                       waits=[("sv", sv_m)], inc=("sc", 1))
                sc_s = S.get("sc")
                S.emit("vector", (lambda e, h=h, scale=scale: e.tensor_scalar(
                       nsc[0:1, 24 + h:25 + h], nsc[0:1, 16 + h:17 + h],
                       -scale, None, ALU.mult)),
                       waits=[("sc", sc_s)])
                S.emit("vector", (lambda e, h=h: e.tensor_copy(
                       nbf[0:1, h:h + 1], nsc[0:1, 24 + h:25 + h])), inc=("sv", 1))
                sv_nb = S.get("sv")
                S.emit("tensor", (lambda e, h=h: e.matmul(
                       ps_b1[:, 0:1], orw_sb[0:1, :], nbf[0:1, h:h + 1],
                       start=True, stop=True)),
                       waits=[("sv", sv_nb), ("sd", sd_consts)], inc=("st", 1))
                st_b = S.get("st")
                S.emit("vector", (lambda e, h=h, bcol0=bcol0: e.tensor_copy(
                       b_sb[:, bcol0 + h:bcol0 + h + 1], ps_b1[:, 0:1])),
                       waits=[("st", st_b)], inc=("sv", 1))
            S.flush(blk, sems)
        bias_done_v = S.get("sv")

        with nc.psum_tensor(pref + "ps_s", [128, 2 * 512], F32) as ps_s, \
             nc.psum_tensor(pref + "ps_pva", [65, 512], F32) as ps_pva, \
             nc.psum_tensor(pref + "ps_pvb", [64, 512], F32) as ps_pvb, \
             nc.psum_tensor(pref + "ps_bc2", [128, 512], F32) as ps_bc2:
            bcol0 = 12
            pvdone_prev = 0
            otst_free = {}
            for h in range(NH):
                kh = 0 if layer == 0 else h
                for qc in range(NW):
                    ktmax = 4 * qc + 4
                    scnt = {}
                    exp_cnt = {}
                    for kt in range(ktmax):
                        w = []
                        if kt < 2:
                            w.append(("sv", pvdone_prev))
                            w.append(("sv", bias_done_v))
                        S.emit("tensor", (lambda e, kt=kt, h=h, qc=qc, kh=kh:
                               e.matmul(
                                   ps_s[:, (kt % 2) * 512:(kt % 2) * 512 + 512],
                                   k_sb[0:QP, kh * Tt + 128 * kt:
                                        kh * Tt + 128 * (kt + 1)],
                                   q_sb[0:QP, h * Tt + 512 * qc:
                                        h * Tt + 512 * (qc + 1)],
                                   start=True, stop=True)),
                               waits=w, inc=("st", 1))
                        scnt[("S", kt)] = S.get("st")
                        we = [("st", scnt[("S", kt)])]
                        if kt >= 2:
                            we.append(("st", scnt.get(("PV", kt - 2), 0)))
                        S.emit("scalar", (lambda e, kt=kt, h=h: e.activation(
                               p_sb[:, (kt % 2) * 512:(kt % 2) * 512 + 512],
                               ps_s[:, (kt % 2) * 512:(kt % 2) * 512 + 512],
                               AF.Exp, bias=b_sb[:, bcol0 + h:bcol0 + h + 1],
                               scale=float(scale))),
                               waits=we, inc=("sc", 1))
                        exp_cnt[kt] = S.get("sc")
                        if kt >= 4 * qc:
                            mi = kt - 4 * qc
                            S.emit("vector", (lambda e, kt=kt, mi=mi:
                                   e.tensor_tensor(
                                   p_sb[:, (kt % 2) * 512:(kt % 2) * 512 + 512],
                                   p_sb[:, (kt % 2) * 512:(kt % 2) * 512 + 512],
                                   masks_sb[:, 512 * mi:512 * (mi + 1)],
                                   ALU.mult)),
                                   waits=[("sc", exp_cnt[kt]),
                                          ("sd", sd_consts)], inc=("sv", 1))
                            pready = [("sv", S.get("sv"))]
                        else:
                            pready = [("sc", exp_cnt[kt])]
                        vb = kh * KT * VW
                        if layer == 0:
                            vaps = [(ps_pva[0:65, :],
                                     (vb + VW * kt, vb + VW * kt + 65))]
                        else:
                            vaps = [(ps_pva[0:65, :],
                                     (vb + VW * kt, vb + VW * kt + 65)),
                                    (ps_pvb[0:64, :],
                                     (vb + VW * kt + 65, vb + VW * kt + 129))]
                        for i, (o, (v0, v1)) in enumerate(vaps):
                            S.emit("tensor", (lambda e, o=o, v0=v0, v1=v1, kt=kt,
                                              ktmax=ktmax: e.matmul(
                                   o, v_sb[:, v0:v1],
                                   p_sb[:, (kt % 2) * 512:(kt % 2) * 512 + 512],
                                   start=(kt == 0), stop=(kt == ktmax - 1))),
                                   waits=pready if i == 0 else [], inc=("st", 1))
                        scnt[("PV", kt)] = S.get("st")
                    pv_all = S.get("st")
                    osem = "sdo%d" % (qc % 2)
                    S.emit("scalar", (lambda e: e.activation(
                           rrow[0:1, :], ps_pva[64:65, :], AF.Ln)),
                           waits=[("st", pv_all),
                                  (osem, otst_free.get((h, qc), 0))])
                    S.emit("scalar", (lambda e: e.activation(
                           rbf[0:1, :], rrow[0:1, :], AF.Exp, scale=-1.0)),
                           inc=("sc", 1))
                    sv_r = S.get("sc")
                    S.emit("tensor", (lambda e: e.matmul(
                           ps_bc2[:, :], orw_sb[0:1, :], rbf[0:1, :],
                           start=True, stop=True)),
                           waits=[("sc", sv_r), ("sd", sd_consts)],
                           inc=("st", 1))
                    st_bc = S.get("st")
                    ob = (qc % 2) * 512
                    S.emit("vector", (lambda e: e.tensor_copy(
                           rbc[:, :], ps_bc2[:, :])), waits=[("st", st_bc)])
                    S.emit("vector", (lambda e, ob=ob: e.tensor_tensor(
                           otst[0:64, ob:ob + 512], ps_pva[0:64, :],
                           rbc[0:64, :], ALU.mult)), inc=("sv", 1))
                    if layer == 1:
                        S.emit("vector", (lambda e, ob=ob: e.tensor_tensor(
                               otst[64:128, ob:ob + 512], ps_pvb[0:64, :],
                               rbc[0:64, :], ALU.mult)), inc=("sv", 1))
                    pvdone_prev = S.get("sv")
                    nd = S.get("sv")
                    S.emit("gpsimd", (lambda e, h=h, qc=qc, ob=ob: e.dma_start(
                           out=ot_dst(h, qc),
                           in_=otst[0:DH, ob:ob + 512])),
                           waits=[("sv", nd)], inc=(osem, 16))
                    nqc = (h, qc + 2) if qc + 2 < NW else (h + 1, qc + 2 - NW)
                    otst_free[nqc] = S.get(osem)
            S.emit("gpsimd", lambda e: e.nop(),
                   waits=[("sdo0", S.get("sdo0")), ("sdo1", S.get("sdo1"))])
            S.flush(blk, sems)



def build_A():
    nc = bass.Bass("TRN2", target_bir_lowering=False)
    xin = nc.dram_tensor("xin", [D, TLOC], F32, kind="ExternalInput")
    ln1 = nc.dram_tensor("ln1", [D, TLOC], BF16, kind="ExternalOutput")
    oc_d = nc.dram_tensor("ones_col", [128, 1], BF16, kind="ExternalInput")
    orw_d = nc.dram_tensor("ones_row", [1, 128], BF16, kind="ExternalInput")

    with ExitStack() as ex:
        x_sb = ex.enter_context(nc.sbuf_tensor("x_sb", [128, CC * 1024], F32))
        o_sb = ex.enter_context(nc.sbuf_tensor("o_sb", [128, CC * 1024], BF16))
        lb = ln_bufs(nc, ex)
        oc_sb = ex.enter_context(nc.sbuf_tensor("oc_sb", [128, 1], BF16))
        orw_sb = ex.enter_context(nc.sbuf_tensor("orw_sb", [1, 128], BF16))
        ps_stats = ex.enter_context(nc.psum_tensor("ps_stats", [1, 2048], F32))
        ps_bc = ex.enter_context(nc.psum_tensor("ps_bc", [128, 2048], F32))
        sems = {n: ex.enter_context(nc.semaphore(n)) for n in
                ("sd", "sdo0", "st", "sv", "sc")}
        blk = ex.enter_context(nc.Block())
        S = Sched()
        S.emit("sync", lambda e: e.dma_start(
               out=x_sb[:, :].rearrange("p (c t) -> p c t", t=TLOC),
               in_=xin.ap().rearrange("(c p) t -> p c t", p=128)), inc=("sd", 16))
        S.emit("sync", lambda e: e.dma_start(out=oc_sb[:, :], in_=oc_d[:, :]),
               inc=("sd", 16))
        S.emit("sync", lambda e: e.dma_start(out=orw_sb[:, :], in_=orw_d[:, :]),
               inc=("sd", 16))
        sd_c = S.get("sd")
        vfin = ln_emit(S, x_sb, o_sb, lb["xsq"], lb["t1"], lb["srow"],
                       lb["rowbf"], lb["abc"], lb["bbc"], ps_stats, ps_bc,
                       oc_sb, orw_sb, sd_c, wx=[("sd", sd_c)])
        S.emit("sync", lambda e: e.dma_start(
               out=ln1.ap().rearrange("(c p) t -> p c t", p=128),
               in_=o_sb[:, :].rearrange("p (c t) -> p c t", t=TLOC)),
               waits=[("sv", vfin)], inc=("sdo0", 16))
        S.emit("sync", lambda e: e.nop(), waits=[("sdo0", 16)])
        S.flush(blk, {**sems, "swa": None, "swb": None})
    return nc



def build_proj(layer):
    nc = bass.Bass("TRN2", target_bir_lowering=False)
    xin = nc.dram_tensor("xin", [D, TLOC], F32, kind="ExternalInput")
    ofs = nc.dram_tensor("ofs", [D, TLOC], BF16, kind="ExternalInput")
    wo_d = nc.dram_tensor("wo", [D, D], F32, kind="ExternalInput")
    xo = nc.dram_tensor("xo", [D, TLOC], F32, kind="ExternalOutput")
    lno = nc.dram_tensor("lno", [D, TLOC], BF16, kind="ExternalOutput")
    oc_d = nc.dram_tensor("ones_col", [128, 1], BF16, kind="ExternalInput")
    orw_d = nc.dram_tensor("ones_row", [1, 128], BF16, kind="ExternalInput")
    bo_d = nc.dram_tensor("bo", [128, 16], F32, kind="ExternalInput") \
        if layer == 1 else None
    with ExitStack() as ex:
        sems = {n: ex.enter_context(nc.semaphore(n)) for n in
                ("sd", "sdx", "swa", "swb", "sdo0", "st", "sv", "sc", "sp")}
        blk = ex.enter_context(nc.Block())
        S = Sched()
        o_load = (lambda dest: [lambda e: e.dma_start(
            out=dest[:, :].rearrange("p (c t) -> p c t", t=TLOC),
            in_=ofs.ap().rearrange("(c p) t -> p c t", p=128))])
        emit_proj(nc, S, blk, sems, layer, "p", xin, o_load, wo_d, xo, lno,
                  oc_d, orw_d, bo_d)
    return nc


def emit_proj(nc, S, blk, sems, layer, pref, xin, o_load, wo_d, xo, lno,
              oc_d, orw_d, bo_d, o_eng="sync", o_sem="sdx", o_wait=()):
    wo3 = wo_d.ap().rearrange("(c p) f -> p c f", p=128)

    with ExitStack() as ex:
        x_sb = ex.enter_context(nc.sbuf_tensor(pref + "x_sb", [128, CC * 1024], F32))
        o_sb = ex.enter_context(nc.sbuf_tensor(pref + "o_sb", [128, CC * 1024], BF16))
        ln_sb = ex.enter_context(nc.sbuf_tensor(pref + "ln_sb", [128, CC * 1024], BF16))
        oc_sb = ex.enter_context(nc.sbuf_tensor(pref + "oc_sb", [128, 1], BF16))
        orw_sb = ex.enter_context(nc.sbuf_tensor(pref + "orw_sb", [1, 128], BF16))
        bo_sb = ex.enter_context(nc.sbuf_tensor(pref + "bo_sb", [128, 16], F32))
        ex1 = ex.enter_context(ExitStack())
        ws = WStream(S, nc, ex1, pref + "wo", CC, 128)

        S.emit("sync", lambda e: e.dma_start(
               out=x_sb[:, :].rearrange("p (c t) -> p c t", t=TLOC),
               in_=xin.ap().rearrange("(c p) t -> p c t", p=128)), inc=("sdx", 16))
        for fdma in o_load(o_sb):
            S.emit(o_eng, fdma, waits=list(o_wait), inc=(o_sem, 16))
        S.emit("sync", lambda e: e.dma_start(out=oc_sb[:, :], in_=oc_d[:, :]),
               inc=("sd", 16))
        S.emit("sync", lambda e: e.dma_start(out=orw_sb[:, :], in_=orw_d[:, :]),
               inc=("sd", 16))
        if layer == 1:
            S.emit("sync", lambda e: e.dma_start(out=bo_sb[:, :], in_=bo_d[:, :]),
                   inc=("sd", 16))
        sd_c = S.get("sd")
        sdx_c = S.get("sdx")
        o_c = S.get(o_sem)

        with nc.psum_tensor(pref + "ps", [128, 2 * 512], F32) as ps:
            pe_war = {0: 0, 1: 0}
            hcur = ws.load(wo3[:, :, 0:128], 128)
            for f in range(CC):
                hnext = ws.load(wo3[:, :, 128 * (f + 1):128 * (f + 2)], 128) \
                    if f + 1 < CC else None
                for t in range(2):
                    for c in range(CC):
                        w = []
                        if c == 0:
                            w = [("sp", ws.conv_count(hcur)), ("sdx", sdx_c),
                                 (o_sem, o_c), ("sv", pe_war[t])]
                        S.emit("tensor", (lambda e, f=f, t=t, c=c, hcur=hcur:
                               e.matmul(ps[:, (t % 2) * 512:(t % 2) * 512 + 512],
                                        ws.tile(hcur, c, 0, 128),
                                        o_sb[:, 1024 * c + 512 * t:
                                             1024 * c + 512 * t + 512],
                                        start=(c == 0), stop=(c == CC - 1))),
                               waits=w, inc=("st", 1) if c == CC - 1 else None)
                    mm_d = S.get("st")
                    if layer == 1:
                        S.emit("vector", (lambda e, f=f, t=t: e.scalar_tensor_tensor(
                               x_sb[:, 1024 * f + 512 * t:1024 * f + 512 * t + 512],
                               ps[:, (t % 2) * 512:(t % 2) * 512 + 512],
                               bo_sb[:, f:f + 1],
                               x_sb[:, 1024 * f + 512 * t:1024 * f + 512 * t + 512],
                               ALU.add, ALU.add)),
                               waits=[("st", mm_d), ("sd", sd_c)], inc=("sv", 1))
                    else:
                        S.emit("vector", (lambda e, f=f, t=t: e.tensor_tensor(
                               x_sb[:, 1024 * f + 512 * t:1024 * f + 512 * t + 512],
                               ps[:, (t % 2) * 512:(t % 2) * 512 + 512],
                               x_sb[:, 1024 * f + 512 * t:1024 * f + 512 * t + 512],
                               ALU.add)),
                               waits=[("st", mm_d), ("sdx", sdx_c)], inc=("sv", 1))
                    pe_war[t] = S.get("sv")
                ws.done(hcur)
                hcur = hnext
            x_done = S.get("sv")
            S.flush(blk, sems)
        ex1.close()

        with nc.psum_tensor(pref + "ps_stats", [1, 2048], F32) as ps_stats, \
             nc.psum_tensor(pref + "ps_bc", [128, 2048], F32) as ps_bc, \
             ExitStack() as ex2:
            lb = ln_bufs(nc, ex2, pref)
            vfin = ln_emit(S, x_sb, ln_sb, lb["xsq"], lb["t1"], lb["srow"],
                           lb["rowbf"], lb["abc"], lb["bbc"], ps_stats, ps_bc,
                           oc_sb, orw_sb, sd_c, wx=[("sv", x_done)])
            S.emit("sync", lambda e: e.dma_start(
                   out=xo.ap().rearrange("(c p) t -> p c t", p=128),
                   in_=x_sb[:, :].rearrange("p (c t) -> p c t", t=TLOC)),
                   waits=[("sv", x_done)], inc=("sdo0", 16))
            S.emit("sync", lambda e: e.dma_start(
                   out=lno.ap().rearrange("(c p) t -> p c t", p=128),
                   in_=ln_sb[:, :].rearrange("p (c t) -> p c t", t=TLOC)),
                   waits=[("sv", vfin)], inc=("sdo0", 16))
            S.emit("sync", lambda e: e.nop(), waits=[("sdo0", S.get("sdo0"))])
            S.flush(blk, sems)



def _ws_load4(ws, src_halves, ncols):
    S = ws.S
    slot = ws.n % 2
    ws.n += 1
    sem = ws.sems[slot]
    W = ws.maxw
    rc = ws.rc
    base = slot * rc * W
    stg, bf = ws.stg, ws.bf
    hw = ncols // 2
    v = stg[:, base:base + rc * ncols].rearrange("p (c f) -> p c f", f=ncols)
    for s in range(2):
        S.emit("sync", (lambda e, s=s: e.dma_start(
               out=v[:, :, hw * s:hw * (s + 1)], in_=src_halves[s])),
               waits=[("st", ws.last_use[slot])], inc=(sem, 16))
    d = S.get(sem)
    S.emit(CONV_ENG, (lambda e: e.tensor_copy(
           bf[:, base:base + rc * ncols], stg[:, base:base + rc * ncols])),
           waits=[(sem, d)], inc=("sp", 1))
    return (slot, S.get("sp"), ncols, rc)


def _gu_phase(S, nc, ws, ps, nmp, hoff, ln_sb, h_sb, sgw4, silu_first,
              we_ap=None, pair_cols=256, hw_st=0, ln_wait=(), war_in=None):
    evac_war = dict(war_in) if war_in else {}
    prev_h = 0
    hcur = _ws_load4(ws, sgw4(0), pair_cols)
    for mp in range(nmp):
        hnext = _ws_load4(ws, sgw4(mp + 1), pair_cols) if mp + 1 < nmp else None
        for t in range(2):
            idx = mp * 2 + t
            sp = 2 * (idx % 2)
            for half in range(2):
                for c in range(CC):
                    w = []
                    if c == 0:
                        w = [("sp", ws.conv_count(hcur)),
                             ("sv", evac_war.get(sp + half, 0))] + list(ln_wait)
                    S.emit("tensor", (lambda e, c=c, half=half, sp=sp, t=t,
                                      hcur=hcur: e.matmul(
                           ps[:, (sp + half) * 512:(sp + half) * 512 + 512],
                           ws.tile(hcur, c, 128 * half, 128),
                           ln_sb[:, 1024 * c + 512 * t:1024 * c + 512 * t + 512],
                           start=(c == 0), stop=(c == CC - 1))),
                           waits=w, inc=("st", 1) if c == CC - 1 else None)
            mm_d = S.get("st")
            gh = 0 if silu_first else 1
            uh = 1 - gh
            S.emit("scalar", (lambda e, sp=sp, gh=gh: e.activation(
                   vstg_g[0:128, :], ps[:, (sp + gh) * 512:(sp + gh) * 512 + 512],
                   AF.Silu)), waits=[("sv", prev_h), ("st", mm_d)], inc=("sc", 1))
            sc_d = S.get("sc")
            hw = [("sc", sc_d), ("st", hw_st)]
            if we_ap is None:
                S.emit("vector", (lambda e, sp=sp, uh=uh, mp=mp, t=t,
                                  hoff=hoff: e.tensor_tensor(
                       h_sb[:, hoff + 1024 * mp + 512 * t:
                            hoff + 1024 * mp + 512 * t + 512],
                       vstg_g[0:128, :],
                       ps[:, (sp + uh) * 512:(sp + uh) * 512 + 512], ALU.mult)),
                       waits=hw, inc=("sv", 1))
            else:
                S.emit("vector", (lambda e, sp=sp, uh=uh: e.tensor_tensor(
                       vstg_h[0:128, :], vstg_g[0:128, :],
                       ps[:, (sp + uh) * 512:(sp + uh) * 512 + 512], ALU.mult)),
                       waits=hw)
                S.emit("vector", (lambda e, mp=mp, t=t, hoff=hoff, we_ap=we_ap:
                       e.tensor_tensor(
                       h_sb[:, hoff + 1024 * mp + 512 * t:
                            hoff + 1024 * mp + 512 * t + 512],
                       vstg_h[0:128, :], we_ap[:, 512 * t:512 * t + 512],
                       ALU.mult)), inc=("sv", 1))
            evac_war[sp + 0] = S.get("sv")
            evac_war[sp + 1] = S.get("sv")
            prev_h = S.get("sv")
        ws.done(hcur)
        hcur = hnext
    return S.get("sv"), evac_war


def _down_phase(S, nc, ws, ps, x_sb, h_sb, hoff, nch, dn3, h_ready, x_wait=(),
                war_in=None):
    evac_war = dict(war_in) if war_in else {}
    hcur = ws.load(dn3(0), 128, rows_chunks=nch)
    for f2 in range(CC):
        hnext = ws.load(dn3(f2 + 1), 128, rows_chunks=nch) if f2 + 1 < CC else None
        for t in range(2):
            sp = (f2 * 2 + t) % 4
            for c in range(nch):
                w = []
                if c == 0:
                    w = [("sp", ws.conv_count(hcur)), ("sv", h_ready),
                         ("sv", evac_war.get(sp, 0))]
                S.emit("tensor", (lambda e, c=c, sp=sp, t=t, hcur=hcur,
                                  hoff=hoff: e.matmul(
                       ps[:, sp * 512:sp * 512 + 512],
                       ws.tile(hcur, c, 0, 128),
                       h_sb[:, hoff + 1024 * c + 512 * t:
                            hoff + 1024 * c + 512 * t + 512],
                       start=(c == 0), stop=(c == nch - 1))),
                       waits=w, inc=("st", 1) if c == nch - 1 else None)
            mm_d = S.get("st")
            S.emit("vector", (lambda e, f2=f2, t=t, sp=sp: e.tensor_tensor(
                   x_sb[:, 1024 * f2 + 512 * t:1024 * f2 + 512 * t + 512],
                   ps[:, sp * 512:sp * 512 + 512],
                   x_sb[:, 1024 * f2 + 512 * t:1024 * f2 + 512 * t + 512],
                   ALU.add)), waits=[("st", mm_d)] + list(x_wait),
                   inc=("sv", 1))
            evac_war[sp] = S.get("sv")
        ws.done(hcur)
        hcur = hnext
    return S.get("st"), evac_war


def build_moe():
    nc = bass.Bass("TRN2", target_bir_lowering=False)
    xin = nc.dram_tensor("xin", [D, TLOC], F32, kind="ExternalInput")
    lnin = nc.dram_tensor("lnin", [D, TLOC], BF16, kind="ExternalInput")
    gw_d = nc.dram_tensor("gw", [D, 8], F32, kind="ExternalInput")
    gb_d = nc.dram_tensor("gbbc", [128, 8], F32, kind="ExternalInput")
    sgw_d = nc.dram_tensor("sgw", [D, 5632], F32, kind="ExternalInput")
    sdw_d = nc.dram_tensor("sdw", [2816, D], F32, kind="ExternalInput")
    upw_d = nc.dram_tensor("upw", [8 * D, 2816], F32, kind="ExternalInput")
    dnw_d = nc.dram_tensor("dnw", [8 * 1408, D], F32, kind="ExternalInput")
    xo = nc.dram_tensor("xo", [D, TLOC], F32, kind="ExternalOutput")
    lno = nc.dram_tensor("lno", [D, TLOC], BF16, kind="ExternalOutput")
    oc_d = nc.dram_tensor("ones_col", [128, 1], BF16, kind="ExternalInput")
    orw_d = nc.dram_tensor("ones_row", [1, 128], BF16, kind="ExternalInput")
    idf_d = nc.dram_tensor("idn_f32", [128, 128], F32, kind="ExternalInput")
    with ExitStack() as ex:
        sems = {n: ex.enter_context(nc.semaphore(n)) for n in
                ("sd", "sdx", "swa", "swb", "sdo0", "st", "sv", "sc", "sp")}
        blk = ex.enter_context(nc.Block())
        S = Sched()
        emit_moe(nc, S, blk, sems, "m", xin, lnin, gw_d, gb_d, sgw_d, sdw_d,
                 upw_d, dnw_d, xo, lno, oc_d, orw_d, idf_d)
    return nc


def emit_moe(nc, S, blk, sems, pref, xin, lnin, gw_d, gb_d, sgw_d, sdw_d,
             upw_d, dnw_d, xo, lno, oc_d, orw_d, idf_d):
    global vstg_g, vstg_h
    with ExitStack() as ex:
        x_sb = ex.enter_context(nc.sbuf_tensor(pref + "x_sb", [128, CC * 1024], F32))
        ln_sb = ex.enter_context(nc.sbuf_tensor(pref + "ln_sb", [128, CC * 1024], BF16))
        h_sb = ex.enter_context(nc.sbuf_tensor(pref + "h_sb", [128, 11 * 1024], BF16))
        vstg_g = ex.enter_context(nc.sbuf_tensor(pref + "vstg_g", [128, 512], BF16))
        vstg_h = ex.enter_context(nc.sbuf_tensor(pref + "vstg_h", [128, 512], BF16))
        webc = ex.enter_context(nc.sbuf_tensor(pref + "webc", [128, 1024], F32))
        oc_sb = ex.enter_context(nc.sbuf_tensor(pref + "oc_sb", [128, 1], BF16))
        orw_sb = ex.enter_context(nc.sbuf_tensor(pref + "orw_sb", [1, 128], BF16))
        idf_sb = ex.enter_context(nc.sbuf_tensor(pref + "idf_sb", [128, 128], F32))
        gb_sb = ex.enter_context(nc.sbuf_tensor(pref + "gb_sb", [128, 8], F32))
        gwst = ex.enter_context(nc.sbuf_tensor(pref + "gwst", [128, CC * 8], F32))
        gwbf = ex.enter_context(nc.sbuf_tensor(pref + "gwbf", [128, CC * 8], BF16))
        sfm = ex.enter_context(nc.sbuf_tensor(pref + "sfm", [8, 1024], F32))
        stm = ex.enter_context(nc.sbuf_tensor(pref + "stm", [128, 64], F32))
        ztm = ex.enter_context(nc.sbuf_tensor(pref + "ztm", [128, 64], F32))
        mtm = ex.enter_context(nc.sbuf_tensor(pref + "mtm", [128, 64], F32))
        m2tm = ex.enter_context(nc.sbuf_tensor(pref + "m2tm", [128, 64], F32))
        wetm = ex.enter_context(nc.sbuf_tensor(pref + "wetm", [128, 64], F32))
        rcol = ex.enter_context(nc.sbuf_tensor(pref + "rcol", [128, 8], F32))
        werow = ex.enter_context(nc.sbuf_tensor(pref + "werow", [1, 8 * 1024], BF16))
        S.emit("sync", lambda e: e.dma_start(
               out=x_sb[:, :].rearrange("p (c t) -> p c t", t=TLOC),
               in_=xin.ap().rearrange("(c p) t -> p c t", p=128)), inc=("sdx", 16))
        S.emit("sync", lambda e: e.dma_start(
               out=ln_sb[:, :].rearrange("p (c t) -> p c t", t=TLOC),
               in_=lnin.ap().rearrange("(c p) t -> p c t", p=128)), inc=("sdx", 16))
        S.emit("sync", lambda e: e.dma_start(out=oc_sb[:, :], in_=oc_d[:, :]),
               inc=("sd", 16))
        S.emit("sync", lambda e: e.dma_start(out=orw_sb[:, :], in_=orw_d[:, :]),
               inc=("sd", 16))
        S.emit("sync", lambda e: e.dma_start(out=idf_sb[:, :], in_=idf_d[:, :]),
               inc=("sd", 16))
        S.emit("sync", lambda e: e.dma_start(out=gb_sb[:, :], in_=gb_d[:, :]),
               inc=("sd", 16))
        S.emit("sync", lambda e: e.dma_start(
               out=gwst[:, :].rearrange("p (c f) -> p c f", f=8),
               in_=gw_d.ap().rearrange("(c p) f -> p c f", p=128)), inc=("sd", 16))
        sd_c = S.get("sd")
        sdx_c = S.get("sdx")
        S.emit("vector", lambda e: e.tensor_copy(gwbf[:, :], gwst[:, :]),
               waits=[("sd", sd_c)], inc=("sv", 1))
        gw_conv = S.get("sv")

        with nc.psum_tensor(pref + "ps_g", [8, 1024], F32) as ps_g, \
             nc.psum_tensor(pref + "ps_tr2", [128, 8], F32) as ps_tr2, \
             nc.psum_tensor(pref + "ps_wt", [1, 128], F32) as ps_wt:
            for t in range(2):
                for c in range(CC):
                    S.emit("tensor", (lambda e, t=t, c=c: e.matmul(
                           ps_g[0:8, 512 * t:512 * t + 512],
                           gwbf[:, 8 * c:8 * (c + 1)],
                           ln_sb[:, 1024 * c + 512 * t:1024 * c + 512 * t + 512],
                           start=(c == 0), stop=(c == CC - 1))),
                           waits=[("sv", gw_conv), ("sdx", sdx_c)] if c == 0 else [],
                           inc=("st", 1) if c == CC - 1 else None)
            gmm = S.get("st")
            S.emit("scalar", (lambda e: e.activation(
                   sfm[0:8, :], ps_g[0:8, :], AF.Sigmoid)),
                   waits=[("st", gmm)], inc=("sc", 1))
            sig_d = S.get("sc")
            for ti in range(8):
                S.emit("tensor", (lambda e, ti=ti: e.transpose(
                       ps_tr2[:, 0:8], sfm[0:8, 128 * ti:128 * (ti + 1)],
                       idf_sb[0:8, 0:8])),
                       waits=[("sc", sig_d), ("sd", sd_c),
                              ("sv", S.get("sv"))], inc=("st", 1))
                trd = S.get("st")
                S.emit("vector", (lambda e, ti=ti: e.tensor_copy(
                       stm[:, 8 * ti:8 * ti + 8], ps_tr2[:, 0:8])),
                       waits=[("st", trd)], inc=("sv", 1))
            sv_s = S.get("sv")
            for ti in range(8):
                o = 8 * ti
                S.emit("vector", (lambda e, o=o: e.tensor_tensor(
                       ztm[:, o:o + 8], stm[:, o:o + 8], gb_sb[:, 0:8], ALU.add)),
                       waits=[("sv", sv_s), ("sd", sd_c)])
                S.emit("vector", (lambda e, o=o: e.tensor_reduce(
                       rcol[:, 0:1], ztm[:, o:o + 8], axis=AX.X, op=ALU.max)))
                S.emit("vector", (lambda e, o=o: e.tensor_scalar(
                       mtm[:, o:o + 8], ztm[:, o:o + 8], rcol[:, 0:1], None,
                       ALU.is_equal)))
                S.emit("vector", (lambda e, o=o: e.tensor_tensor(
                       m2tm[:, o:o + 8], mtm[:, o:o + 8], stm[:, o:o + 8],
                       ALU.mult)))
                S.emit("vector", (lambda e, o=o: e.tensor_reduce(
                       rcol[:, 1:2], m2tm[:, o:o + 8], axis=AX.X, op=ALU.add)))
                S.emit("vector", (lambda e, o=o: e.scalar_tensor_tensor(
                       ztm[:, o:o + 8], mtm[:, o:o + 8], -1e30, ztm[:, o:o + 8],
                       ALU.mult, ALU.add)))
                S.emit("vector", (lambda e, o=o: e.tensor_reduce(
                       rcol[:, 2:3], ztm[:, o:o + 8], axis=AX.X, op=ALU.max)))
                S.emit("vector", (lambda e, o=o: e.tensor_scalar(
                       m2tm[:, o:o + 8], ztm[:, o:o + 8], rcol[:, 2:3], None,
                       ALU.is_equal)))
                S.emit("vector", (lambda e, o=o: e.tensor_tensor(
                       ztm[:, o:o + 8], m2tm[:, o:o + 8], stm[:, o:o + 8],
                       ALU.mult)))
                S.emit("vector", (lambda e, o=o: e.tensor_reduce(
                       rcol[:, 3:4], ztm[:, o:o + 8], axis=AX.X, op=ALU.add)))
                S.emit("vector", (lambda e: e.tensor_tensor(
                       rcol[:, 4:5], rcol[:, 1:2], rcol[:, 3:4], ALU.add)))
                S.emit("vector", (lambda e: e.reciprocal(
                       rcol[:, 5:6], rcol[:, 4:5])))
                S.emit("vector", (lambda e, o=o: e.tensor_scalar(
                       wetm[:, o:o + 8], mtm[:, o:o + 8], rcol[:, 1:2], None,
                       ALU.mult)))
                S.emit("vector", (lambda e, o=o: e.tensor_scalar(
                       ztm[:, o:o + 8], m2tm[:, o:o + 8], rcol[:, 3:4], None,
                       ALU.mult)))
                S.emit("vector", (lambda e, o=o: e.tensor_tensor(
                       wetm[:, o:o + 8], wetm[:, o:o + 8], ztm[:, o:o + 8],
                       ALU.add)))
                S.emit("vector", (lambda e, o=o: e.tensor_scalar(
                       wetm[:, o:o + 8], wetm[:, o:o + 8], rcol[:, 5:6], None,
                       ALU.mult)), inc=("sv", 1))
            we_d = S.get("sv")
            for ti in range(8):
                for e_ in range(8):
                    S.emit("tensor", (lambda e, ti=ti, e_=e_: e.transpose(
                           ps_wt[0:1, :], wetm[:, 8 * ti + e_:8 * ti + e_ + 1],
                           idf_sb[0:128, 0:128])),
                           waits=[("sv", we_d), ("sv", S.get("sv"))],
                           inc=("st", 1))
                    wtd = S.get("st")
                    S.emit("vector", (lambda e, ti=ti, e_=e_: e.tensor_copy(
                           werow[0:1, 1024 * e_ + 128 * ti:
                                 1024 * e_ + 128 * (ti + 1)], ps_wt[0:1, :])),
                           waits=[("st", wtd)], inc=("sv", 1))
            S.flush(blk, sems)
        werow_d = S.get("sv")

        sgw3 = sgw_d.ap().rearrange("(c p) f -> p c f", p=128)
        sdw3 = sdw_d.ap().rearrange("(c p) f -> p c f", p=128)
        upw3 = upw_d.ap().rearrange("(e c p) f -> e p c f", p=128, e=8)
        dnw3 = dnw_d.ap().rearrange("(e c p) f -> e p c f", p=128, e=8)

        def sgw_pair(j):
            return (sgw3[:, :, 128 * j:128 * (j + 1)],
                    sgw3[:, :, 2816 + 128 * j:2816 + 128 * (j + 1)])

        def upw_pair(e_, j):
            return (upw3[e_, :, :, 128 * j:128 * (j + 1)],
                    upw3[e_, :, :, 1408 + 128 * j:1408 + 128 * (j + 1)])

        with ExitStack() as ex1:
            ws = WStream(S, nc, ex1, pref + "moe", CC, 256)
            with nc.psum_tensor(pref + "ps_m", [128, 4 * 512], F32) as ps_m, \
                 nc.psum_tensor(pref + "ps_bc3", [128, 1024], F32) as ps_bc3:
                hw_st = 0
                pwar = None
                for rnd in range(2):
                    h_done, pwar = _gu_phase(S, nc, ws, ps_m, 11, 0, ln_sb, h_sb,
                                       (lambda mp, rnd=rnd:
                                        sgw_pair(11 * rnd + mp)),
                                       silu_first=False, we_ap=None, hw_st=hw_st,
                                       ln_wait=[("sdx", sdx_c)], war_in=pwar)
                    hw_st, pwar = _down_phase(S, nc, ws, ps_m, x_sb, h_sb, 0, 11,
                                        (lambda f2, rnd=rnd: sdw3[:,
                                         11 * rnd:11 * (rnd + 1),
                                         128 * f2:128 * (f2 + 1)]),
                                        h_done, x_wait=[("sdx", sdx_c)],
                                        war_in=pwar)
                for e_ in range(8):
                    for t in range(2):
                        S.emit("tensor", (lambda e, t=t, e_=e_: e.matmul(
                               ps_bc3[:, 512 * t:512 * t + 512], orw_sb[0:1, :],
                               werow[0:1, 1024 * e_ + 512 * t:
                                     1024 * e_ + 512 * t + 512],
                               start=True, stop=True)),
                               waits=[("sv", werow_d), ("sd", sd_c)],
                               inc=("st", 1))
                    bc_d = S.get("st")
                    S.emit("vector", (lambda e: e.tensor_copy(
                           webc[:, :], ps_bc3[:, :])),
                           waits=[("st", bc_d)], inc=("sv", 1))
                    h_done, pwar = _gu_phase(S, nc, ws, ps_m, 11, 0, ln_sb, h_sb,
                                       (lambda mp, e_=e_: upw_pair(e_, mp)),
                                       silu_first=True, we_ap=webc,
                                       hw_st=hw_st, ln_wait=[("sdx", sdx_c)],
                                       war_in=pwar)
                    hw_st, pwar = _down_phase(S, nc, ws, ps_m, x_sb, h_sb, 0, 11,
                                        (lambda f2, e_=e_: dnw3[e_, :, :,
                                         128 * f2:128 * (f2 + 1)]),
                                        h_done, x_wait=[("sdx", sdx_c)],
                                        war_in=pwar)
                x_done = S.get("sv")
                S.flush(blk, sems)

        with nc.psum_tensor(pref + "ps_stats", [1, 2048], F32) as ps_stats, \
             nc.psum_tensor(pref + "ps_bc", [128, 2048], F32) as ps_bc, \
             ExitStack() as ex2:
            lb = ln_bufs(nc, ex2, pref)
            vfin = ln_emit(S, x_sb, ln_sb, lb["xsq"], lb["t1"], lb["srow"],
                           lb["rowbf"], lb["abc"], lb["bbc"], ps_stats, ps_bc,
                           oc_sb, orw_sb, sd_c, wx=[("sv", x_done)])
            S.emit("sync", lambda e: e.dma_start(
                   out=xo.ap().rearrange("(c p) t -> p c t", p=128),
                   in_=x_sb[:, :].rearrange("p (c t) -> p c t", t=TLOC)),
                   waits=[("sv", x_done)], inc=("sdo0", 16))
            S.emit("sync", lambda e: e.dma_start(
                   out=lno.ap().rearrange("(c p) t -> p c t", p=128),
                   in_=ln_sb[:, :].rearrange("p (c t) -> p c t", t=TLOC)),
                   waits=[("sv", vfin)], inc=("sdo0", 16))
            S.emit("sync", lambda e: e.nop(), waits=[("sdo0", S.get("sdo0"))])
            S.flush(blk, sems)



def build_ffn1():
    nc = bass.Bass("TRN2", target_bir_lowering=False)
    xin = nc.dram_tensor("xin", [D, TLOC], F32, kind="ExternalInput")
    lnin = nc.dram_tensor("lnin", [D, TLOC], BF16, kind="ExternalInput")
    wgu_d = nc.dram_tensor("wgu1", [D, 16384], F32, kind="ExternalInput")
    wdn_d = nc.dram_tensor("wdn1", [8192, D], F32, kind="ExternalInput")
    xo = nc.dram_tensor("xo", [D, TLOC], F32, kind="ExternalOutput")
    with ExitStack() as ex:
        sems = {n: ex.enter_context(nc.semaphore(n)) for n in
                ("sd", "sdx", "swa", "swb", "sdo0", "st", "sv", "sc", "sp")}
        blk = ex.enter_context(nc.Block())
        S = Sched()
        emit_ffn1(nc, S, blk, sems, "g", xin, lnin, wgu_d, wdn_d, xo)
    return nc


def emit_ffn1(nc, S, blk, sems, pref, xin, lnin, wgu_d, wdn_d, xo):
    global vstg_g, vstg_h
    with ExitStack() as ex:
        x_sb = ex.enter_context(nc.sbuf_tensor(pref + "x_sb", [128, CC * 1024], F32))
        ln_sb = ex.enter_context(nc.sbuf_tensor(pref + "ln_sb", [128, CC * 1024], BF16))
        h_sb = ex.enter_context(nc.sbuf_tensor(pref + "h_sb", [128, 16 * 1024], BF16))
        vstg_g = ex.enter_context(nc.sbuf_tensor(pref + "vstg_g", [128, 512], BF16))
        vstg_h = ex.enter_context(nc.sbuf_tensor(pref + "vstg_h", [128, 512], BF16))
        S.emit("sync", lambda e: e.dma_start(
               out=x_sb[:, :].rearrange("p (c t) -> p c t", t=TLOC),
               in_=xin.ap().rearrange("(c p) t -> p c t", p=128)), inc=("sdx", 16))
        S.emit("sync", lambda e: e.dma_start(
               out=ln_sb[:, :].rearrange("p (c t) -> p c t", t=TLOC),
               in_=lnin.ap().rearrange("(c p) t -> p c t", p=128)), inc=("sdx", 16))
        sdx_c = S.get("sdx")
        wgu3 = wgu_d.ap().rearrange("(c p) f -> p c f", p=128)
        wdn3 = wdn_d.ap().rearrange("(g c p) f -> g p c f", p=128, g=4)

        def wgu_pair(j):
            return (wgu3[:, :, 128 * j:128 * (j + 1)],
                    wgu3[:, :, 8192 + 128 * j:8192 + 128 * (j + 1)])

        with ExitStack() as ex1:
            ws = WStream(S, nc, ex1, pref + "f1", CC, 256)
            with nc.psum_tensor(pref + "ps_m", [128, 4 * 512], F32) as ps_m:
                hw_st = 0
                pwar = None
                for grp in range(4):
                    h_done, pwar = _gu_phase(S, nc, ws, ps_m, 16, 0, ln_sb, h_sb,
                                       (lambda mp, grp=grp:
                                        wgu_pair(16 * grp + mp)),
                                       silu_first=True, we_ap=None, hw_st=hw_st,
                                       ln_wait=[("sdx", sdx_c)], war_in=pwar)
                    hw_st, pwar = _down_phase(S, nc, ws, ps_m, x_sb, h_sb, 0, 16,
                                        (lambda f2, grp=grp: wdn3[grp, :, :,
                                         128 * f2:128 * (f2 + 1)]),
                                        h_done, x_wait=[("sdx", sdx_c)],
                                        war_in=pwar)
                x_done = S.get("sv")
                S.emit("sync", lambda e: e.dma_start(
                       out=xo.ap().rearrange("(c p) t -> p c t", p=128),
                       in_=x_sb[:, :].rearrange("p (c t) -> p c t", t=TLOC)),
                       waits=[("sv", x_done)], inc=("sdo0", 16))
                S.emit("sync", lambda e: e.nop(),
                       waits=[("sdo0", S.get("sdo0"))])
                S.flush(blk, sems)



_cache = {}


def _get(name, builder):
    if name not in _cache:
        _cache[name] = builder()
    return _cache[name]


def _consts(d):
    return {"ones_col": d["ones_col"], "ones_row": d["ones_row"]}


def kernel(**inputs):
    per_core = prep_host(inputs)
    return kernel_fused(per_core, inputs)


def kernel_multilaunch(**inputs):
    per_core = prep_host(inputs)
    cores = list(range(NCORES))

    def run(name, builder, maps):
        ncp = _get(name, builder)
        return run_bass_kernel_spmd(ncp, maps, core_ids=cores)

    resA = run("A", build_A, [{"xin": d["xin"], **_consts(d)} for d in per_core])
    ln1f = np.concatenate([np.asarray(resA.results[r]["ln1"]) for r in cores],
                          axis=1)
    resB0 = run("B0", lambda: build_attn(0),
                [{"lnf": ln1f, "wall": d["wall0"], "cos": d["cos0"],
                  "sin": d["sin0"], "masks": d["masks"], "idn_bf": d["idn_bf"],
                  **_consts(d)} for d in per_core])
    o0 = np.concatenate([np.asarray(resB0.results[r]["ot"]) for r in cores],
                        axis=0)
    resC0 = run("C0", lambda: build_proj(0),
                [{"xin": d["xin"], "ofs": _cc(o0[:, TLOC * r:TLOC * (r + 1)]),
                  "wo": d["wo0"], **_consts(d)}
                 for r, d in zip(cores, per_core)])
    resD = run("D", build_moe,
               [{"xin": np.asarray(resC0.results[r]["xo"]),
                 "lnin": np.asarray(resC0.results[r]["lno"]),
                 "gw": d["gw"], "gbbc": d["gbbc"], "sgw": d["sgw"],
                 "sdw": d["sdw"], "upw": d["upw"].reshape(8 * D, 2816),
                 "dnw": d["dnw"].reshape(8 * 1408, D),
                 "idn_f32": d["idn_f32"], **_consts(d)}
                for r, d in zip(cores, per_core)])
    ln3f = np.concatenate([np.asarray(resD.results[r]["lno"]) for r in cores],
                          axis=1)
    resB1 = run("B1", lambda: build_attn(1),
                [{"lnf": ln3f, "wall": d["wall1"], "bqkv1": d["bqkv1"],
                  "cos": d["cos1"], "sin": d["sin1"], "masks": d["masks"],
                  "idn_bf": d["idn_bf"], **_consts(d)} for d in per_core])
    o1 = np.concatenate([np.asarray(resB1.results[r]["ot"]) for r in cores],
                        axis=0)
    resC1 = run("C1", lambda: build_proj(1),
                [{"xin": np.asarray(resD.results[r]["xo"]),
                  "ofs": _cc(o1[:, TLOC * r:TLOC * (r + 1)]),
                  "wo": d["wo1"], "bo": _cc(np.asarray(
                      inputs["L1_attn_b_out"]).reshape(CC, 128).T.astype(
                          np.float32)), **_consts(d)}
                 for r, d in zip(cores, per_core)])
    resG = run("G", build_ffn1,
               [{"xin": np.asarray(resC1.results[r]["xo"]),
                 "lnin": np.asarray(resC1.results[r]["lno"]),
                 "wgu1": d["wgu1"], "wdn1": d["wdn1"]}
                for r, d in zip(cores, per_core)])

    out = np.zeros((1, T, D), dtype=np.float32)
    for r in cores:
        out[0, TLOC * r:TLOC * (r + 1), :] = \
            np.asarray(resG.results[r]["xo"], dtype=np.float32).T
    return out





_SCOPES = []


def _scope(nc, name):
    if _SCOPES:
        n, sid = _SCOPES.pop()
        nc.leave_named_scope(n, sid, False)
    if name is not None:
        sid, _ = nc.enter_named_scope(name, False)
        _SCOPES.append((name, sid))

def emit_barrier(S, sems):
    for eng in ("vector", "scalar", "gpsimd"):
        S.emit(eng, lambda e: e.drain())
    for eng in Sched.ENGINES:
        S.emit(eng, lambda e: e.nop(), inc=("bar", 1))
    tgt = S.get("bar")
    for eng in Sched.ENGINES:
        S.emit(eng, lambda e: e.nop(), waits=[("bar", tgt)])


def build_fused():
    nc = bass.Bass("TRN2", target_bir_lowering=False)
    I32 = mybir.dt.int32
    xin = nc.dram_tensor("xin", [D, TLOC], F32, kind="ExternalInput")
    xo = nc.dram_tensor("xo", [D, TLOC], F32, kind="ExternalOutput")
    coff_d = nc.dram_tensor("coff", [1, 1], I32, kind="ExternalInput")
    wall0 = nc.dram_tensor("wall0", [D, 704], F32, kind="ExternalInput")
    wall1 = nc.dram_tensor("wall1", [D, 1280], F32, kind="ExternalInput")
    bqkv_d = nc.dram_tensor("bqkv1", [128, 12], F32, kind="ExternalInput")
    wo0_d = nc.dram_tensor("wo0", [D, D], F32, kind="ExternalInput")
    wo1_d = nc.dram_tensor("wo1", [D, D], F32, kind="ExternalInput")
    bo_d = nc.dram_tensor("bo", [128, 16], F32, kind="ExternalInput")
    gw_d = nc.dram_tensor("gw", [D, 8], F32, kind="ExternalInput")
    gb_d = nc.dram_tensor("gbbc", [128, 8], F32, kind="ExternalInput")
    sgw_d = nc.dram_tensor("sgw", [D, 5632], F32, kind="ExternalInput")
    sdw_d = nc.dram_tensor("sdw", [2816, D], F32, kind="ExternalInput")
    upw_d = nc.dram_tensor("upw", [8 * D, 2816], F32, kind="ExternalInput")
    dnw_d = nc.dram_tensor("dnw", [8 * 1408, D], F32, kind="ExternalInput")
    wgu_d = nc.dram_tensor("wgu1", [D, 16384], F32, kind="ExternalInput")
    wdn_d = nc.dram_tensor("wdn1", [8192, D], F32, kind="ExternalInput")
    cos0_d = nc.dram_tensor("cos0", [64, T], BF16, kind="ExternalInput")
    sin0_d = nc.dram_tensor("sin0", [64, T], BF16, kind="ExternalInput")
    cos1_d = nc.dram_tensor("cos1", [128, T], BF16, kind="ExternalInput")
    sin1_d = nc.dram_tensor("sin1", [128, T], BF16, kind="ExternalInput")
    masks_d = nc.dram_tensor("masks", [4, 128, 512], BF16, kind="ExternalInput")
    idn_d = nc.dram_tensor("idn_bf", [128, 128], BF16, kind="ExternalInput")
    idf_d = nc.dram_tensor("idn_f32", [128, 128], F32, kind="ExternalInput")
    oc_d = nc.dram_tensor("ones_col", [128, 1], BF16, kind="ExternalInput")
    orw_d = nc.dram_tensor("ones_row", [1, 128], BF16, kind="ExternalInput")
    ln1b = nc.dram_tensor("ln1b", [D, TLOC], BF16)
    ag1 = nc.dram_tensor("ag1", [8 * D, TLOC], BF16)
    ob0 = nc.dram_tensor("ob0", [D, TLOC], BF16)
    oag0 = nc.dram_tensor("oag0", [8 * D, TLOC], BF16)
    x1d = nc.dram_tensor("x1d", [D, TLOC], F32)
    ln2d = nc.dram_tensor("ln2d", [D, TLOC], BF16)
    x2d = nc.dram_tensor("x2d", [D, TLOC], F32)
    ln3b = nc.dram_tensor("ln3b", [D, TLOC], BF16)
    ag3 = nc.dram_tensor("ag3", [8 * D, TLOC], BF16)
    ob1 = nc.dram_tensor("ob1", [D, TLOC], BF16)
    oag1 = nc.dram_tensor("oag1", [8 * D, TLOC], BF16)
    x3d = nc.dram_tensor("x3d", [D, TLOC], F32)
    ln4d = nc.dram_tensor("ln4d", [D, TLOC], BF16)

    RG = [list(range(NCORES))]

    def agtile(ag):
        agv = ag.ap().rearrange("(b c p) t -> p b c t", p=128, b=NCORES)
        def f(tw):
            return agv[:, tw // 2, :, 512 * (tw % 2):512 * (tw % 2) + 512]
        return f

    def obdst(ob, DH):
        def f(h, qc):
            r0 = 256 * (qc // 2) + DH * h
            c0 = 512 * (qc % 2)
            return ob[r0:r0 + DH, c0:c0 + 512]
        return f

    with ExitStack() as ex:
        names = ("sd", "sdx", "sdg", "sdw0", "sdw1", "sda0", "sda1",
                 "sdo0", "sdo1", "swa", "swb", "st", "sv", "sc", "sp", "bar",
                 "cc1", "cc2", "cc3", "cc4")
        sems = {n: ex.enter_context(nc.semaphore(n)) for n in names}
        coff_sb = ex.enter_context(nc.sbuf_tensor("coff_sb", [1, 1], I32))
        ro = ex.enter_context(nc.gpsimd.register("ro"))
        blk = ex.enter_context(nc.Block())
        S = Sched()

        ro2 = ex.enter_context(nc.gpsimd.register("ro2"))
        S.emit("gpsimd", lambda e: e.dma_start(out=coff_sb[:, :],
               in_=coff_d[:, :]), inc=("sdg", 16))
        S.emit("gpsimd", lambda e: e.reg_load(ro, coff_sb[0:1, 0:1]),
               waits=[("sdg", 16)])
        S.emit("gpsimd", lambda e: e.reg_add(ro2, ro, 128 * TLOC))

        with ExitStack() as exA, \
             nc.psum_tensor("A_ps_stats", [1, 2048], F32) as ps_stats, \
             nc.psum_tensor("A_ps_bc", [128, 2048], F32) as ps_bc:
            x_sb = exA.enter_context(nc.sbuf_tensor("A_x", [128, CC * 1024], F32))
            o_sb = exA.enter_context(nc.sbuf_tensor("A_o", [128, CC * 1024], BF16))
            oc_sb = exA.enter_context(nc.sbuf_tensor("A_oc", [128, 1], BF16))
            orw_sb = exA.enter_context(nc.sbuf_tensor("A_orw", [1, 128], BF16))
            lb = ln_bufs(nc, exA, "A_")
            S.emit("sync", lambda e: e.dma_start(
                   out=x_sb[:, :].rearrange("p (c t) -> p c t", t=TLOC),
                   in_=xin.ap().rearrange("(c p) t -> p c t", p=128)),
                   inc=("sd", 16))
            S.emit("sync", lambda e: e.dma_start(out=oc_sb[:, :], in_=oc_d[:, :]),
                   inc=("sd", 16))
            S.emit("sync", lambda e: e.dma_start(out=orw_sb[:, :], in_=orw_d[:, :]),
                   inc=("sd", 16))
            sd_c = S.get("sd")
            vfin = ln_emit(S, x_sb, o_sb, lb["xsq"], lb["t1"], lb["srow"],
                           lb["rowbf"], lb["abc"], lb["bbc"], ps_stats, ps_bc,
                           oc_sb, orw_sb, sd_c, wx=[("sd", sd_c)])
            S.emit("sync", lambda e: e.dma_start(
                   out=ln1b.ap().rearrange("(c p) t -> p c t", p=128),
                   in_=o_sb[:, :].rearrange("p (c t) -> p c t", t=TLOC)),
                   waits=[("sv", vfin)], inc=("sdo0", 16))
            S.emit("sync", lambda e: e.nop(), waits=[("sdo0", S.get("sdo0"))])
            S.flush(blk, sems)
        emit_barrier(S, sems)
        S.flush(blk, sems)

        S.emit("gpsimd", lambda e: e.collective_compute(
               "AllGather", ALU.bypass, replica_groups=RG,
               ins=[ln1b.ap().opt()], outs=[ag1.ap().opt()]), inc=("cc1", 1))

        emit_attn(nc, S, blk, sems, 0, T, "a0", agtile(ag1), obdst(ob0, 64),
                  cos0_d, sin0_d, masks_d, idn_d, orw_d, oc_d, wall0, None,
                  src_ready=[("cc1", 1)])
        emit_barrier(S, sems)
        S.flush(blk, sems)

        S.emit("gpsimd", lambda e: e.collective_compute(
               "AllGather", ALU.bypass, replica_groups=RG,
               ins=[ob0.ap().opt()], outs=[oag0.ap().opt()]), inc=("cc2", 1))

        def mk_o_load(oag):
            def f(dest):
                dmas = []
                for hh, reg in ((0, ro), (1, ro2)):
                    dmas.append((lambda e, hh=hh, reg=reg: e.dma_start(
                        out=bass.AP(dest, hh * TLOC,
                                    [[CC * TLOC, 128], [2 * TLOC, 8],
                                     [1, TLOC]]),
                        in_=bass.AP(oag, reg,
                                    [[TLOC, 128], [2048 * TLOC, 8],
                                     [1, TLOC]]))))
                return dmas
            return f
        o_load0 = mk_o_load(oag0)
        emit_proj(nc, S, blk, sems, 0, "p0", xin, o_load0, wo0_d, x1d, ln2d,
                  oc_d, orw_d, None, o_eng="gpsimd", o_sem="sdg",
                  o_wait=[("cc2", 1)])
        emit_barrier(S, sems)
        S.flush(blk, sems)

        emit_moe(nc, S, blk, sems, "m", x1d, ln2d, gw_d, gb_d, sgw_d, sdw_d,
                 upw_d, dnw_d, x2d, ln3b, oc_d, orw_d, idf_d)
        emit_barrier(S, sems)
        S.flush(blk, sems)

        S.emit("gpsimd", lambda e: e.collective_compute(
               "AllGather", ALU.bypass, replica_groups=RG,
               ins=[ln3b.ap().opt()], outs=[ag3.ap().opt()]), inc=("cc3", 1))

        emit_attn(nc, S, blk, sems, 1, T, "a1", agtile(ag3), obdst(ob1, 128),
                  cos1_d, sin1_d, masks_d, idn_d, orw_d, oc_d, wall1, bqkv_d,
                  src_ready=[("cc3", 1)])
        emit_barrier(S, sems)
        S.flush(blk, sems)

        S.emit("gpsimd", lambda e: e.collective_compute(
               "AllGather", ALU.bypass, replica_groups=RG,
               ins=[ob1.ap().opt()], outs=[oag1.ap().opt()]), inc=("cc4", 1))

        _scope(nc, "ph6_proj1")
        o_load1 = mk_o_load(oag1)
        emit_proj(nc, S, blk, sems, 1, "p1", x2d, o_load1, wo1_d, x3d, ln4d,
                  oc_d, orw_d, bo_d, o_eng="gpsimd", o_sem="sdg",
                  o_wait=[("cc4", 1)])
        emit_barrier(S, sems)
        S.flush(blk, sems)

        emit_ffn1(nc, S, blk, sems, "g", x3d, ln4d, wgu_d, wdn_d, xo)
        S.flush(blk, sems)
    return nc


def kernel_fused(per_core, inputs):
    cores = list(range(NCORES))
    ncf = _get("FUSED", build_fused)
    maps = []
    for r, d in enumerate(per_core):
        maps.append({
            "xin": d["xin"], "coff": np.array([[256 * r * TLOC]], np.int32),
            "wall0": d["wall0"], "wall1": d["wall1"], "bqkv1": d["bqkv1"],
            "wo0": d["wo0"], "wo1": d["wo1"],
            "bo": _cc(np.asarray(inputs["L1_attn_b_out"]).reshape(CC, 128).T
                      .astype(np.float32)),
            "gw": d["gw"], "gbbc": d["gbbc"], "sgw": d["sgw"], "sdw": d["sdw"],
            "upw": d["upw"].reshape(8 * D, 2816),
            "dnw": d["dnw"].reshape(8 * 1408, D),
            "wgu1": d["wgu1"], "wdn1": d["wdn1"],
            "cos0": d["cos0"], "sin0": d["sin0"],
            "cos1": d["cos1"], "sin1": d["sin1"],
            "masks": d["masks"], "idn_bf": d["idn_bf"], "idn_f32": d["idn_f32"],
            "ones_col": d["ones_col"], "ones_row": d["ones_row"]})
    res = run_bass_kernel_spmd(ncf, maps, core_ids=cores)
    out = np.zeros((1, T, D), dtype=np.float32)
    for r in cores:
        out[0, TLOC * r:TLOC * (r + 1), :] = \
            np.asarray(res.results[r]["xo"], dtype=np.float32).T
    return out


# version 06
# speedup vs baseline: 1.0234x; 1.0234x over previous
from contextlib import ExitStack

import numpy as np
import ml_dtypes

import concourse.bass as bass
import concourse.mybir as mybir
from concourse.bass_utils import run_bass_kernel_spmd

BFnp = ml_dtypes.bfloat16
F32 = mybir.dt.float32
BF16 = mybir.dt.bfloat16
ALU = mybir.AluOpType
AF = mybir.ActivationFunctionType
AX = mybir.AxisListType

NCORES = 8
CONV_ENG = "vector"
D = 2048
CC = D // 128
T = 8192
TLOC = T // NCORES
EPS = 1e-5

L0_HQ, L0_HK, L0_DH = 32, 8, 64
L1_H, L1_DH = 16, 128



def _rope0_tables():
    pos = np.arange(T, dtype=np.float64)
    inv = 1.0 / (10000.0 ** (np.arange(0, L0_DH, 2, dtype=np.float64) / L0_DH))
    fr = pos[:, None] * inv[None, :]
    c = np.cos(fr).T
    s = np.sin(fr).T
    return (np.concatenate([c, c], 0).astype(BFnp),
            np.concatenate([s, s], 0).astype(BFnp))


def _rope1_tables():
    half = L1_DH // 2
    pos = np.arange(T, dtype=np.float64)
    th = 10000.0 ** (-np.arange(half, dtype=np.float64) / half)
    ang = pos[:, None] * th[None, :]
    c = np.cos(ang).T
    s = np.sin(ang).T
    return (np.concatenate([c, c], 0).astype(BFnp),
            np.concatenate([s, s], 0).astype(BFnp))


def _causal_masks():
    m = np.zeros((4, 128, 512), dtype=BFnp)
    for i in range(4):
        d = i * 128
        k = np.arange(128)[:, None]
        q = np.arange(512)[None, :]
        m[i] = (k <= (q - d)).astype(BFnp)
    return m


_DEINT = np.concatenate([np.arange(0, 128, 2), np.arange(1, 128, 2)])


def _cc(a):
    return np.ascontiguousarray(a)


def prep_host(inp):
    x = np.asarray(inp["L0_attn_x"])
    wq = np.asarray(inp["L0_attn_Wq"])
    wkv = np.asarray(inp["L0_attn_Wkv"])
    wo = np.asarray(inp["L0_attn_Wo"])
    gw = np.asarray(inp["L0_ffn_gate_w"])
    gb = np.asarray(inp["L0_ffn_gate_bias"])
    sgw = np.asarray(inp["L0_ffn_shared_gw"])
    sdw = np.asarray(inp["L0_ffn_shared_dw"])
    upw = np.asarray(inp["L0_ffn_up_w"])
    dnw = np.asarray(inp["L0_ffn_down_w"])
    wqkv1 = np.asarray(inp["L1_attn_w_qkv"])
    bqkv1 = np.asarray(inp["L1_attn_b_qkv"])
    wo1 = np.asarray(inp["L1_attn_w_out"])
    bo1 = np.asarray(inp["L1_attn_b_out"])
    wgu1 = np.asarray(inp["L1_ffn_w_gate_up"])
    wdn1 = np.asarray(inp["L1_ffn_w_down"])

    cos0, sin0 = _rope0_tables()
    cos1, sin1 = _rope1_tables()
    masks = _causal_masks()
    ones_col = np.ones((128, 1), dtype=BFnp)
    ones_row = np.ones((1, 128), dtype=BFnp)
    idn_bf = np.eye(128, dtype=BFnp)
    idn_f32 = np.eye(128, dtype=np.float32)

    wqkv1T = wqkv1.T

    def _rot(w, half):
        return np.concatenate([-w[:, half:], w[:, :half]], axis=1)

    per_core = []
    for r in range(NCORES):
        d = {}
        d["xin"] = _cc(x[0, TLOC * r:TLOC * (r + 1), :].T)
        wq_r = wq[:, 256 * r:256 * (r + 1)]
        wk_r = wkv[:, 64 * r:64 * r + 64]
        wv_r = wkv[:, 512 + 64 * r:512 + 64 * r + 64]
        qrot = np.concatenate([_rot(wq_r[:, 64 * i:64 * (i + 1)], 32)
                               for i in range(4)], axis=1)
        krot = _rot(wk_r, 32)
        d["wall0"] = _cc(np.concatenate(
            [wq_r, wk_r, wv_r, qrot, krot], axis=1).astype(np.float32))
        d["wo0"] = _cc(wo)
        d["gw"] = _cc(gw.T)
        d["gbbc"] = _cc(np.broadcast_to(gb, (128, 8)))
        d["sgw"] = _cc(sgw.T)
        d["sdw"] = _cc(sdw.T)
        d["upw"] = _cc(upw.transpose(0, 2, 1))
        d["dnw"] = _cc(dnw.transpose(0, 2, 1))
        cols, bias = [], []
        rcols, rbias = [], []
        for base, perm in ((0, True), (D, True), (2 * D, False)):
            for h in (2 * r, 2 * r + 1):
                c0 = base + 128 * h
                blk = wqkv1T[:, c0:c0 + 128]
                bb = bqkv1[c0:c0 + 128]
                if perm:
                    blk = blk[:, _DEINT]
                    bb = bb[_DEINT]
                    rcols.append(_rot(blk, 64))
                    rbias.append(np.concatenate([-bb[64:], bb[:64]]))
                cols.append(blk)
                bias.append(bb)
        d["wall1"] = _cc(np.concatenate(cols + rcols, axis=1).astype(np.float32))
        bias12 = np.stack(bias + rbias + [np.zeros(128)] * 2, axis=1)
        d["bqkv1"] = _cc(bias12.astype(np.float32))
        d["wo1"] = _cc(wo1.T)
        d["bo1"] = _cc(bo1[:, None])
        d["wgu1"] = _cc(wgu1.T)
        d["wdn1"] = _cc(wdn1.T)
        d["cos0"], d["sin0"] = cos0, sin0
        d["cos1"], d["sin1"] = cos1, sin1
        d["masks"] = masks
        d["ones_col"] = ones_col
        d["ones_row"] = ones_row
        d["idn_bf"] = idn_bf
        d["idn_f32"] = idn_f32
        per_core.append(d)
    return per_core



class Ctr:
    def __init__(self):
        self.v = {}

    def bump(self, name, n=1):
        self.v[name] = self.v.get(name, 0) + n
        return self.v[name]

    def get(self, name):
        return self.v.get(name, 0)



class Sched:

    ENGINES = ("sync", "tensor", "vector", "scalar", "gpsimd")

    def __init__(self):
        self.ops = {e: [] for e in self.ENGINES}
        self.cnt = {}

    def get(self, sem):
        return self.cnt.get(sem, 0)

    def emit(self, eng, fn, waits=(), inc=None):
        w = [(s, v) for (s, v) in waits if v > 0]
        self.ops[eng].append((fn, w, inc))
        if inc is not None:
            self.cnt[inc[0]] = self.cnt.get(inc[0], 0) + inc[1]
            return self.cnt[inc[0]]
        return None

    def flush(self, blk, sems):
        for eng in self.ENGINES:
            ops = self.ops[eng]
            if not ops:
                continue

            def make(ops, eng=eng):
                def body(e):
                    waited = {}
                    pe = e if eng in ("tensor", "sync") else _EngProxy(e)
                    for fn, waits, inc in ops:
                        for s, v in waits:
                            if waited.get(s, 0) < v:
                                e.wait_ge(sems[s], v)
                                waited[s] = v
                        inst = fn(pe)
                        if inc is not None:
                            inst.then_inc(sems[inc[0]], inc[1])
                return body

            getattr(blk, eng)(make(ops))
        self.ops = {e: [] for e in self.ENGINES}


class _EngProxy:

    _PASS = ("wait_ge", "nop", "drain", "dma_start", "then_inc")

    def __init__(self, eng):
        self._e = eng
        self._written = []
        self._read = []

    @staticmethod
    def _iv(ap):
        lo = ap.offset
        hi = lo + 1
        for step, count in ap.ap:
            hi += (count - 1) * step
        return (ap.tensor.name, lo, hi)

    @staticmethod
    def _hits(iv, lst):
        n, lo, hi = iv
        for (n2, lo2, hi2) in lst:
            if n2 == n and lo < hi2 and lo2 < hi:
                return True
        return False

    def __getattr__(self, name):
        meth = getattr(self._e, name)
        if name in self._PASS or name.startswith("_"):
            return meth

        def call(*args, **kwargs):
            aps = [a for a in list(args) + list(kwargs.values())
                   if isinstance(a, bass.AP)]
            if aps:
                wr = self._iv(aps[0])
                rds = [self._iv(a) for a in aps[1:]]
                hazard = (self._hits(wr, self._written)
                          or self._hits(wr, self._read)
                          or any(self._hits(r, self._written) for r in rds))
                if hazard or len(self._written) + len(self._read) > 96:
                    self._e.drain()
                    self._written.clear()
                    self._read.clear()
                self._written.append(wr)
                self._read.extend(rds)
            return meth(*args, **kwargs)
        return call






def ln_emit(S, x_sb, out_sb, xsq, t1, srow, rowbf, abc, bbc,
            ps_stats, ps_bc, oc_sb, orw_sb, sd_consts, wx=()):
    CH = 1024
    m_off, tmp_off, a_off, b_off = 0, 1024, 2048, 3072
    v0 = S.get("sv")
    for c in range(CC):
        S.emit("vector", (lambda e, c=c: e.tensor_copy(
               out_sb[:, CH * c:CH * (c + 1)], x_sb[:, CH * c:CH * (c + 1)])),
               waits=list(wx))
        S.emit("vector", (lambda e, c=c: e.tensor_tensor(
               xsq[:, CH * c:CH * (c + 1)], out_sb[:, CH * c:CH * (c + 1)],
               out_sb[:, CH * c:CH * (c + 1)], ALU.mult)), inc=("sv", 1))
    for c in range(CC):
        for t in range(2):
            last = (c == CC - 1 and t == 1)
            S.emit("tensor", (lambda e, c=c, t=t: e.matmul(
                   ps_stats[0:1, 512 * t:512 * (t + 1)], oc_sb[:, 0:1],
                   out_sb[:, CH * c + 512 * t:CH * c + 512 * (t + 1)],
                   start=(c == 0), stop=(c == CC - 1))),
                   waits=[("sv", v0 + c + 1), ("sd", sd_consts)])
            S.emit("tensor", (lambda e, c=c, t=t: e.matmul(
                   ps_stats[0:1, 1024 + 512 * t:1536 + 512 * t], oc_sb[:, 0:1],
                   xsq[:, CH * c + 512 * t:CH * c + 512 * (t + 1)],
                   start=(c == 0), stop=(c == CC - 1))),
                   inc=("st", 1) if last else None)
    t_stats = S.get("st")
    for t in range(2):
        o = 512 * t
        S.emit("vector", (lambda e, o=o: e.tensor_scalar(
               srow[0:1, m_off + o:m_off + o + 512], ps_stats[0:1, o:o + 512],
               1.0 / D, None, ALU.mult)), waits=[("st", t_stats)])
        S.emit("vector", (lambda e, o=o: e.tensor_scalar(
               srow[0:1, tmp_off + o:tmp_off + o + 512],
               ps_stats[0:1, 1024 + o:1536 + o], 1.0 / D, EPS,
               ALU.mult, ALU.add)))
        S.emit("vector", (lambda e, o=o: e.tensor_tensor(
               srow[0:1, a_off + o:a_off + o + 512],
               srow[0:1, m_off + o:m_off + o + 512],
               srow[0:1, m_off + o:m_off + o + 512], ALU.mult)))
        S.emit("vector", (lambda e, o=o: e.tensor_tensor(
               srow[0:1, b_off + o:b_off + o + 512],
               srow[0:1, tmp_off + o:tmp_off + o + 512],
               srow[0:1, a_off + o:a_off + o + 512], ALU.subtract)))
        S.emit("vector", (lambda e, o=o: e.tensor_copy(
               srow[0:1, tmp_off + o:tmp_off + o + 512],
               srow[0:1, b_off + o:b_off + o + 512])), inc=("sv", 1))
    v_recip = S.get("sv")
    for t in range(2):
        o = 512 * t
        S.emit("scalar", (lambda e, o=o: e.activation(
               srow[0:1, b_off + o:b_off + o + 512],
               srow[0:1, tmp_off + o:tmp_off + o + 512], AF.Ln)),
               waits=[("sv", v_recip)])
        S.emit("scalar", (lambda e, o=o: e.activation(
               srow[0:1, a_off + o:a_off + o + 512],
               srow[0:1, b_off + o:b_off + o + 512], AF.Exp, scale=-0.5)),
               inc=("sc", 1))
    a_sqrt = S.get("sc")
    for t in range(2):
        o = 512 * t
        S.emit("vector", (lambda e, o=o: e.scalar_tensor_tensor(
               srow[0:1, b_off + o:b_off + o + 512],
               srow[0:1, m_off + o:m_off + o + 512], -1.0,
               srow[0:1, a_off + o:a_off + o + 512], ALU.mult, ALU.mult)),
               waits=[("sc", a_sqrt)])
        S.emit("vector", (lambda e, o=o: e.tensor_copy(
               rowbf[0:1, o:o + 512], srow[0:1, a_off + o:a_off + o + 512])))
        S.emit("vector", (lambda e, o=o: e.tensor_copy(
               rowbf[0:1, 1024 + o:1536 + o],
               srow[0:1, b_off + o:b_off + o + 512])), inc=("sv", 1))
    v_rows = S.get("sv")
    for i in range(4):
        o = 512 * i
        S.emit("tensor", (lambda e, o=o: e.matmul(
               ps_bc[:, o:o + 512], orw_sb[0:1, :], rowbf[0:1, o:o + 512],
               start=True, stop=True)),
               waits=[("sv", v_rows), ("sd", sd_consts)],
               inc=("st", 1) if i == 3 else None)
    t_bc = S.get("st")
    S.emit("vector", lambda e: e.tensor_copy(abc[:, :], ps_bc[:, 0:1024]),
           waits=[("st", t_bc)])
    S.emit("vector", lambda e: e.tensor_copy(bbc[:, :], ps_bc[:, 1024:2048]))
    for c in range(CC):
        S.emit("vector", (lambda e, c=c: e.tensor_tensor(
               t1[:, CH * (c % 2):CH * (c % 2 + 1)],
               out_sb[:, CH * c:CH * (c + 1)], abc[:, :], ALU.mult)))
        S.emit("vector", (lambda e, c=c: e.tensor_tensor(
               out_sb[:, CH * c:CH * (c + 1)],
               t1[:, CH * (c % 2):CH * (c % 2 + 1)], bbc[:, :], ALU.add)),
               inc=("sv", 1))
    return S.get("sv")


def ln_bufs(nc, ex, pref=""):
    b = {}
    b["xsq"] = ex.enter_context(nc.sbuf_tensor(pref + "xsq", [128, CC * 1024], BF16))
    b["t1"] = ex.enter_context(nc.sbuf_tensor(pref + "t1", [128, 2 * 1024], BF16))
    b["srow"] = ex.enter_context(nc.sbuf_tensor(pref + "srow", [1, 4096], F32))
    b["rowbf"] = ex.enter_context(nc.sbuf_tensor(pref + "rowbf", [1, 2048], BF16))
    b["abc"] = ex.enter_context(nc.sbuf_tensor(pref + "abc", [128, 1024], BF16))
    b["bbc"] = ex.enter_context(nc.sbuf_tensor(pref + "bbc", [128, 1024], BF16))
    return b



class WStream:

    def __init__(self, S, nc, ex, name, rows_chunks, maxw):
        self.S = S
        self.rc = rows_chunks
        self.maxw = maxw
        self.bf = ex.enter_context(
            nc.sbuf_tensor(name + "_bf", [128, 2 * rows_chunks * maxw], BF16))
        self.n = 0
        self.last_use = {0: 0, 1: 0}
        self.sems = ("swa", "swb")

    def load(self, dram_slice_3d, ncols, rows_chunks=None):
        rc = rows_chunks if rows_chunks is not None else self.rc
        slot = self.n % 2
        self.n += 1
        sem = self.sems[slot]
        S = self.S
        W = self.maxw
        bf = self.bf
        base = slot * self.rc * W
        S.emit("gpsimd", (lambda e: e.dma_start(
               out=bf[:, base:base + rc * ncols]
                   .rearrange("p (c f) -> p c f", f=ncols),
               in_=dram_slice_3d)),
               waits=[("st", self.last_use[slot])], inc=(sem, 16))
        return (slot, sem, S.get(sem), ncols, rc)

    def tile(self, h, c, coloff, M):
        slot, _, _, ncols, rc = h
        base = slot * self.rc * self.maxw
        return self.bf[:, base + c * ncols + coloff:base + c * ncols + coloff + M]

    def wready(self, h):
        return (h[1], h[2])

    def done(self, h):
        self.last_use[h[0]] = self.S.get("st")


def _ws_load4(ws, src_halves, ncols):
    S = ws.S
    slot = ws.n % 2
    ws.n += 1
    sem = ws.sems[slot]
    W = ws.maxw
    rc = ws.rc
    base = slot * rc * W
    bf = ws.bf
    hw = ncols // 2
    v = bf[:, base:base + rc * ncols].rearrange("p (c f) -> p c f", f=ncols)
    for s_ in range(2):
        S.emit("gpsimd", (lambda e, s_=s_: e.dma_start(
               out=v[:, :, hw * s_:hw * (s_ + 1)], in_=src_halves[s_])),
               waits=[("st", ws.last_use[slot])], inc=(sem, 16))
    return (slot, sem, S.get(sem), ncols, rc)
def build_attn(layer, Tt=T):
    nc = bass.Bass("TRN2", target_bir_lowering=False)
    HROWS = (4 * 64) if layer == 0 else (2 * 128)
    FW = 704 if layer == 0 else 1280
    TBL_P = 64 if layer == 0 else 128
    lnf = nc.dram_tensor("lnf", [D, Tt], BF16, kind="ExternalInput")
    ot = nc.dram_tensor("ot", [HROWS, Tt], BF16, kind="ExternalOutput")
    cos_d = nc.dram_tensor("cos", [TBL_P, Tt], BF16, kind="ExternalInput")
    sin_d = nc.dram_tensor("sin", [TBL_P, Tt], BF16, kind="ExternalInput")
    masks_d = nc.dram_tensor("masks", [4, 128, 512], BF16, kind="ExternalInput")
    idn_d = nc.dram_tensor("idn_bf", [128, 128], BF16, kind="ExternalInput")
    orw_d = nc.dram_tensor("ones_row", [1, 128], BF16, kind="ExternalInput")
    oc_d = nc.dram_tensor("ones_col", [128, 1], BF16, kind="ExternalInput")
    wall_d = nc.dram_tensor("wall", [D, FW], F32, kind="ExternalInput")
    bqkv_d = nc.dram_tensor("bqkv1", [128, 12], F32, kind="ExternalInput") \
        if layer == 1 else None
    lnf3 = lnf.ap().rearrange("(c p) t -> p c t", p=128)
    DH = 64 if layer == 0 else 128
    with ExitStack() as ex:
        sems = {n: ex.enter_context(nc.semaphore(n)) for n in
                ("sd", "sdw0", "sdw1", "sda0", "sda1", "sdo0", "sdo1",
                 "st", "sv", "sc", "sp")}
        blk = ex.enter_context(nc.Block())
        S = Sched()
        emit_attn(nc, S, blk, sems, layer, Tt, "a",
                  (lambda tw: lnf3[:, :, 512 * tw:512 * (tw + 1)]),
                  (lambda h, qc: ot[DH * h:DH * (h + 1),
                                    512 * qc:512 * (qc + 1)]),
                  cos_d, sin_d, masks_d, idn_d, orw_d, oc_d, wall_d, bqkv_d)
    return nc


def emit_attn(nc, S, blk, sems, layer, Tt, pref, lnf_tile, ot_dst,
              cos_d, sin_d, masks_d, idn_d, orw_d, oc_d, wall_d, bqkv_d,
              src_ready=()):
    if layer == 0:
        NH, DH = 4, 64
        scale = DH ** -0.5
        MCH = [("q", 0, 64, 0, 384), ("q", 1, 64, 64, 448),
               ("q", 2, 64, 128, 512), ("q", 3, 64, 192, 576),
               ("k", 0, 64, 256, 640), ("v", 0, 64, 320, None)]
        FW = 704
        TBL_P = 64
    else:
        NH, DH = 2, 128
        scale = DH ** -0.5
        MCH = [("q", 0, 128, 0, 768), ("q", 1, 128, 128, 896),
               ("k", 0, 128, 256, 1024), ("k", 1, 128, 384, 1152),
               ("v", 0, 128, 512, None), ("v", 1, 128, 640, None)]
        FW = 1280
        TBL_P = 128
    HROWS = NH * DH
    NKV = 1 if layer == 0 else NH
    QP = 64 if layer == 0 else 128
    NW = Tt // 512
    KT = Tt // 128
    VW = DH + 1

    with ExitStack() as ex:
        q_sb = ex.enter_context(nc.sbuf_tensor(pref + "q_sb", [QP, NH * Tt], BF16))
        k_sb = ex.enter_context(nc.sbuf_tensor(pref + "k_sb", [QP, NKV * Tt], BF16))
        v_sb = ex.enter_context(nc.sbuf_tensor(pref + "v_sb", [128, NKV * KT * VW], BF16))
        lnst = ex.enter_context(nc.sbuf_tensor(pref + "lnst", [128, 2 * CC * 512], BF16))
        cs_sb = ex.enter_context(nc.sbuf_tensor(pref + "cs_sb", [TBL_P, 2 * 2 * 512], BF16))
        masks_sb = ex.enter_context(nc.sbuf_tensor(pref + "masks_sb", [128, 4 * 512], BF16))
        w_bf = ex.enter_context(nc.sbuf_tensor(pref + "w_bf", [128, CC * FW], BF16))
        b_sb = ex.enter_context(nc.sbuf_tensor(pref + "b_sb", [128, 16], F32))
        ropeA = ex.enter_context(nc.sbuf_tensor(pref + "ropeA", [128, 512], F32))
        ropeB = ex.enter_context(nc.sbuf_tensor(pref + "ropeB", [128, 512], F32))
        vstg = ex.enter_context(nc.sbuf_tensor(pref + "vstg", [128, 512], BF16))
        p_sb = ex.enter_context(nc.sbuf_tensor(pref + "p_sb", [128, 2 * 512], BF16))
        otst = ex.enter_context(nc.sbuf_tensor(pref + "otst", [128, 2 * 512], BF16))
        qsq = ex.enter_context(nc.sbuf_tensor(pref + "qsq", [128, 512], BF16))
        nrow = ex.enter_context(nc.sbuf_tensor(pref + "nrow", [1, 64], F32))
        nsc = ex.enter_context(nc.sbuf_tensor(pref + "nsc", [1, 64], F32))
        nbf = ex.enter_context(nc.sbuf_tensor(pref + "nbf", [1, 8], BF16))
        rrow = ex.enter_context(nc.sbuf_tensor(pref + "rrow", [1, 512], F32))
        rbf = ex.enter_context(nc.sbuf_tensor(pref + "rbf", [1, 512], BF16))
        rbc = ex.enter_context(nc.sbuf_tensor(pref + "rbc", [128, 512], F32))
        idn_sb = ex.enter_context(nc.sbuf_tensor(pref + "idn_sb", [128, 128], BF16))
        orw_sb = ex.enter_context(nc.sbuf_tensor(pref + "orw_sb", [1, 128], BF16))
        oc_sb = ex.enter_context(nc.sbuf_tensor(pref + "oc_sb", [128, 1], BF16))
        S.emit("sync", lambda e: e.dma_start(
               out=masks_sb[:, :].rearrange("p (m t) -> p m t", t=512),
               in_=masks_d.ap().rearrange("m p t -> p m t")), inc=("sd", 16))
        S.emit("sync", lambda e: e.dma_start(out=idn_sb[:, :], in_=idn_d[:, :]),
               inc=("sd", 16))
        S.emit("sync", lambda e: e.dma_start(out=orw_sb[:, :], in_=orw_d[:, :]),
               inc=("sd", 16))
        S.emit("sync", lambda e: e.dma_start(out=oc_sb[:, :], in_=oc_d[:, :]),
               inc=("sd", 16))
        if layer == 1:
            S.emit("sync", lambda e: e.dma_start(
                out=b_sb[:, 0:12], in_=bqkv_d[:, :]), inc=("sd", 16))
        sd_consts = S.get("sd")
        for c in range(CC):
            S.emit("gpsimd", (lambda e, c=c:
                              e.dma_start(out=w_bf[:, FW * c:FW * (c + 1)],
                                          in_=wall_d[128 * c:128 * (c + 1), :])),
                   inc=("sdw0", 16))
        w_ready = ("sdw0", S.get("sdw0"))
        S.emit("vector", lambda e: e.memset(v_sb[:, :], 1.0), inc=("sv", 1))
        wconv_done_v = S.get("sv")

        with nc.psum_tensor(pref + "ps_qkv", [128, 4 * 512], F32) as ps_qkv, \
             nc.psum_tensor(pref + "ps_tr", [128, 512], BF16) as ps_tr:
            evac_cnt = {}
            win_mm = {}
            vstg_free = {}
            ucount = 0
            for tw in range(NW):
                asem = "sda%d" % (tw % 2)
                warw = ([("st", win_mm.get(tw - 2, 0))] if tw >= 2 else []) \
                    + list(src_ready)
                S.emit("gpsimd", (lambda e, tw=tw: e.dma_start(
                    out=lnst[:, (tw % 2) * CC * 512:(tw % 2 + 1) * CC * 512]
                        .rearrange("p (c t) -> p c t", t=512),
                    in_=lnf_tile(tw))),
                    waits=warw, inc=(asem, 16))
                S.emit("gpsimd", (lambda e, tw=tw: e.dma_start(
                    out=cs_sb[:, (tw % 2) * 1024:(tw % 2) * 1024 + 512],
                    in_=cos_d[:, 512 * tw:512 * (tw + 1)])),
                    waits=warw, inc=(asem, 16))
                S.emit("gpsimd", (lambda e, tw=tw: e.dma_start(
                    out=cs_sb[:, (tw % 2) * 1024 + 512:(tw % 2) * 1024 + 1024],
                    in_=sin_d[:, 512 * tw:512 * (tw + 1)])),
                    waits=warw, inc=(asem, 16))
                dma_done = S.get(asem)

                for mi, (kind, h, M, col, colrot) in enumerate(MCH):
                    sp = 2 * (ucount % 2)
                    ucount += 1
                    ps_m = ps_qkv[0:M, 512 * sp:512 * sp + 512]
                    ps_r = ps_qkv[0:M, 512 * (sp + 1):512 * (sp + 1) + 512]
                    chains = [(ps_m, col)] + ([(ps_r, colrot)]
                                              if colrot is not None else [])
                    for ps_o, coff in chains:
                        for c in range(CC):
                            w = []
                            if c == 0:
                                w = [(asem, dma_done), ("sv", wconv_done_v),
                                     w_ready,
                                     ("sv", evac_cnt.get(sp, 0)),
                                     ("sd", sd_consts)]
                            S.emit("tensor", (lambda e, ps_o=ps_o, coff=coff, c=c,
                                              M=M, tw=tw: e.matmul(
                                ps_o,
                                w_bf[:, FW * c + coff:FW * c + coff + M],
                                lnst[:, (tw % 2) * CC * 512 + 512 * c:
                                     (tw % 2) * CC * 512 + 512 * (c + 1)],
                                start=(c == 0), stop=(c == CC - 1))),
                                waits=w,
                                inc=("st", 1) if c == CC - 1 else None)
                    mm_done = S.get("st")
                    win_mm[tw] = mm_done

                    cosap = cs_sb[0:M, (tw % 2) * 1024:(tw % 2) * 1024 + 512]
                    sinap = cs_sb[0:M, (tw % 2) * 1024 + 512:(tw % 2) * 1024 + 1024]
                    if kind in ("q", "k"):
                        dst = q_sb if kind == "q" else k_sb
                        base = (h * Tt + 512 * tw)
                        if layer == 1:
                            S.emit("vector", (lambda e, ps_m=ps_m, cosap=cosap,
                                              mi=mi, M=M: e.scalar_tensor_tensor(
                                   ropeA[0:M, :], ps_m, b_sb[:, mi:mi + 1], cosap,
                                   ALU.add, ALU.mult)),
                                   waits=[("st", mm_done), (asem, dma_done),
                                          ("sd", sd_consts)])
                            S.emit("vector", (lambda e, ps_r=ps_r, sinap=sinap,
                                              mi=mi, M=M: e.scalar_tensor_tensor(
                                   ropeB[0:M, :], ps_r, b_sb[:, 6 + mi:7 + mi],
                                   sinap, ALU.add, ALU.mult)))
                        else:
                            S.emit("vector", (lambda e, ps_m=ps_m, cosap=cosap,
                                              M=M: e.tensor_tensor(
                                   ropeA[0:M, :], ps_m, cosap, ALU.mult)),
                                   waits=[("st", mm_done), (asem, dma_done),
                                          ("sd", sd_consts)])
                            S.emit("vector", (lambda e, ps_r=ps_r, sinap=sinap,
                                              M=M: e.tensor_tensor(
                                   ropeB[0:M, :], ps_r, sinap, ALU.mult)))
                        S.emit("vector", (lambda e, dst=dst, base=base, M=M:
                               e.tensor_tensor(dst[0:M, base:base + 512],
                                               ropeA[0:M, :], ropeB[0:M, :],
                                               ALU.add)), inc=("sv", 1))
                        evac_cnt[sp] = S.get("sv")
                    else:
                        vw = [("st", mm_done), ("st", vstg_free.get(h, 0)),
                              ("sd", sd_consts)]
                        if layer == 1:
                            S.emit("vector", (lambda e, ps_m=ps_m, M=M, mi=mi:
                                   e.tensor_scalar(vstg[0:M, :], ps_m,
                                                   b_sb[:, mi:mi + 1], None,
                                                   ALU.add)),
                                   waits=vw, inc=("sv", 1))
                        else:
                            S.emit("vector", (lambda e, ps_m=ps_m, M=M:
                                   e.tensor_copy(vstg[0:M, :], ps_m)),
                                   waits=vw, inc=("sv", 1))
                        evac_cnt[sp] = S.get("sv")
                        vdone = S.get("sv")
                        cp_prev = 0
                        for j in range(4):
                            kt = 4 * tw + j
                            S.emit("tensor", (lambda e, j=j, M=M:
                                   e.transpose(ps_tr[0:128, 0:M],
                                               vstg[0:M, 128 * j:128 * (j + 1)],
                                               idn_sb[0:M, 0:M])),
                                   waits=[("sv", vdone), ("sv", cp_prev),
                                          ("sd", sd_consts)],
                                   inc=("st", 1))
                            trd = S.get("st")
                            vb = h * KT * VW
                            if layer == 0:
                                S.emit("vector", (lambda e, kt=kt, vb=vb:
                                       e.tensor_copy(
                                           v_sb[:, vb + VW * kt:vb + VW * kt + 64],
                                           ps_tr[0:128, 0:64])),
                                       waits=[("st", trd)], inc=("sv", 1))
                            else:
                                S.emit("vector", (lambda e, kt=kt, vb=vb:
                                       e.tensor_copy(
                                           v_sb[:, vb + VW * kt:vb + VW * kt + 64],
                                           ps_tr[0:128, 0:64])),
                                       waits=[("st", trd)])
                                S.emit("vector", (lambda e, kt=kt, vb=vb:
                                       e.tensor_copy(
                                           v_sb[:, vb + VW * kt + 65:
                                                vb + VW * kt + 129],
                                           ps_tr[0:128, 64:128])),
                                       inc=("sv", 1))
                            cp_prev = S.get("sv")
                        vstg_free[h] = S.get("st")
            S.flush(blk, sems)

        qkv_done_v = S.get("sv")
        with nc.psum_tensor(pref + "ps_n", [1, 512], F32) as ps_n, \
             nc.psum_tensor(pref + "ps_b1", [128, 512], F32) as ps_b1:
            units = [("q", h) for h in range(NH)] + [("k", h) for h in range(NKV)]
            for ui, (kind, h) in enumerate(units):
                src_sb = q_sb if kind == "q" else k_sb
                for tc in range(NW):
                    base = h * Tt + 512 * tc
                    S.emit("vector", (lambda e, src_sb=src_sb, base=base:
                           e.tensor_tensor(qsq[0:QP, :],
                                           src_sb[0:QP, base:base + 512],
                                           src_sb[0:QP, base:base + 512],
                                           ALU.mult)),
                           waits=[("sv", qkv_done_v)], inc=("sv", 1))
                    sq_done = S.get("sv")
                    S.emit("tensor", (lambda e: e.matmul(
                           ps_n[0:1, :], oc_sb[0:QP, 0:1], qsq[0:QP, :],
                           start=True, stop=True)),
                           waits=[("sv", sq_done), ("sd", sd_consts)],
                           inc=("st", 1))
                    nm_done = S.get("st")
                    S.emit("vector", (lambda e, tc=tc: e.tensor_reduce(
                           nrow[0:1, tc:tc + 1], ps_n[0:1, :], axis=AX.X,
                           op=ALU.max)),
                           waits=[("st", nm_done)], inc=("sv", 1))
                S.emit("vector", (lambda e, ui=ui: e.tensor_reduce(
                       nsc[0:1, ui:ui + 1], nrow[0:1, 0:NW], axis=AX.X,
                       op=ALU.max)), inc=("sv", 1))
            bcol0 = 12
            for h in range(NH):
                kvi = NH + (0 if layer == 0 else h)
                S.emit("vector", (lambda e, h=h, kvi=kvi: e.tensor_tensor(
                       nsc[0:1, 8 + h:9 + h], nsc[0:1, h:h + 1],
                       nsc[0:1, kvi:kvi + 1], ALU.mult)), inc=("sv", 1))
                sv_m = S.get("sv")
                S.emit("scalar", (lambda e, h=h: e.activation(
                       nsc[0:1, 16 + h:17 + h], nsc[0:1, 8 + h:9 + h], AF.Sqrt)),
                       waits=[("sv", sv_m)], inc=("sc", 1))
                sc_s = S.get("sc")
                S.emit("vector", (lambda e, h=h, scale=scale: e.tensor_scalar(
                       nsc[0:1, 24 + h:25 + h], nsc[0:1, 16 + h:17 + h],
                       -scale, None, ALU.mult)),
                       waits=[("sc", sc_s)])
                S.emit("vector", (lambda e, h=h: e.tensor_copy(
                       nbf[0:1, h:h + 1], nsc[0:1, 24 + h:25 + h])), inc=("sv", 1))
                sv_nb = S.get("sv")
                S.emit("tensor", (lambda e, h=h: e.matmul(
                       ps_b1[:, 0:1], orw_sb[0:1, :], nbf[0:1, h:h + 1],
                       start=True, stop=True)),
                       waits=[("sv", sv_nb), ("sd", sd_consts)], inc=("st", 1))
                st_b = S.get("st")
                S.emit("vector", (lambda e, h=h, bcol0=bcol0: e.tensor_copy(
                       b_sb[:, bcol0 + h:bcol0 + h + 1], ps_b1[:, 0:1])),
                       waits=[("st", st_b)], inc=("sv", 1))
            S.flush(blk, sems)
        bias_done_v = S.get("sv")

        with nc.psum_tensor(pref + "ps_s", [128, 2 * 512], F32) as ps_s, \
             nc.psum_tensor(pref + "ps_pva", [65, 512], F32) as ps_pva, \
             nc.psum_tensor(pref + "ps_pvb", [64, 512], F32) as ps_pvb, \
             nc.psum_tensor(pref + "ps_bc2", [128, 512], F32) as ps_bc2:
            bcol0 = 12
            pvdone_prev = 0
            otst_free = {}
            for h in range(NH):
                kh = 0 if layer == 0 else h
                for qc in range(NW):
                    ktmax = 4 * qc + 4
                    scnt = {}
                    exp_cnt = {}
                    for kt in range(ktmax):
                        w = []
                        if kt < 2:
                            w.append(("sv", pvdone_prev))
                            w.append(("sv", bias_done_v))
                        S.emit("tensor", (lambda e, kt=kt, h=h, qc=qc, kh=kh:
                               e.matmul(
                                   ps_s[:, (kt % 2) * 512:(kt % 2) * 512 + 512],
                                   k_sb[0:QP, kh * Tt + 128 * kt:
                                        kh * Tt + 128 * (kt + 1)],
                                   q_sb[0:QP, h * Tt + 512 * qc:
                                        h * Tt + 512 * (qc + 1)],
                                   start=True, stop=True)),
                               waits=w, inc=("st", 1))
                        scnt[("S", kt)] = S.get("st")
                        we = [("st", scnt[("S", kt)])]
                        if kt >= 2:
                            we.append(("st", scnt.get(("PV", kt - 2), 0)))
                        S.emit("scalar", (lambda e, kt=kt, h=h: e.activation(
                               p_sb[:, (kt % 2) * 512:(kt % 2) * 512 + 512],
                               ps_s[:, (kt % 2) * 512:(kt % 2) * 512 + 512],
                               AF.Exp, bias=b_sb[:, bcol0 + h:bcol0 + h + 1],
                               scale=float(scale))),
                               waits=we, inc=("sc", 1))
                        exp_cnt[kt] = S.get("sc")
                        if kt >= 4 * qc:
                            mi = kt - 4 * qc
                            S.emit("vector", (lambda e, kt=kt, mi=mi:
                                   e.tensor_tensor(
                                   p_sb[:, (kt % 2) * 512:(kt % 2) * 512 + 512],
                                   p_sb[:, (kt % 2) * 512:(kt % 2) * 512 + 512],
                                   masks_sb[:, 512 * mi:512 * (mi + 1)],
                                   ALU.mult)),
                                   waits=[("sc", exp_cnt[kt]),
                                          ("sd", sd_consts)], inc=("sv", 1))
                            pready = [("sv", S.get("sv"))]
                        else:
                            pready = [("sc", exp_cnt[kt])]
                        vb = kh * KT * VW
                        if layer == 0:
                            vaps = [(ps_pva[0:65, :],
                                     (vb + VW * kt, vb + VW * kt + 65))]
                        else:
                            vaps = [(ps_pva[0:65, :],
                                     (vb + VW * kt, vb + VW * kt + 65)),
                                    (ps_pvb[0:64, :],
                                     (vb + VW * kt + 65, vb + VW * kt + 129))]
                        for i, (o, (v0, v1)) in enumerate(vaps):
                            S.emit("tensor", (lambda e, o=o, v0=v0, v1=v1, kt=kt,
                                              ktmax=ktmax: e.matmul(
                                   o, v_sb[:, v0:v1],
                                   p_sb[:, (kt % 2) * 512:(kt % 2) * 512 + 512],
                                   start=(kt == 0), stop=(kt == ktmax - 1))),
                                   waits=pready if i == 0 else [], inc=("st", 1))
                        scnt[("PV", kt)] = S.get("st")
                    pv_all = S.get("st")
                    osem = "sdo%d" % (qc % 2)
                    S.emit("scalar", (lambda e: e.activation(
                           rrow[0:1, :], ps_pva[64:65, :], AF.Ln)),
                           waits=[("st", pv_all),
                                  (osem, otst_free.get((h, qc), 0))])
                    S.emit("scalar", (lambda e: e.activation(
                           rbf[0:1, :], rrow[0:1, :], AF.Exp, scale=-1.0)),
                           inc=("sc", 1))
                    sv_r = S.get("sc")
                    S.emit("tensor", (lambda e: e.matmul(
                           ps_bc2[:, :], orw_sb[0:1, :], rbf[0:1, :],
                           start=True, stop=True)),
                           waits=[("sc", sv_r), ("sd", sd_consts)],
                           inc=("st", 1))
                    st_bc = S.get("st")
                    ob = (qc % 2) * 512
                    S.emit("vector", (lambda e: e.tensor_copy(
                           rbc[:, :], ps_bc2[:, :])), waits=[("st", st_bc)])
                    S.emit("vector", (lambda e, ob=ob: e.tensor_tensor(
                           otst[0:64, ob:ob + 512], ps_pva[0:64, :],
                           rbc[0:64, :], ALU.mult)), inc=("sv", 1))
                    if layer == 1:
                        S.emit("vector", (lambda e, ob=ob: e.tensor_tensor(
                               otst[64:128, ob:ob + 512], ps_pvb[0:64, :],
                               rbc[0:64, :], ALU.mult)), inc=("sv", 1))
                    pvdone_prev = S.get("sv")
                    nd = S.get("sv")
                    S.emit("gpsimd", (lambda e, h=h, qc=qc, ob=ob: e.dma_start(
                           out=ot_dst(h, qc),
                           in_=otst[0:DH, ob:ob + 512])),
                           waits=[("sv", nd)], inc=(osem, 16))
                    nqc = (h, qc + 2) if qc + 2 < NW else (h + 1, qc + 2 - NW)
                    otst_free[nqc] = S.get(osem)
            S.emit("gpsimd", lambda e: e.nop(),
                   waits=[("sdo0", S.get("sdo0")), ("sdo1", S.get("sdo1"))])
            S.flush(blk, sems)



def build_A():
    nc = bass.Bass("TRN2", target_bir_lowering=False)
    xin = nc.dram_tensor("xin", [D, TLOC], F32, kind="ExternalInput")
    ln1 = nc.dram_tensor("ln1", [D, TLOC], BF16, kind="ExternalOutput")
    oc_d = nc.dram_tensor("ones_col", [128, 1], BF16, kind="ExternalInput")
    orw_d = nc.dram_tensor("ones_row", [1, 128], BF16, kind="ExternalInput")

    with ExitStack() as ex:
        x_sb = ex.enter_context(nc.sbuf_tensor("x_sb", [128, CC * 1024], F32))
        o_sb = ex.enter_context(nc.sbuf_tensor("o_sb", [128, CC * 1024], BF16))
        lb = ln_bufs(nc, ex)
        oc_sb = ex.enter_context(nc.sbuf_tensor("oc_sb", [128, 1], BF16))
        orw_sb = ex.enter_context(nc.sbuf_tensor("orw_sb", [1, 128], BF16))
        ps_stats = ex.enter_context(nc.psum_tensor("ps_stats", [1, 2048], F32))
        ps_bc = ex.enter_context(nc.psum_tensor("ps_bc", [128, 2048], F32))
        sems = {n: ex.enter_context(nc.semaphore(n)) for n in
                ("sd", "sdo0", "st", "sv", "sc")}
        blk = ex.enter_context(nc.Block())
        S = Sched()
        S.emit("sync", lambda e: e.dma_start(
               out=x_sb[:, :].rearrange("p (c t) -> p c t", t=TLOC),
               in_=xin.ap().rearrange("(c p) t -> p c t", p=128)), inc=("sd", 16))
        S.emit("sync", lambda e: e.dma_start(out=oc_sb[:, :], in_=oc_d[:, :]),
               inc=("sd", 16))
        S.emit("sync", lambda e: e.dma_start(out=orw_sb[:, :], in_=orw_d[:, :]),
               inc=("sd", 16))
        sd_c = S.get("sd")
        vfin = ln_emit(S, x_sb, o_sb, lb["xsq"], lb["t1"], lb["srow"],
                       lb["rowbf"], lb["abc"], lb["bbc"], ps_stats, ps_bc,
                       oc_sb, orw_sb, sd_c, wx=[("sd", sd_c)])
        S.emit("sync", lambda e: e.dma_start(
               out=ln1.ap().rearrange("(c p) t -> p c t", p=128),
               in_=o_sb[:, :].rearrange("p (c t) -> p c t", t=TLOC)),
               waits=[("sv", vfin)], inc=("sdo0", 16))
        S.emit("sync", lambda e: e.nop(), waits=[("sdo0", 16)])
        S.flush(blk, {**sems, "swa": None, "swb": None})
    return nc



def build_proj(layer):
    nc = bass.Bass("TRN2", target_bir_lowering=False)
    xin = nc.dram_tensor("xin", [D, TLOC], F32, kind="ExternalInput")
    ofs = nc.dram_tensor("ofs", [D, TLOC], BF16, kind="ExternalInput")
    wo_d = nc.dram_tensor("wo", [D, D], F32, kind="ExternalInput")
    xo = nc.dram_tensor("xo", [D, TLOC], F32, kind="ExternalOutput")
    lno = nc.dram_tensor("lno", [D, TLOC], BF16, kind="ExternalOutput")
    oc_d = nc.dram_tensor("ones_col", [128, 1], BF16, kind="ExternalInput")
    orw_d = nc.dram_tensor("ones_row", [1, 128], BF16, kind="ExternalInput")
    bo_d = nc.dram_tensor("bo", [128, 16], F32, kind="ExternalInput") \
        if layer == 1 else None
    with ExitStack() as ex:
        sems = {n: ex.enter_context(nc.semaphore(n)) for n in
                ("sd", "sdx", "swa", "swb", "sdw0", "sdw1", "sdo0", "st", "sv",
                 "sc", "sp")}
        blk = ex.enter_context(nc.Block())
        S = Sched()
        o_load = (lambda dest: [lambda e: e.dma_start(
            out=dest[:, :].rearrange("p (c t) -> p c t", t=TLOC),
            in_=ofs.ap().rearrange("(c p) t -> p c t", p=128))])
        emit_proj(nc, S, blk, sems, layer, "p", xin, o_load, wo_d, xo, lno,
                  oc_d, orw_d, bo_d)
    return nc


def emit_proj(nc, S, blk, sems, layer, pref, xin, o_load, wo_d, xo, lno,
              oc_d, orw_d, bo_d, o_eng="sync", o_sem="sdx", o_wait=()):
    wo3 = wo_d.ap().rearrange("(c p) f -> p c f", p=128)

    with ExitStack() as ex:
        x_sb = ex.enter_context(nc.sbuf_tensor(pref + "x_sb", [128, CC * 1024], F32))
        o_sb = ex.enter_context(nc.sbuf_tensor(pref + "o_sb", [128, CC * 1024], BF16))
        ln_sb = ex.enter_context(nc.sbuf_tensor(pref + "ln_sb", [128, CC * 1024], BF16))
        oc_sb = ex.enter_context(nc.sbuf_tensor(pref + "oc_sb", [128, 1], BF16))
        orw_sb = ex.enter_context(nc.sbuf_tensor(pref + "orw_sb", [1, 128], BF16))
        bo_sb = ex.enter_context(nc.sbuf_tensor(pref + "bo_sb", [128, 16], F32))
        ex1 = ex.enter_context(ExitStack())
        ws = WStream(S, nc, ex1, pref + "wo", CC, 128)

        S.emit("sync", lambda e: e.dma_start(
               out=x_sb[:, :].rearrange("p (c t) -> p c t", t=TLOC),
               in_=xin.ap().rearrange("(c p) t -> p c t", p=128)), inc=("sdx", 16))
        for fdma in o_load(o_sb):
            S.emit(o_eng, fdma, waits=list(o_wait), inc=(o_sem, 16))
        S.emit("sync", lambda e: e.dma_start(out=oc_sb[:, :], in_=oc_d[:, :]),
               inc=("sd", 16))
        S.emit("sync", lambda e: e.dma_start(out=orw_sb[:, :], in_=orw_d[:, :]),
               inc=("sd", 16))
        if layer == 1:
            S.emit("sync", lambda e: e.dma_start(out=bo_sb[:, :], in_=bo_d[:, :]),
                   inc=("sd", 16))
        sd_c = S.get("sd")
        sdx_c = S.get("sdx")
        o_c = S.get(o_sem)

        with nc.psum_tensor(pref + "ps", [128, 2 * 512], F32) as ps:
            pe_war = {0: 0, 1: 0}
            hcur = ws.load(wo3[:, :, 0:128], 128)
            for f in range(CC):
                hnext = ws.load(wo3[:, :, 128 * (f + 1):128 * (f + 2)], 128) \
                    if f + 1 < CC else None
                for t in range(2):
                    for c in range(CC):
                        w = []
                        if c == 0:
                            w = [ws.wready(hcur), ("sdx", sdx_c),
                                 (o_sem, o_c), ("sv", pe_war[t])]
                        S.emit("tensor", (lambda e, f=f, t=t, c=c, hcur=hcur:
                               e.matmul(ps[:, (t % 2) * 512:(t % 2) * 512 + 512],
                                        ws.tile(hcur, c, 0, 128),
                                        o_sb[:, 1024 * c + 512 * t:
                                             1024 * c + 512 * t + 512],
                                        start=(c == 0), stop=(c == CC - 1))),
                               waits=w, inc=("st", 1) if c == CC - 1 else None)
                    mm_d = S.get("st")
                    if layer == 1:
                        S.emit("vector", (lambda e, f=f, t=t: e.scalar_tensor_tensor(
                               x_sb[:, 1024 * f + 512 * t:1024 * f + 512 * t + 512],
                               ps[:, (t % 2) * 512:(t % 2) * 512 + 512],
                               bo_sb[:, f:f + 1],
                               x_sb[:, 1024 * f + 512 * t:1024 * f + 512 * t + 512],
                               ALU.add, ALU.add)),
                               waits=[("st", mm_d), ("sd", sd_c)], inc=("sv", 1))
                    else:
                        S.emit("vector", (lambda e, f=f, t=t: e.tensor_tensor(
                               x_sb[:, 1024 * f + 512 * t:1024 * f + 512 * t + 512],
                               ps[:, (t % 2) * 512:(t % 2) * 512 + 512],
                               x_sb[:, 1024 * f + 512 * t:1024 * f + 512 * t + 512],
                               ALU.add)),
                               waits=[("st", mm_d), ("sdx", sdx_c)], inc=("sv", 1))
                    pe_war[t] = S.get("sv")
                ws.done(hcur)
                hcur = hnext
            x_done = S.get("sv")
            S.flush(blk, sems)
        ex1.close()

        with nc.psum_tensor(pref + "ps_stats", [1, 2048], F32) as ps_stats, \
             nc.psum_tensor(pref + "ps_bc", [128, 2048], F32) as ps_bc, \
             ExitStack() as ex2:
            lb = ln_bufs(nc, ex2, pref)
            vfin = ln_emit(S, x_sb, ln_sb, lb["xsq"], lb["t1"], lb["srow"],
                           lb["rowbf"], lb["abc"], lb["bbc"], ps_stats, ps_bc,
                           oc_sb, orw_sb, sd_c, wx=[("sv", x_done)])
            S.emit("sync", lambda e: e.dma_start(
                   out=xo.ap().rearrange("(c p) t -> p c t", p=128),
                   in_=x_sb[:, :].rearrange("p (c t) -> p c t", t=TLOC)),
                   waits=[("sv", x_done)], inc=("sdo0", 16))
            S.emit("sync", lambda e: e.dma_start(
                   out=lno.ap().rearrange("(c p) t -> p c t", p=128),
                   in_=ln_sb[:, :].rearrange("p (c t) -> p c t", t=TLOC)),
                   waits=[("sv", vfin)], inc=("sdo0", 16))
            S.emit("sync", lambda e: e.nop(), waits=[("sdo0", S.get("sdo0"))])
            S.flush(blk, sems)



def _gu_phase(S, nc, ws, ps, nmp, hoff, ln_sb, h_sb, sgw4, silu_first,
              we_ap=None, pair_cols=256, hw_st=0, ln_wait=(), war_in=None):
    evac_war = dict(war_in) if war_in else {}
    prev_h = 0
    hcur = _ws_load4(ws, sgw4(0), pair_cols)
    for mp in range(nmp):
        hnext = _ws_load4(ws, sgw4(mp + 1), pair_cols) if mp + 1 < nmp else None
        for t in range(2):
            idx = mp * 2 + t
            sp = 2 * (idx % 2)
            for half in range(2):
                for c in range(CC):
                    w = []
                    if c == 0:
                        w = [ws.wready(hcur),
                             ("sv", evac_war.get(sp + half, 0))] + list(ln_wait)
                    S.emit("tensor", (lambda e, c=c, half=half, sp=sp, t=t,
                                      hcur=hcur: e.matmul(
                           ps[:, (sp + half) * 512:(sp + half) * 512 + 512],
                           ws.tile(hcur, c, 128 * half, 128),
                           ln_sb[:, 1024 * c + 512 * t:1024 * c + 512 * t + 512],
                           start=(c == 0), stop=(c == CC - 1))),
                           waits=w, inc=("st", 1) if c == CC - 1 else None)
            mm_d = S.get("st")
            gh = 0 if silu_first else 1
            uh = 1 - gh
            S.emit("scalar", (lambda e, sp=sp, gh=gh: e.activation(
                   vstg_g[0:128, :], ps[:, (sp + gh) * 512:(sp + gh) * 512 + 512],
                   AF.Silu)), waits=[("sv", prev_h), ("st", mm_d)], inc=("sc", 1))
            sc_d = S.get("sc")
            hw = [("sc", sc_d), ("st", hw_st)]
            if we_ap is None:
                S.emit("vector", (lambda e, sp=sp, uh=uh, mp=mp, t=t,
                                  hoff=hoff: e.tensor_tensor(
                       h_sb[:, hoff + 1024 * mp + 512 * t:
                            hoff + 1024 * mp + 512 * t + 512],
                       vstg_g[0:128, :],
                       ps[:, (sp + uh) * 512:(sp + uh) * 512 + 512], ALU.mult)),
                       waits=hw, inc=("sv", 1))
            else:
                S.emit("vector", (lambda e, sp=sp, uh=uh: e.tensor_tensor(
                       vstg_h[0:128, :], vstg_g[0:128, :],
                       ps[:, (sp + uh) * 512:(sp + uh) * 512 + 512], ALU.mult)),
                       waits=hw)
                S.emit("vector", (lambda e, mp=mp, t=t, hoff=hoff, we_ap=we_ap:
                       e.tensor_tensor(
                       h_sb[:, hoff + 1024 * mp + 512 * t:
                            hoff + 1024 * mp + 512 * t + 512],
                       vstg_h[0:128, :], we_ap[:, 512 * t:512 * t + 512],
                       ALU.mult)), inc=("sv", 1))
            evac_war[sp + 0] = S.get("sv")
            evac_war[sp + 1] = S.get("sv")
            prev_h = S.get("sv")
        ws.done(hcur)
        hcur = hnext
    return S.get("sv"), evac_war


def _down_phase(S, nc, ws, ps, x_sb, h_sb, hoff, nch, dn3, h_ready, x_wait=(),
                war_in=None):
    evac_war = dict(war_in) if war_in else {}
    hcur = ws.load(dn3(0), 128, rows_chunks=nch)
    for f2 in range(CC):
        hnext = ws.load(dn3(f2 + 1), 128, rows_chunks=nch) if f2 + 1 < CC else None
        for t in range(2):
            sp = (f2 * 2 + t) % 4
            for c in range(nch):
                w = []
                if c == 0:
                    w = [ws.wready(hcur), ("sv", h_ready),
                         ("sv", evac_war.get(sp, 0))]
                S.emit("tensor", (lambda e, c=c, sp=sp, t=t, hcur=hcur,
                                  hoff=hoff: e.matmul(
                       ps[:, sp * 512:sp * 512 + 512],
                       ws.tile(hcur, c, 0, 128),
                       h_sb[:, hoff + 1024 * c + 512 * t:
                            hoff + 1024 * c + 512 * t + 512],
                       start=(c == 0), stop=(c == nch - 1))),
                       waits=w, inc=("st", 1) if c == nch - 1 else None)
            mm_d = S.get("st")
            S.emit("vector", (lambda e, f2=f2, t=t, sp=sp: e.tensor_tensor(
                   x_sb[:, 1024 * f2 + 512 * t:1024 * f2 + 512 * t + 512],
                   ps[:, sp * 512:sp * 512 + 512],
                   x_sb[:, 1024 * f2 + 512 * t:1024 * f2 + 512 * t + 512],
                   ALU.add)), waits=[("st", mm_d)] + list(x_wait),
                   inc=("sv", 1))
            evac_war[sp] = S.get("sv")
        ws.done(hcur)
        hcur = hnext
    return S.get("st"), evac_war


def build_moe():
    nc = bass.Bass("TRN2", target_bir_lowering=False)
    xin = nc.dram_tensor("xin", [D, TLOC], F32, kind="ExternalInput")
    lnin = nc.dram_tensor("lnin", [D, TLOC], BF16, kind="ExternalInput")
    gw_d = nc.dram_tensor("gw", [D, 8], F32, kind="ExternalInput")
    gb_d = nc.dram_tensor("gbbc", [128, 8], F32, kind="ExternalInput")
    sgw_d = nc.dram_tensor("sgw", [D, 5632], F32, kind="ExternalInput")
    sdw_d = nc.dram_tensor("sdw", [2816, D], F32, kind="ExternalInput")
    upw_d = nc.dram_tensor("upw", [8 * D, 2816], F32, kind="ExternalInput")
    dnw_d = nc.dram_tensor("dnw", [8 * 1408, D], F32, kind="ExternalInput")
    xo = nc.dram_tensor("xo", [D, TLOC], F32, kind="ExternalOutput")
    lno = nc.dram_tensor("lno", [D, TLOC], BF16, kind="ExternalOutput")
    oc_d = nc.dram_tensor("ones_col", [128, 1], BF16, kind="ExternalInput")
    orw_d = nc.dram_tensor("ones_row", [1, 128], BF16, kind="ExternalInput")
    idf_d = nc.dram_tensor("idn_f32", [128, 128], F32, kind="ExternalInput")
    with ExitStack() as ex:
        sems = {n: ex.enter_context(nc.semaphore(n)) for n in
                ("sd", "sdx", "swa", "swb", "sdw0", "sdw1", "sdo0", "st", "sv",
                 "sc", "sp")}
        blk = ex.enter_context(nc.Block())
        S = Sched()
        emit_moe(nc, S, blk, sems, "m", xin, lnin, gw_d, gb_d, sgw_d, sdw_d,
                 upw_d, dnw_d, xo, lno, oc_d, orw_d, idf_d)
    return nc


def emit_moe(nc, S, blk, sems, pref, xin, lnin, gw_d, gb_d, sgw_d, sdw_d,
             upw_d, dnw_d, xo, lno, oc_d, orw_d, idf_d):
    global vstg_g, vstg_h
    with ExitStack() as ex:
        x_sb = ex.enter_context(nc.sbuf_tensor(pref + "x_sb", [128, CC * 1024], F32))
        ln_sb = ex.enter_context(nc.sbuf_tensor(pref + "ln_sb", [128, CC * 1024], BF16))
        h_sb = ex.enter_context(nc.sbuf_tensor(pref + "h_sb", [128, 11 * 1024], BF16))
        vstg_g = ex.enter_context(nc.sbuf_tensor(pref + "vstg_g", [128, 512], BF16))
        vstg_h = ex.enter_context(nc.sbuf_tensor(pref + "vstg_h", [128, 512], BF16))
        webc = ex.enter_context(nc.sbuf_tensor(pref + "webc", [128, 1024], F32))
        oc_sb = ex.enter_context(nc.sbuf_tensor(pref + "oc_sb", [128, 1], BF16))
        orw_sb = ex.enter_context(nc.sbuf_tensor(pref + "orw_sb", [1, 128], BF16))
        idf_sb = ex.enter_context(nc.sbuf_tensor(pref + "idf_sb", [128, 128], F32))
        gb_sb = ex.enter_context(nc.sbuf_tensor(pref + "gb_sb", [128, 8], F32))
        gwbf = ex.enter_context(nc.sbuf_tensor(pref + "gwbf", [128, CC * 8], BF16))
        sfm = ex.enter_context(nc.sbuf_tensor(pref + "sfm", [8, 1024], F32))
        stm = ex.enter_context(nc.sbuf_tensor(pref + "stm", [128, 64], F32))
        ztm = ex.enter_context(nc.sbuf_tensor(pref + "ztm", [128, 64], F32))
        mtm = ex.enter_context(nc.sbuf_tensor(pref + "mtm", [128, 64], F32))
        m2tm = ex.enter_context(nc.sbuf_tensor(pref + "m2tm", [128, 64], F32))
        wetm = ex.enter_context(nc.sbuf_tensor(pref + "wetm", [128, 64], F32))
        rcol = ex.enter_context(nc.sbuf_tensor(pref + "rcol", [128, 8], F32))
        werow = ex.enter_context(nc.sbuf_tensor(pref + "werow", [1, 8 * 1024], BF16))
        S.emit("sync", lambda e: e.dma_start(
               out=x_sb[:, :].rearrange("p (c t) -> p c t", t=TLOC),
               in_=xin.ap().rearrange("(c p) t -> p c t", p=128)), inc=("sdx", 16))
        S.emit("sync", lambda e: e.dma_start(
               out=ln_sb[:, :].rearrange("p (c t) -> p c t", t=TLOC),
               in_=lnin.ap().rearrange("(c p) t -> p c t", p=128)), inc=("sdx", 16))
        S.emit("sync", lambda e: e.dma_start(out=oc_sb[:, :], in_=oc_d[:, :]),
               inc=("sd", 16))
        S.emit("sync", lambda e: e.dma_start(out=orw_sb[:, :], in_=orw_d[:, :]),
               inc=("sd", 16))
        S.emit("sync", lambda e: e.dma_start(out=idf_sb[:, :], in_=idf_d[:, :]),
               inc=("sd", 16))
        S.emit("sync", lambda e: e.dma_start(out=gb_sb[:, :], in_=gb_d[:, :]),
               inc=("sd", 16))
        S.emit("gpsimd", lambda e: e.dma_start(
               out=gwbf[:, :].rearrange("p (c f) -> p c f", f=8),
               in_=gw_d.ap().rearrange("(c p) f -> p c f", p=128)),
               inc=("sdw0", 16))
        gw_ready = ("sdw0", S.get("sdw0"))
        sd_c = S.get("sd")
        sdx_c = S.get("sdx")

        with nc.psum_tensor(pref + "ps_g", [8, 1024], F32) as ps_g, \
             nc.psum_tensor(pref + "ps_tr2", [128, 8], F32) as ps_tr2, \
             nc.psum_tensor(pref + "ps_wt", [1, 128], F32) as ps_wt:
            for t in range(2):
                for c in range(CC):
                    S.emit("tensor", (lambda e, t=t, c=c: e.matmul(
                           ps_g[0:8, 512 * t:512 * t + 512],
                           gwbf[:, 8 * c:8 * (c + 1)],
                           ln_sb[:, 1024 * c + 512 * t:1024 * c + 512 * t + 512],
                           start=(c == 0), stop=(c == CC - 1))),
                           waits=[gw_ready, ("sdx", sdx_c)] if c == 0 else [],
                           inc=("st", 1) if c == CC - 1 else None)
            gmm = S.get("st")
            S.emit("scalar", (lambda e: e.activation(
                   sfm[0:8, :], ps_g[0:8, :], AF.Sigmoid)),
                   waits=[("st", gmm)], inc=("sc", 1))
            sig_d = S.get("sc")
            for ti in range(8):
                S.emit("tensor", (lambda e, ti=ti: e.transpose(
                       ps_tr2[:, 0:8], sfm[0:8, 128 * ti:128 * (ti + 1)],
                       idf_sb[0:8, 0:8])),
                       waits=[("sc", sig_d), ("sd", sd_c),
                              ("sv", S.get("sv"))], inc=("st", 1))
                trd = S.get("st")
                S.emit("vector", (lambda e, ti=ti: e.tensor_copy(
                       stm[:, 8 * ti:8 * ti + 8], ps_tr2[:, 0:8])),
                       waits=[("st", trd)], inc=("sv", 1))
            sv_s = S.get("sv")
            for ti in range(8):
                o = 8 * ti
                S.emit("vector", (lambda e, o=o: e.tensor_tensor(
                       ztm[:, o:o + 8], stm[:, o:o + 8], gb_sb[:, 0:8], ALU.add)),
                       waits=[("sv", sv_s), ("sd", sd_c)])
                S.emit("vector", (lambda e, o=o: e.tensor_reduce(
                       rcol[:, 0:1], ztm[:, o:o + 8], axis=AX.X, op=ALU.max)))
                S.emit("vector", (lambda e, o=o: e.tensor_scalar(
                       mtm[:, o:o + 8], ztm[:, o:o + 8], rcol[:, 0:1], None,
                       ALU.is_equal)))
                S.emit("vector", (lambda e, o=o: e.tensor_tensor(
                       m2tm[:, o:o + 8], mtm[:, o:o + 8], stm[:, o:o + 8],
                       ALU.mult)))
                S.emit("vector", (lambda e, o=o: e.tensor_reduce(
                       rcol[:, 1:2], m2tm[:, o:o + 8], axis=AX.X, op=ALU.add)))
                S.emit("vector", (lambda e, o=o: e.scalar_tensor_tensor(
                       ztm[:, o:o + 8], mtm[:, o:o + 8], -1e30, ztm[:, o:o + 8],
                       ALU.mult, ALU.add)))
                S.emit("vector", (lambda e, o=o: e.tensor_reduce(
                       rcol[:, 2:3], ztm[:, o:o + 8], axis=AX.X, op=ALU.max)))
                S.emit("vector", (lambda e, o=o: e.tensor_scalar(
                       m2tm[:, o:o + 8], ztm[:, o:o + 8], rcol[:, 2:3], None,
                       ALU.is_equal)))
                S.emit("vector", (lambda e, o=o: e.tensor_tensor(
                       ztm[:, o:o + 8], m2tm[:, o:o + 8], stm[:, o:o + 8],
                       ALU.mult)))
                S.emit("vector", (lambda e, o=o: e.tensor_reduce(
                       rcol[:, 3:4], ztm[:, o:o + 8], axis=AX.X, op=ALU.add)))
                S.emit("vector", (lambda e: e.tensor_tensor(
                       rcol[:, 4:5], rcol[:, 1:2], rcol[:, 3:4], ALU.add)))
                S.emit("vector", (lambda e: e.reciprocal(
                       rcol[:, 5:6], rcol[:, 4:5])))
                S.emit("vector", (lambda e, o=o: e.tensor_scalar(
                       wetm[:, o:o + 8], mtm[:, o:o + 8], rcol[:, 1:2], None,
                       ALU.mult)))
                S.emit("vector", (lambda e, o=o: e.tensor_scalar(
                       ztm[:, o:o + 8], m2tm[:, o:o + 8], rcol[:, 3:4], None,
                       ALU.mult)))
                S.emit("vector", (lambda e, o=o: e.tensor_tensor(
                       wetm[:, o:o + 8], wetm[:, o:o + 8], ztm[:, o:o + 8],
                       ALU.add)))
                S.emit("vector", (lambda e, o=o: e.tensor_scalar(
                       wetm[:, o:o + 8], wetm[:, o:o + 8], rcol[:, 5:6], None,
                       ALU.mult)), inc=("sv", 1))
            we_d = S.get("sv")
            for ti in range(8):
                for e_ in range(8):
                    S.emit("tensor", (lambda e, ti=ti, e_=e_: e.transpose(
                           ps_wt[0:1, :], wetm[:, 8 * ti + e_:8 * ti + e_ + 1],
                           idf_sb[0:128, 0:128])),
                           waits=[("sv", we_d), ("sv", S.get("sv"))],
                           inc=("st", 1))
                    wtd = S.get("st")
                    S.emit("vector", (lambda e, ti=ti, e_=e_: e.tensor_copy(
                           werow[0:1, 1024 * e_ + 128 * ti:
                                 1024 * e_ + 128 * (ti + 1)], ps_wt[0:1, :])),
                           waits=[("st", wtd)], inc=("sv", 1))
            S.flush(blk, sems)
        werow_d = S.get("sv")

        sgw3 = sgw_d.ap().rearrange("(c p) f -> p c f", p=128)
        sdw3 = sdw_d.ap().rearrange("(c p) f -> p c f", p=128)
        upw3 = upw_d.ap().rearrange("(e c p) f -> e p c f", p=128, e=8)
        dnw3 = dnw_d.ap().rearrange("(e c p) f -> e p c f", p=128, e=8)

        def sgw_pair(j):
            return (sgw3[:, :, 128 * j:128 * (j + 1)],
                    sgw3[:, :, 2816 + 128 * j:2816 + 128 * (j + 1)])

        def upw_pair(e_, j):
            return (upw3[e_, :, :, 128 * j:128 * (j + 1)],
                    upw3[e_, :, :, 1408 + 128 * j:1408 + 128 * (j + 1)])

        with ExitStack() as ex1:
            ws = WStream(S, nc, ex1, pref + "moe", CC, 256)
            with nc.psum_tensor(pref + "ps_m", [128, 4 * 512], F32) as ps_m, \
                 nc.psum_tensor(pref + "ps_bc3", [128, 1024], F32) as ps_bc3:
                hw_st = 0
                pwar = None
                for rnd in range(2):
                    h_done, pwar = _gu_phase(S, nc, ws, ps_m, 11, 0, ln_sb, h_sb,
                                       (lambda mp, rnd=rnd:
                                        sgw_pair(11 * rnd + mp)),
                                       silu_first=False, we_ap=None, hw_st=hw_st,
                                       ln_wait=[("sdx", sdx_c)], war_in=pwar)
                    hw_st, pwar = _down_phase(S, nc, ws, ps_m, x_sb, h_sb, 0, 11,
                                        (lambda f2, rnd=rnd: sdw3[:,
                                         11 * rnd:11 * (rnd + 1),
                                         128 * f2:128 * (f2 + 1)]),
                                        h_done, x_wait=[("sdx", sdx_c)],
                                        war_in=pwar)
                for e_ in range(8):
                    for t in range(2):
                        S.emit("tensor", (lambda e, t=t, e_=e_: e.matmul(
                               ps_bc3[:, 512 * t:512 * t + 512], orw_sb[0:1, :],
                               werow[0:1, 1024 * e_ + 512 * t:
                                     1024 * e_ + 512 * t + 512],
                               start=True, stop=True)),
                               waits=[("sv", werow_d), ("sd", sd_c)],
                               inc=("st", 1))
                    bc_d = S.get("st")
                    S.emit("vector", (lambda e: e.tensor_copy(
                           webc[:, :], ps_bc3[:, :])),
                           waits=[("st", bc_d)], inc=("sv", 1))
                    h_done, pwar = _gu_phase(S, nc, ws, ps_m, 11, 0, ln_sb, h_sb,
                                       (lambda mp, e_=e_: upw_pair(e_, mp)),
                                       silu_first=True, we_ap=webc,
                                       hw_st=hw_st, ln_wait=[("sdx", sdx_c)],
                                       war_in=pwar)
                    hw_st, pwar = _down_phase(S, nc, ws, ps_m, x_sb, h_sb, 0, 11,
                                        (lambda f2, e_=e_: dnw3[e_, :, :,
                                         128 * f2:128 * (f2 + 1)]),
                                        h_done, x_wait=[("sdx", sdx_c)],
                                        war_in=pwar)
                x_done = S.get("sv")
                S.flush(blk, sems)

        with nc.psum_tensor(pref + "ps_stats", [1, 2048], F32) as ps_stats, \
             nc.psum_tensor(pref + "ps_bc", [128, 2048], F32) as ps_bc, \
             ExitStack() as ex2:
            lb = ln_bufs(nc, ex2, pref)
            vfin = ln_emit(S, x_sb, ln_sb, lb["xsq"], lb["t1"], lb["srow"],
                           lb["rowbf"], lb["abc"], lb["bbc"], ps_stats, ps_bc,
                           oc_sb, orw_sb, sd_c, wx=[("sv", x_done)])
            S.emit("sync", lambda e: e.dma_start(
                   out=xo.ap().rearrange("(c p) t -> p c t", p=128),
                   in_=x_sb[:, :].rearrange("p (c t) -> p c t", t=TLOC)),
                   waits=[("sv", x_done)], inc=("sdo0", 16))
            S.emit("sync", lambda e: e.dma_start(
                   out=lno.ap().rearrange("(c p) t -> p c t", p=128),
                   in_=ln_sb[:, :].rearrange("p (c t) -> p c t", t=TLOC)),
                   waits=[("sv", vfin)], inc=("sdo0", 16))
            S.emit("sync", lambda e: e.nop(), waits=[("sdo0", S.get("sdo0"))])
            S.flush(blk, sems)



def build_ffn1():
    nc = bass.Bass("TRN2", target_bir_lowering=False)
    xin = nc.dram_tensor("xin", [D, TLOC], F32, kind="ExternalInput")
    lnin = nc.dram_tensor("lnin", [D, TLOC], BF16, kind="ExternalInput")
    wgu_d = nc.dram_tensor("wgu1", [D, 16384], F32, kind="ExternalInput")
    wdn_d = nc.dram_tensor("wdn1", [8192, D], F32, kind="ExternalInput")
    xo = nc.dram_tensor("xo", [D, TLOC], F32, kind="ExternalOutput")
    with ExitStack() as ex:
        sems = {n: ex.enter_context(nc.semaphore(n)) for n in
                ("sd", "sdx", "swa", "swb", "sdw0", "sdw1", "sdo0", "st", "sv",
                 "sc", "sp")}
        blk = ex.enter_context(nc.Block())
        S = Sched()
        emit_ffn1(nc, S, blk, sems, "g", xin, lnin, wgu_d, wdn_d, xo)
    return nc


def emit_ffn1(nc, S, blk, sems, pref, xin, lnin, wgu_d, wdn_d, xo):
    global vstg_g, vstg_h
    with ExitStack() as ex:
        x_sb = ex.enter_context(nc.sbuf_tensor(pref + "x_sb", [128, CC * 1024], F32))
        ln_sb = ex.enter_context(nc.sbuf_tensor(pref + "ln_sb", [128, CC * 1024], BF16))
        h_sb = ex.enter_context(nc.sbuf_tensor(pref + "h_sb", [128, 16 * 1024], BF16))
        vstg_g = ex.enter_context(nc.sbuf_tensor(pref + "vstg_g", [128, 512], BF16))
        vstg_h = ex.enter_context(nc.sbuf_tensor(pref + "vstg_h", [128, 512], BF16))
        S.emit("sync", lambda e: e.dma_start(
               out=x_sb[:, :].rearrange("p (c t) -> p c t", t=TLOC),
               in_=xin.ap().rearrange("(c p) t -> p c t", p=128)), inc=("sdx", 16))
        S.emit("sync", lambda e: e.dma_start(
               out=ln_sb[:, :].rearrange("p (c t) -> p c t", t=TLOC),
               in_=lnin.ap().rearrange("(c p) t -> p c t", p=128)), inc=("sdx", 16))
        sdx_c = S.get("sdx")
        wgu3 = wgu_d.ap().rearrange("(c p) f -> p c f", p=128)
        wdn3 = wdn_d.ap().rearrange("(g c p) f -> g p c f", p=128, g=4)

        def wgu_pair(j):
            return (wgu3[:, :, 128 * j:128 * (j + 1)],
                    wgu3[:, :, 8192 + 128 * j:8192 + 128 * (j + 1)])

        with ExitStack() as ex1:
            ws = WStream(S, nc, ex1, pref + "f1", CC, 256)
            with nc.psum_tensor(pref + "ps_m", [128, 4 * 512], F32) as ps_m:
                hw_st = 0
                pwar = None
                for grp in range(4):
                    h_done, pwar = _gu_phase(S, nc, ws, ps_m, 16, 0, ln_sb, h_sb,
                                       (lambda mp, grp=grp:
                                        wgu_pair(16 * grp + mp)),
                                       silu_first=True, we_ap=None, hw_st=hw_st,
                                       ln_wait=[("sdx", sdx_c)], war_in=pwar)
                    hw_st, pwar = _down_phase(S, nc, ws, ps_m, x_sb, h_sb, 0, 16,
                                        (lambda f2, grp=grp: wdn3[grp, :, :,
                                         128 * f2:128 * (f2 + 1)]),
                                        h_done, x_wait=[("sdx", sdx_c)],
                                        war_in=pwar)
                x_done = S.get("sv")
                S.emit("sync", lambda e: e.dma_start(
                       out=xo.ap().rearrange("(c p) t -> p c t", p=128),
                       in_=x_sb[:, :].rearrange("p (c t) -> p c t", t=TLOC)),
                       waits=[("sv", x_done)], inc=("sdo0", 16))
                S.emit("sync", lambda e: e.nop(),
                       waits=[("sdo0", S.get("sdo0"))])
                S.flush(blk, sems)



_cache = {}


def _get(name, builder):
    if name not in _cache:
        _cache[name] = builder()
    return _cache[name]


def _consts(d):
    return {"ones_col": d["ones_col"], "ones_row": d["ones_row"]}


def kernel(**inputs):
    per_core = prep_host(inputs)
    return kernel_fused(per_core, inputs)


def kernel_multilaunch(**inputs):
    per_core = prep_host(inputs)
    cores = list(range(NCORES))

    def run(name, builder, maps):
        ncp = _get(name, builder)
        return run_bass_kernel_spmd(ncp, maps, core_ids=cores)

    resA = run("A", build_A, [{"xin": d["xin"], **_consts(d)} for d in per_core])
    ln1f = np.concatenate([np.asarray(resA.results[r]["ln1"]) for r in cores],
                          axis=1)
    resB0 = run("B0", lambda: build_attn(0),
                [{"lnf": ln1f, "wall": d["wall0"], "cos": d["cos0"],
                  "sin": d["sin0"], "masks": d["masks"], "idn_bf": d["idn_bf"],
                  **_consts(d)} for d in per_core])
    o0 = np.concatenate([np.asarray(resB0.results[r]["ot"]) for r in cores],
                        axis=0)
    resC0 = run("C0", lambda: build_proj(0),
                [{"xin": d["xin"], "ofs": _cc(o0[:, TLOC * r:TLOC * (r + 1)]),
                  "wo": d["wo0"], **_consts(d)}
                 for r, d in zip(cores, per_core)])
    resD = run("D", build_moe,
               [{"xin": np.asarray(resC0.results[r]["xo"]),
                 "lnin": np.asarray(resC0.results[r]["lno"]),
                 "gw": d["gw"], "gbbc": d["gbbc"], "sgw": d["sgw"],
                 "sdw": d["sdw"], "upw": d["upw"].reshape(8 * D, 2816),
                 "dnw": d["dnw"].reshape(8 * 1408, D),
                 "idn_f32": d["idn_f32"], **_consts(d)}
                for r, d in zip(cores, per_core)])
    ln3f = np.concatenate([np.asarray(resD.results[r]["lno"]) for r in cores],
                          axis=1)
    resB1 = run("B1", lambda: build_attn(1),
                [{"lnf": ln3f, "wall": d["wall1"], "bqkv1": d["bqkv1"],
                  "cos": d["cos1"], "sin": d["sin1"], "masks": d["masks"],
                  "idn_bf": d["idn_bf"], **_consts(d)} for d in per_core])
    o1 = np.concatenate([np.asarray(resB1.results[r]["ot"]) for r in cores],
                        axis=0)
    resC1 = run("C1", lambda: build_proj(1),
                [{"xin": np.asarray(resD.results[r]["xo"]),
                  "ofs": _cc(o1[:, TLOC * r:TLOC * (r + 1)]),
                  "wo": d["wo1"], "bo": _cc(np.asarray(
                      inputs["L1_attn_b_out"]).reshape(CC, 128).T.astype(
                          np.float32)), **_consts(d)}
                 for r, d in zip(cores, per_core)])
    resG = run("G", build_ffn1,
               [{"xin": np.asarray(resC1.results[r]["xo"]),
                 "lnin": np.asarray(resC1.results[r]["lno"]),
                 "wgu1": d["wgu1"], "wdn1": d["wdn1"]}
                for r, d in zip(cores, per_core)])

    out = np.zeros((1, T, D), dtype=np.float32)
    for r in cores:
        out[0, TLOC * r:TLOC * (r + 1), :] = \
            np.asarray(resG.results[r]["xo"], dtype=np.float32).T
    return out





_SCOPES = []


def _scope(nc, name):
    if _SCOPES:
        n, sid = _SCOPES.pop()
        nc.leave_named_scope(n, sid, False)
    if name is not None:
        sid, _ = nc.enter_named_scope(name, False)
        _SCOPES.append((name, sid))

def emit_barrier(S, sems):
    for eng in ("vector", "scalar", "gpsimd"):
        S.emit(eng, lambda e: e.drain())
    for eng in Sched.ENGINES:
        S.emit(eng, lambda e: e.nop(), inc=("bar", 1))
    tgt = S.get("bar")
    for eng in Sched.ENGINES:
        S.emit(eng, lambda e: e.nop(), waits=[("bar", tgt)])


def build_fused():
    nc = bass.Bass("TRN2", target_bir_lowering=False)
    I32 = mybir.dt.int32
    xin = nc.dram_tensor("xin", [D, TLOC], F32, kind="ExternalInput")
    xo = nc.dram_tensor("xo", [D, TLOC], F32, kind="ExternalOutput")
    coff_d = nc.dram_tensor("coff", [1, 1], I32, kind="ExternalInput")
    wall0 = nc.dram_tensor("wall0", [D, 704], F32, kind="ExternalInput")
    wall1 = nc.dram_tensor("wall1", [D, 1280], F32, kind="ExternalInput")
    bqkv_d = nc.dram_tensor("bqkv1", [128, 12], F32, kind="ExternalInput")
    wo0_d = nc.dram_tensor("wo0", [D, D], F32, kind="ExternalInput")
    wo1_d = nc.dram_tensor("wo1", [D, D], F32, kind="ExternalInput")
    bo_d = nc.dram_tensor("bo", [128, 16], F32, kind="ExternalInput")
    gw_d = nc.dram_tensor("gw", [D, 8], F32, kind="ExternalInput")
    gb_d = nc.dram_tensor("gbbc", [128, 8], F32, kind="ExternalInput")
    sgw_d = nc.dram_tensor("sgw", [D, 5632], F32, kind="ExternalInput")
    sdw_d = nc.dram_tensor("sdw", [2816, D], F32, kind="ExternalInput")
    upw_d = nc.dram_tensor("upw", [8 * D, 2816], F32, kind="ExternalInput")
    dnw_d = nc.dram_tensor("dnw", [8 * 1408, D], F32, kind="ExternalInput")
    wgu_d = nc.dram_tensor("wgu1", [D, 16384], F32, kind="ExternalInput")
    wdn_d = nc.dram_tensor("wdn1", [8192, D], F32, kind="ExternalInput")
    cos0_d = nc.dram_tensor("cos0", [64, T], BF16, kind="ExternalInput")
    sin0_d = nc.dram_tensor("sin0", [64, T], BF16, kind="ExternalInput")
    cos1_d = nc.dram_tensor("cos1", [128, T], BF16, kind="ExternalInput")
    sin1_d = nc.dram_tensor("sin1", [128, T], BF16, kind="ExternalInput")
    masks_d = nc.dram_tensor("masks", [4, 128, 512], BF16, kind="ExternalInput")
    idn_d = nc.dram_tensor("idn_bf", [128, 128], BF16, kind="ExternalInput")
    idf_d = nc.dram_tensor("idn_f32", [128, 128], F32, kind="ExternalInput")
    oc_d = nc.dram_tensor("ones_col", [128, 1], BF16, kind="ExternalInput")
    orw_d = nc.dram_tensor("ones_row", [1, 128], BF16, kind="ExternalInput")
    ln1b = nc.dram_tensor("ln1b", [D, TLOC], BF16)
    ag1 = nc.dram_tensor("ag1", [8 * D, TLOC], BF16, addr_space="Shared")
    ob0 = nc.dram_tensor("ob0", [D, TLOC], BF16)
    oag0 = nc.dram_tensor("oag0", [8 * D, TLOC], BF16, addr_space="Shared")
    x1d = nc.dram_tensor("x1d", [D, TLOC], F32)
    ln2d = nc.dram_tensor("ln2d", [D, TLOC], BF16)
    x2d = nc.dram_tensor("x2d", [D, TLOC], F32)
    ln3b = nc.dram_tensor("ln3b", [D, TLOC], BF16)
    ag3 = nc.dram_tensor("ag3", [8 * D, TLOC], BF16, addr_space="Shared")
    ob1 = nc.dram_tensor("ob1", [D, TLOC], BF16)
    oag1 = nc.dram_tensor("oag1", [8 * D, TLOC], BF16, addr_space="Shared")
    x3d = nc.dram_tensor("x3d", [D, TLOC], F32)
    ln4d = nc.dram_tensor("ln4d", [D, TLOC], BF16)

    RG = [list(range(NCORES))]

    def agtile(ag):
        agv = ag.ap().rearrange("(b c p) t -> p b c t", p=128, b=NCORES)
        def f(tw):
            return agv[:, tw // 2, :, 512 * (tw % 2):512 * (tw % 2) + 512]
        return f

    def obdst(ob, DH):
        def f(h, qc):
            r0 = 256 * (qc // 2) + DH * h
            c0 = 512 * (qc % 2)
            return ob[r0:r0 + DH, c0:c0 + 512]
        return f

    with ExitStack() as ex:
        names = ("sd", "sdx", "sdg", "sdw0", "sdw1", "sda0", "sda1",
                 "sdo0", "sdo1", "swa", "swb", "st", "sv", "sc", "sp", "bar",
                 "cc1", "cc2", "cc3", "cc4")
        sems = {n: ex.enter_context(nc.semaphore(n)) for n in names}
        coff_sb = ex.enter_context(nc.sbuf_tensor("coff_sb", [1, 1], I32))
        ro = ex.enter_context(nc.gpsimd.register("ro"))
        blk = ex.enter_context(nc.Block())
        S = Sched()

        ro2 = ex.enter_context(nc.gpsimd.register("ro2"))
        S.emit("gpsimd", lambda e: e.dma_start(out=coff_sb[:, :],
               in_=coff_d[:, :]), inc=("sdg", 16))
        S.emit("gpsimd", lambda e: e.reg_load(ro, coff_sb[0:1, 0:1]),
               waits=[("sdg", 16)])
        S.emit("gpsimd", lambda e: e.reg_add(ro2, ro, 128 * TLOC))

        with ExitStack() as exA, \
             nc.psum_tensor("A_ps_stats", [1, 2048], F32) as ps_stats, \
             nc.psum_tensor("A_ps_bc", [128, 2048], F32) as ps_bc:
            x_sb = exA.enter_context(nc.sbuf_tensor("A_x", [128, CC * 1024], F32))
            o_sb = exA.enter_context(nc.sbuf_tensor("A_o", [128, CC * 1024], BF16))
            oc_sb = exA.enter_context(nc.sbuf_tensor("A_oc", [128, 1], BF16))
            orw_sb = exA.enter_context(nc.sbuf_tensor("A_orw", [1, 128], BF16))
            lb = ln_bufs(nc, exA, "A_")
            S.emit("sync", lambda e: e.dma_start(
                   out=x_sb[:, :].rearrange("p (c t) -> p c t", t=TLOC),
                   in_=xin.ap().rearrange("(c p) t -> p c t", p=128)),
                   inc=("sd", 16))
            S.emit("sync", lambda e: e.dma_start(out=oc_sb[:, :], in_=oc_d[:, :]),
                   inc=("sd", 16))
            S.emit("sync", lambda e: e.dma_start(out=orw_sb[:, :], in_=orw_d[:, :]),
                   inc=("sd", 16))
            sd_c = S.get("sd")
            vfin = ln_emit(S, x_sb, o_sb, lb["xsq"], lb["t1"], lb["srow"],
                           lb["rowbf"], lb["abc"], lb["bbc"], ps_stats, ps_bc,
                           oc_sb, orw_sb, sd_c, wx=[("sd", sd_c)])
            S.emit("sync", lambda e: e.dma_start(
                   out=ln1b.ap().rearrange("(c p) t -> p c t", p=128),
                   in_=o_sb[:, :].rearrange("p (c t) -> p c t", t=TLOC)),
                   waits=[("sv", vfin)], inc=("sdo0", 16))
            S.emit("sync", lambda e: e.nop(), waits=[("sdo0", S.get("sdo0"))])
            S.flush(blk, sems)
        emit_barrier(S, sems)
        S.flush(blk, sems)

        S.emit("gpsimd", lambda e: e.collective_compute(
               "AllGather", ALU.bypass, replica_groups=RG,
               ins=[ln1b.ap().opt()], outs=[ag1.ap().opt()]), inc=("cc1", 1))

        emit_attn(nc, S, blk, sems, 0, T, "a0", agtile(ag1), obdst(ob0, 64),
                  cos0_d, sin0_d, masks_d, idn_d, orw_d, oc_d, wall0, None,
                  src_ready=[("cc1", 1)])
        emit_barrier(S, sems)
        S.flush(blk, sems)

        S.emit("gpsimd", lambda e: e.collective_compute(
               "AllGather", ALU.bypass, replica_groups=RG,
               ins=[ob0.ap().opt()], outs=[oag0.ap().opt()]), inc=("cc2", 1))

        def mk_o_load(oag):
            def f(dest):
                dmas = []
                for hh, reg in ((0, ro), (1, ro2)):
                    dmas.append((lambda e, hh=hh, reg=reg: e.dma_start(
                        out=bass.AP(dest, hh * TLOC,
                                    [[CC * TLOC, 128], [2 * TLOC, 8],
                                     [1, TLOC]]),
                        in_=bass.AP(oag, reg,
                                    [[TLOC, 128], [2048 * TLOC, 8],
                                     [1, TLOC]]))))
                return dmas
            return f
        o_load0 = mk_o_load(oag0)
        emit_proj(nc, S, blk, sems, 0, "p0", xin, o_load0, wo0_d, x1d, ln2d,
                  oc_d, orw_d, None, o_eng="gpsimd", o_sem="sdg",
                  o_wait=[("cc2", 1)])
        emit_barrier(S, sems)
        S.flush(blk, sems)

        emit_moe(nc, S, blk, sems, "m", x1d, ln2d, gw_d, gb_d, sgw_d, sdw_d,
                 upw_d, dnw_d, x2d, ln3b, oc_d, orw_d, idf_d)
        emit_barrier(S, sems)
        S.flush(blk, sems)

        S.emit("gpsimd", lambda e: e.collective_compute(
               "AllGather", ALU.bypass, replica_groups=RG,
               ins=[ln3b.ap().opt()], outs=[ag3.ap().opt()]), inc=("cc3", 1))

        emit_attn(nc, S, blk, sems, 1, T, "a1", agtile(ag3), obdst(ob1, 128),
                  cos1_d, sin1_d, masks_d, idn_d, orw_d, oc_d, wall1, bqkv_d,
                  src_ready=[("cc3", 1)])
        emit_barrier(S, sems)
        S.flush(blk, sems)

        S.emit("gpsimd", lambda e: e.collective_compute(
               "AllGather", ALU.bypass, replica_groups=RG,
               ins=[ob1.ap().opt()], outs=[oag1.ap().opt()]), inc=("cc4", 1))

        _scope(nc, "ph6_proj1")
        o_load1 = mk_o_load(oag1)
        emit_proj(nc, S, blk, sems, 1, "p1", x2d, o_load1, wo1_d, x3d, ln4d,
                  oc_d, orw_d, bo_d, o_eng="gpsimd", o_sem="sdg",
                  o_wait=[("cc4", 1)])
        emit_barrier(S, sems)
        S.flush(blk, sems)

        emit_ffn1(nc, S, blk, sems, "g", x3d, ln4d, wgu_d, wdn_d, xo)
        S.flush(blk, sems)
    return nc


def kernel_fused(per_core, inputs):
    cores = list(range(NCORES))
    ncf = _get("FUSED", build_fused)
    maps = []
    for r, d in enumerate(per_core):
        maps.append({
            "xin": d["xin"], "coff": np.array([[256 * r * TLOC]], np.int32),
            "wall0": d["wall0"], "wall1": d["wall1"], "bqkv1": d["bqkv1"],
            "wo0": d["wo0"], "wo1": d["wo1"],
            "bo": _cc(np.asarray(inputs["L1_attn_b_out"]).reshape(CC, 128).T
                      .astype(np.float32)),
            "gw": d["gw"], "gbbc": d["gbbc"], "sgw": d["sgw"], "sdw": d["sdw"],
            "upw": d["upw"].reshape(8 * D, 2816),
            "dnw": d["dnw"].reshape(8 * 1408, D),
            "wgu1": d["wgu1"], "wdn1": d["wdn1"],
            "cos0": d["cos0"], "sin0": d["sin0"],
            "cos1": d["cos1"], "sin1": d["sin1"],
            "masks": d["masks"], "idn_bf": d["idn_bf"], "idn_f32": d["idn_f32"],
            "ones_col": d["ones_col"], "ones_row": d["ones_row"]})
    res = run_bass_kernel_spmd(ncf, maps, core_ids=cores)
    out = np.zeros((1, T, D), dtype=np.float32)
    for r in cores:
        out[0, TLOC * r:TLOC * (r + 1), :] = \
            np.asarray(res.results[r]["xo"], dtype=np.float32).T
    return out


# version 08
# speedup vs baseline: 1.1808x; 1.1539x over previous
from contextlib import ExitStack

import numpy as np
import ml_dtypes

import concourse.bass as bass
import concourse.mybir as mybir
from concourse.bass_utils import run_bass_kernel_spmd

BFnp = ml_dtypes.bfloat16
F32 = mybir.dt.float32
BF16 = mybir.dt.bfloat16
ALU = mybir.AluOpType
AF = mybir.ActivationFunctionType
AX = mybir.AxisListType

NCORES = 8
CONV_ENG = "vector"
D = 2048
CC = D // 128
T = 8192
TLOC = T // NCORES
EPS = 1e-5

L0_HQ, L0_HK, L0_DH = 32, 8, 64
L1_H, L1_DH = 16, 128



def _rope0_tables():
    pos = np.arange(T, dtype=np.float64)
    inv = 1.0 / (10000.0 ** (np.arange(0, L0_DH, 2, dtype=np.float64) / L0_DH))
    fr = pos[:, None] * inv[None, :]
    c = np.cos(fr).T
    s = np.sin(fr).T
    return (np.concatenate([c, c], 0).astype(BFnp),
            np.concatenate([s, s], 0).astype(BFnp))


def _rope1_tables():
    half = L1_DH // 2
    pos = np.arange(T, dtype=np.float64)
    th = 10000.0 ** (-np.arange(half, dtype=np.float64) / half)
    ang = pos[:, None] * th[None, :]
    c = np.cos(ang).T
    s = np.sin(ang).T
    return (np.concatenate([c, c], 0).astype(BFnp),
            np.concatenate([s, s], 0).astype(BFnp))


def _causal_masks():
    m = np.zeros((4, 128, 512), dtype=BFnp)
    for i in range(4):
        d = i * 128
        k = np.arange(128)[:, None]
        q = np.arange(512)[None, :]
        m[i] = (k <= (q - d)).astype(BFnp)
    return m


_DEINT = np.concatenate([np.arange(0, 128, 2), np.arange(1, 128, 2)])


def _cc(a):
    return np.ascontiguousarray(a)


def prep_host(inp):
    x = np.asarray(inp["L0_attn_x"])
    wq = np.asarray(inp["L0_attn_Wq"])
    wkv = np.asarray(inp["L0_attn_Wkv"])
    wo = np.asarray(inp["L0_attn_Wo"])
    gw = np.asarray(inp["L0_ffn_gate_w"])
    gb = np.asarray(inp["L0_ffn_gate_bias"])
    sgw = np.asarray(inp["L0_ffn_shared_gw"])
    sdw = np.asarray(inp["L0_ffn_shared_dw"])
    upw = np.asarray(inp["L0_ffn_up_w"])
    dnw = np.asarray(inp["L0_ffn_down_w"])
    wqkv1 = np.asarray(inp["L1_attn_w_qkv"])
    bqkv1 = np.asarray(inp["L1_attn_b_qkv"])
    wo1 = np.asarray(inp["L1_attn_w_out"])
    bo1 = np.asarray(inp["L1_attn_b_out"])
    wgu1 = np.asarray(inp["L1_ffn_w_gate_up"])
    wdn1 = np.asarray(inp["L1_ffn_w_down"])

    cos0, sin0 = _rope0_tables()
    cos1, sin1 = _rope1_tables()
    masks = _causal_masks()
    ones_col = np.ones((128, 1), dtype=BFnp)
    ones_row = np.ones((1, 128), dtype=BFnp)
    idn_bf = np.eye(128, dtype=BFnp)
    idn_f32 = np.eye(128, dtype=np.float32)

    wqkv1T = wqkv1.T

    def _rot(w, half):
        return np.concatenate([-w[:, half:], w[:, :half]], axis=1)

    per_core = []
    for r in range(NCORES):
        d = {}
        d["xin"] = _cc(x[0, TLOC * r:TLOC * (r + 1), :].T)
        wq_r = wq[:, 256 * r:256 * (r + 1)]
        wk_r = wkv[:, 64 * r:64 * r + 64]
        wv_r = wkv[:, 512 + 64 * r:512 + 64 * r + 64]
        d["wall0"] = _cc(np.concatenate(
            [wq_r, wk_r, wv_r], axis=1).astype(np.float32))
        d["wo0"] = _cc(wo)
        d["gw"] = _cc(gw.T)
        d["gbbc"] = _cc(np.broadcast_to(gb, (128, 8)))
        d["sgw"] = _cc(sgw.T)
        d["sdw"] = _cc(sdw.T)
        d["upw"] = _cc(upw.transpose(0, 2, 1))
        d["dnw"] = _cc(dnw.transpose(0, 2, 1))
        cols, bias = [], []
        for base, perm in ((0, True), (D, True), (2 * D, False)):
            for h in (2 * r, 2 * r + 1):
                c0 = base + 128 * h
                blk = wqkv1T[:, c0:c0 + 128]
                bb = bqkv1[c0:c0 + 128]
                if perm:
                    blk = blk[:, _DEINT]
                    bb = bb[_DEINT]
                cols.append(blk)
                bias.append(bb)
        d["wall1"] = _cc(np.concatenate(cols, axis=1).astype(np.float32))
        bias12 = np.stack(bias + [np.zeros(128)] * 6, axis=1)
        d["bqkv1"] = _cc(bias12.astype(np.float32))
        def _pmat(dh):
            half = dh // 2
            P = np.zeros((dh, dh), dtype=BFnp)
            for i in range(half):
                P[i + half, i] = -1.0
                P[i, i + half] = 1.0
            return P
        d["pm0"] = _pmat(64)
        d["pm1"] = _pmat(128)
        d["wo1"] = _cc(wo1.T)
        d["bo1"] = _cc(bo1[:, None])
        d["wgu1"] = _cc(wgu1.T)
        d["wdn1"] = _cc(wdn1.T)
        d["cos0"], d["sin0"] = cos0, sin0
        d["cos1"], d["sin1"] = cos1, sin1
        d["masks"] = masks
        d["ones_col"] = ones_col
        d["ones_row"] = ones_row
        d["idn_bf"] = idn_bf
        d["idn_f32"] = idn_f32
        per_core.append(d)
    return per_core



class Ctr:
    def __init__(self):
        self.v = {}

    def bump(self, name, n=1):
        self.v[name] = self.v.get(name, 0) + n
        return self.v[name]

    def get(self, name):
        return self.v.get(name, 0)



class Sched:

    ENGINES = ("sync", "tensor", "vector", "scalar", "gpsimd")

    def __init__(self):
        self.ops = {e: [] for e in self.ENGINES}
        self.cnt = {}

    def get(self, sem):
        return self.cnt.get(sem, 0)

    def emit(self, eng, fn, waits=(), inc=None):
        w = [(s, v) for (s, v) in waits if v > 0]
        self.ops[eng].append((fn, w, inc))
        if inc is not None:
            self.cnt[inc[0]] = self.cnt.get(inc[0], 0) + inc[1]
            return self.cnt[inc[0]]
        return None

    def flush(self, blk, sems):
        for eng in self.ENGINES:
            ops = self.ops[eng]
            if not ops:
                continue

            def make(ops, eng=eng):
                def body(e):
                    waited = {}
                    pe = e if eng in ("tensor", "sync") else _EngProxy(e)
                    for fn, waits, inc in ops:
                        for s, v in waits:
                            if waited.get(s, 0) < v:
                                e.wait_ge(sems[s], v)
                                waited[s] = v
                        inst = fn(pe)
                        if inc is not None:
                            inst.then_inc(sems[inc[0]], inc[1])
                return body

            getattr(blk, eng)(make(ops))
        self.ops = {e: [] for e in self.ENGINES}


class _EngProxy:

    _PASS = ("wait_ge", "nop", "drain", "dma_start", "then_inc")

    def __init__(self, eng):
        self._e = eng
        self._written = []
        self._read = []

    @staticmethod
    def _iv(ap):
        lo = ap.offset
        hi = lo + 1
        for step, count in ap.ap:
            hi += (count - 1) * step
        return (ap.tensor.name, lo, hi)

    @staticmethod
    def _hits(iv, lst):
        n, lo, hi = iv
        for (n2, lo2, hi2) in lst:
            if n2 == n and lo < hi2 and lo2 < hi:
                return True
        return False

    def __getattr__(self, name):
        meth = getattr(self._e, name)
        if name in self._PASS or name.startswith("_"):
            return meth

        def call(*args, **kwargs):
            aps = [a for a in list(args) + list(kwargs.values())
                   if isinstance(a, bass.AP)]
            if aps:
                wr = self._iv(aps[0])
                rds = [self._iv(a) for a in aps[1:]]
                hazard = (self._hits(wr, self._written)
                          or self._hits(wr, self._read)
                          or any(self._hits(r, self._written) for r in rds))
                if hazard or len(self._written) + len(self._read) > 96:
                    self._e.drain()
                    self._written.clear()
                    self._read.clear()
                self._written.append(wr)
                self._read.extend(rds)
            return meth(*args, **kwargs)
        return call






def ln_emit(S, x_sb, out_sb, xsq, t1, srow, rowbf, abc, bbc,
            ps_stats, ps_bc, oc_sb, orw_sb, sd_consts, wx=()):
    CH = 1024
    m_off, tmp_off, a_off, b_off = 0, 1024, 2048, 3072
    v0 = S.get("sv")
    for c in range(CC):
        S.emit("vector", (lambda e, c=c: e.tensor_copy(
               out_sb[:, CH * c:CH * (c + 1)], x_sb[:, CH * c:CH * (c + 1)])),
               waits=list(wx))
        S.emit("vector", (lambda e, c=c: e.tensor_tensor(
               xsq[:, CH * c:CH * (c + 1)], out_sb[:, CH * c:CH * (c + 1)],
               out_sb[:, CH * c:CH * (c + 1)], ALU.mult)), inc=("sv", 1))
    for c in range(CC):
        for t in range(2):
            last = (c == CC - 1 and t == 1)
            S.emit("tensor", (lambda e, c=c, t=t: e.matmul(
                   ps_stats[0:1, 512 * t:512 * (t + 1)], oc_sb[:, 0:1],
                   out_sb[:, CH * c + 512 * t:CH * c + 512 * (t + 1)],
                   start=(c == 0), stop=(c == CC - 1))),
                   waits=[("sv", v0 + c + 1), ("sd", sd_consts)])
            S.emit("tensor", (lambda e, c=c, t=t: e.matmul(
                   ps_stats[0:1, 1024 + 512 * t:1536 + 512 * t], oc_sb[:, 0:1],
                   xsq[:, CH * c + 512 * t:CH * c + 512 * (t + 1)],
                   start=(c == 0), stop=(c == CC - 1))),
                   inc=("st", 1) if last else None)
    t_stats = S.get("st")
    for t in range(2):
        o = 512 * t
        S.emit("vector", (lambda e, o=o: e.tensor_scalar(
               srow[0:1, m_off + o:m_off + o + 512], ps_stats[0:1, o:o + 512],
               1.0 / D, None, ALU.mult)), waits=[("st", t_stats)])
        S.emit("vector", (lambda e, o=o: e.tensor_scalar(
               srow[0:1, tmp_off + o:tmp_off + o + 512],
               ps_stats[0:1, 1024 + o:1536 + o], 1.0 / D, EPS,
               ALU.mult, ALU.add)))
        S.emit("vector", (lambda e, o=o: e.tensor_tensor(
               srow[0:1, a_off + o:a_off + o + 512],
               srow[0:1, m_off + o:m_off + o + 512],
               srow[0:1, m_off + o:m_off + o + 512], ALU.mult)))
        S.emit("vector", (lambda e, o=o: e.tensor_tensor(
               srow[0:1, b_off + o:b_off + o + 512],
               srow[0:1, tmp_off + o:tmp_off + o + 512],
               srow[0:1, a_off + o:a_off + o + 512], ALU.subtract)))
        S.emit("vector", (lambda e, o=o: e.tensor_copy(
               srow[0:1, tmp_off + o:tmp_off + o + 512],
               srow[0:1, b_off + o:b_off + o + 512])), inc=("sv", 1))
    v_recip = S.get("sv")
    for t in range(2):
        o = 512 * t
        S.emit("scalar", (lambda e, o=o: e.activation(
               srow[0:1, b_off + o:b_off + o + 512],
               srow[0:1, tmp_off + o:tmp_off + o + 512], AF.Ln)),
               waits=[("sv", v_recip)])
        S.emit("scalar", (lambda e, o=o: e.activation(
               srow[0:1, a_off + o:a_off + o + 512],
               srow[0:1, b_off + o:b_off + o + 512], AF.Exp, scale=-0.5)),
               inc=("sc", 1))
    a_sqrt = S.get("sc")
    for t in range(2):
        o = 512 * t
        S.emit("vector", (lambda e, o=o: e.scalar_tensor_tensor(
               srow[0:1, b_off + o:b_off + o + 512],
               srow[0:1, m_off + o:m_off + o + 512], -1.0,
               srow[0:1, a_off + o:a_off + o + 512], ALU.mult, ALU.mult)),
               waits=[("sc", a_sqrt)])
        S.emit("vector", (lambda e, o=o: e.tensor_copy(
               rowbf[0:1, o:o + 512], srow[0:1, a_off + o:a_off + o + 512])))
        S.emit("vector", (lambda e, o=o: e.tensor_copy(
               rowbf[0:1, 1024 + o:1536 + o],
               srow[0:1, b_off + o:b_off + o + 512])), inc=("sv", 1))
    v_rows = S.get("sv")
    for i in range(4):
        o = 512 * i
        S.emit("tensor", (lambda e, o=o: e.matmul(
               ps_bc[:, o:o + 512], orw_sb[0:1, :], rowbf[0:1, o:o + 512],
               start=True, stop=True)),
               waits=[("sv", v_rows), ("sd", sd_consts)],
               inc=("st", 1) if i == 3 else None)
    t_bc = S.get("st")
    S.emit("vector", lambda e: e.tensor_copy(abc[:, :], ps_bc[:, 0:1024]),
           waits=[("st", t_bc)])
    S.emit("vector", lambda e: e.tensor_copy(bbc[:, :], ps_bc[:, 1024:2048]))
    for c in range(CC):
        S.emit("vector", (lambda e, c=c: e.tensor_tensor(
               t1[:, CH * (c % 2):CH * (c % 2 + 1)],
               out_sb[:, CH * c:CH * (c + 1)], abc[:, :], ALU.mult)))
        S.emit("vector", (lambda e, c=c: e.tensor_tensor(
               out_sb[:, CH * c:CH * (c + 1)],
               t1[:, CH * (c % 2):CH * (c % 2 + 1)], bbc[:, :], ALU.add)),
               inc=("sv", 1))
    return S.get("sv")


def ln_bufs(nc, ex, pref=""):
    b = {}
    b["xsq"] = ex.enter_context(nc.sbuf_tensor(pref + "xsq", [128, CC * 1024], BF16))
    b["t1"] = ex.enter_context(nc.sbuf_tensor(pref + "t1", [128, 2 * 1024], BF16))
    b["srow"] = ex.enter_context(nc.sbuf_tensor(pref + "srow", [1, 4096], F32))
    b["rowbf"] = ex.enter_context(nc.sbuf_tensor(pref + "rowbf", [1, 2048], BF16))
    b["abc"] = ex.enter_context(nc.sbuf_tensor(pref + "abc", [128, 1024], BF16))
    b["bbc"] = ex.enter_context(nc.sbuf_tensor(pref + "bbc", [128, 1024], BF16))
    return b



class WStream:

    def __init__(self, S, nc, ex, name, rows_chunks, maxw):
        self.S = S
        self.rc = rows_chunks
        self.maxw = maxw
        self.bf = ex.enter_context(
            nc.sbuf_tensor(name + "_bf", [128, 2 * rows_chunks * maxw], BF16))
        self.n = 0
        self.last_use = {0: 0, 1: 0}
        self.sems = ("swa", "swb")

    def load(self, dram_slice_3d, ncols, rows_chunks=None):
        rc = rows_chunks if rows_chunks is not None else self.rc
        slot = self.n % 2
        self.n += 1
        sem = self.sems[slot]
        S = self.S
        W = self.maxw
        bf = self.bf
        base = slot * self.rc * W
        S.emit("gpsimd", (lambda e: e.dma_start(
               out=bf[:, base:base + rc * ncols]
                   .rearrange("p (c f) -> p c f", f=ncols),
               in_=dram_slice_3d)),
               waits=[("st", self.last_use[slot])], inc=(sem, 16))
        return (slot, sem, S.get(sem), ncols, rc)

    def tile(self, h, c, coloff, M):
        slot, _, _, ncols, rc = h
        base = slot * self.rc * self.maxw
        return self.bf[:, base + c * ncols + coloff:base + c * ncols + coloff + M]

    def wready(self, h):
        return (h[1], h[2])

    def done(self, h):
        self.last_use[h[0]] = self.S.get("st")


def _ws_load4(ws, src_halves, ncols):
    S = ws.S
    slot = ws.n % 2
    ws.n += 1
    sem = ws.sems[slot]
    W = ws.maxw
    rc = ws.rc
    base = slot * rc * W
    bf = ws.bf
    hw = ncols // 2
    v = bf[:, base:base + rc * ncols].rearrange("p (c f) -> p c f", f=ncols)
    for s_ in range(2):
        S.emit("gpsimd", (lambda e, s_=s_: e.dma_start(
               out=v[:, :, hw * s_:hw * (s_ + 1)], in_=src_halves[s_])),
               waits=[("st", ws.last_use[slot])], inc=(sem, 16))
    return (slot, sem, S.get(sem), ncols, rc)
def build_attn(layer, Tt=T):
    nc = bass.Bass("TRN2", target_bir_lowering=False)
    HROWS = (4 * 64) if layer == 0 else (2 * 128)
    FW = 384 if layer == 0 else 768
    TBL_P = 64 if layer == 0 else 128
    lnf = nc.dram_tensor("lnf", [D, Tt], BF16, kind="ExternalInput")
    ot = nc.dram_tensor("ot", [HROWS, Tt], BF16, kind="ExternalOutput")
    cos_d = nc.dram_tensor("cos", [TBL_P, Tt], BF16, kind="ExternalInput")
    sin_d = nc.dram_tensor("sin", [TBL_P, Tt], BF16, kind="ExternalInput")
    masks_d = nc.dram_tensor("masks", [4, 128, 512], BF16, kind="ExternalInput")
    idn_d = nc.dram_tensor("idn_bf", [128, 128], BF16, kind="ExternalInput")
    orw_d = nc.dram_tensor("ones_row", [1, 128], BF16, kind="ExternalInput")
    oc_d = nc.dram_tensor("ones_col", [128, 1], BF16, kind="ExternalInput")
    wall_d = nc.dram_tensor("wall", [D, FW], F32, kind="ExternalInput")
    bqkv_d = nc.dram_tensor("bqkv1", [128, 12], F32, kind="ExternalInput") \
        if layer == 1 else None
    DH = 64 if layer == 0 else 128
    pm_d = nc.dram_tensor("pm", [DH, DH], BF16, kind="ExternalInput")
    lnf3 = lnf.ap().rearrange("(c p) t -> p c t", p=128)
    with ExitStack() as ex:
        sems = {n: ex.enter_context(nc.semaphore(n)) for n in
                ("sd", "sdw0", "sdw1", "sda0", "sda1", "sdo0", "sdo1",
                 "st", "sv", "sc", "sp")}
        blk = ex.enter_context(nc.Block())
        S = Sched()
        emit_attn(nc, S, blk, sems, layer, Tt, "a",
                  (lambda tw: lnf3[:, :, 512 * tw:512 * (tw + 1)]),
                  (lambda h, qc: ot[DH * h:DH * (h + 1),
                                    512 * qc:512 * (qc + 1)]),
                  cos_d, sin_d, masks_d, idn_d, orw_d, oc_d, wall_d, bqkv_d,
                  pm_d)
    return nc


def emit_attn(nc, S, blk, sems, layer, Tt, pref, lnf_tile, ot_dst,
              cos_d, sin_d, masks_d, idn_d, orw_d, oc_d, wall_d, bqkv_d,
              pm_d, src_ready=()):
    if layer == 0:
        NH, DH = 4, 64
        scale = DH ** -0.5
        MCH = [("q", 0, 64, 0), ("q", 1, 64, 64),
               ("q", 2, 64, 128), ("q", 3, 64, 192),
               ("k", 0, 64, 256), ("v", 0, 64, 320)]
        FW = 384
        TBL_P = 64
    else:
        NH, DH = 2, 128
        scale = DH ** -0.5
        MCH = [("q", 0, 128, 0), ("q", 1, 128, 128),
               ("k", 0, 128, 256), ("k", 1, 128, 384),
               ("v", 0, 128, 512), ("v", 1, 128, 640)]
        FW = 768
        TBL_P = 128
    HROWS = NH * DH
    NKV = 1 if layer == 0 else NH
    QP = 64 if layer == 0 else 128
    NW = Tt // 512
    KT = Tt // 128
    VW = DH + 1

    with ExitStack() as ex:
        q_sb = ex.enter_context(nc.sbuf_tensor(pref + "q_sb", [QP, NH * Tt], BF16))
        k_sb = ex.enter_context(nc.sbuf_tensor(pref + "k_sb", [QP, NKV * Tt], BF16))
        v_sb = ex.enter_context(nc.sbuf_tensor(pref + "v_sb", [128, NKV * KT * VW], BF16))
        lnst = ex.enter_context(nc.sbuf_tensor(pref + "lnst", [128, 2 * CC * 512], BF16))
        cs_sb = ex.enter_context(nc.sbuf_tensor(pref + "cs_sb", [TBL_P, 2 * 2 * 512], BF16))
        masks_sb = ex.enter_context(nc.sbuf_tensor(pref + "masks_sb", [128, 4 * 512], BF16))
        w_bf = ex.enter_context(nc.sbuf_tensor(pref + "w_bf", [128, CC * FW], BF16))
        b_sb = ex.enter_context(nc.sbuf_tensor(pref + "b_sb", [128, 16], F32))
        ropeA = ex.enter_context(nc.sbuf_tensor(pref + "ropeA", [128, 512], F32))
        ropeB = ex.enter_context(nc.sbuf_tensor(pref + "ropeB", [128, 512], F32))
        vstg = ex.enter_context(nc.sbuf_tensor(pref + "vstg", [128, 512], BF16))
        p_sb = ex.enter_context(nc.sbuf_tensor(pref + "p_sb", [128, 4 * 512], BF16))
        otst = ex.enter_context(nc.sbuf_tensor(pref + "otst", [128, 2 * 512], BF16))
        nrow = ex.enter_context(nc.sbuf_tensor(pref + "nrow", [1, 64], F32))
        nsc = ex.enter_context(nc.sbuf_tensor(pref + "nsc", [1, 64], F32))
        nbf = ex.enter_context(nc.sbuf_tensor(pref + "nbf", [1, 8], BF16))
        rbf = ex.enter_context(nc.sbuf_tensor(pref + "rbf", [1, 512], BF16))
        rbc = ex.enter_context(nc.sbuf_tensor(pref + "rbc", [128, 512], F32))
        rrow = rbc
        idn_sb = ex.enter_context(nc.sbuf_tensor(pref + "idn_sb", [128, 128], BF16))
        pm_sb = ex.enter_context(nc.sbuf_tensor(pref + "pm_sb", [128, 128], BF16))
        qraw = ex.enter_context(nc.sbuf_tensor(pref + "qraw", [128, 2 * 512], BF16))
        orw_sb = ex.enter_context(nc.sbuf_tensor(pref + "orw_sb", [1, 128], BF16))
        oc_sb = ex.enter_context(nc.sbuf_tensor(pref + "oc_sb", [128, 1], BF16))
        S.emit("sync", lambda e: e.dma_start(
               out=masks_sb[:, :].rearrange("p (m t) -> p m t", t=512),
               in_=masks_d.ap().rearrange("m p t -> p m t")), inc=("sd", 16))
        S.emit("sync", lambda e: e.dma_start(out=idn_sb[:, :], in_=idn_d[:, :]),
               inc=("sd", 16))
        S.emit("sync", lambda e: e.dma_start(out=orw_sb[:, :], in_=orw_d[:, :]),
               inc=("sd", 16))
        S.emit("sync", lambda e: e.dma_start(out=oc_sb[:, :], in_=oc_d[:, :]),
               inc=("sd", 16))
        S.emit("sync", lambda e: e.dma_start(out=pm_sb[0:DH, 0:DH],
                                             in_=pm_d[:, :]), inc=("sd", 16))
        if layer == 1:
            S.emit("sync", lambda e: e.dma_start(
                out=b_sb[:, 0:12], in_=bqkv_d[:, :]), inc=("sd", 16))
        sd_consts = S.get("sd")
        for c in range(CC):
            S.emit("gpsimd", (lambda e, c=c:
                              e.dma_start(out=w_bf[:, FW * c:FW * (c + 1)],
                                          in_=wall_d[128 * c:128 * (c + 1), :])),
                   inc=("sdw0", 16))
        w_ready = ("sdw0", S.get("sdw0"))
        S.emit("vector", lambda e: e.memset(v_sb[:, :], 1.0), inc=("sv", 1))
        wconv_done_v = S.get("sv")

        with nc.psum_tensor(pref + "ps_qkv", [128, 4 * 512], F32) as ps_qkv, \
             nc.psum_tensor(pref + "ps_tr", [128, 512], BF16) as ps_tr:
            evac_cnt = {}
            win_mm = {}
            vstg_free = {}
            rot_done = {}
            ucount = 0
            for tw in range(NW):
                asem = "sda%d" % (tw % 2)
                warw = ([("st", win_mm.get(tw - 2, 0))] if tw >= 2 else []) \
                    + list(src_ready)
                S.emit("gpsimd", (lambda e, tw=tw: e.dma_start(
                    out=lnst[:, (tw % 2) * CC * 512:(tw % 2 + 1) * CC * 512]
                        .rearrange("p (c t) -> p c t", t=512),
                    in_=lnf_tile(tw))),
                    waits=warw, inc=(asem, 16))
                S.emit("gpsimd", (lambda e, tw=tw: e.dma_start(
                    out=cs_sb[:, (tw % 2) * 1024:(tw % 2) * 1024 + 512],
                    in_=cos_d[:, 512 * tw:512 * (tw + 1)])),
                    waits=warw, inc=(asem, 16))
                S.emit("gpsimd", (lambda e, tw=tw: e.dma_start(
                    out=cs_sb[:, (tw % 2) * 1024 + 512:(tw % 2) * 1024 + 1024],
                    in_=sin_d[:, 512 * tw:512 * (tw + 1)])),
                    waits=warw, inc=(asem, 16))
                dma_done = S.get(asem)

                for mi, (kind, h, M, col) in enumerate(MCH):
                    sp = 2 * (ucount % 2)
                    qpar = ucount % 2
                    ucount += 1
                    ps_m = ps_qkv[0:M, 512 * sp:512 * sp + 512]
                    ps_r = ps_qkv[0:M, 512 * (sp + 1):512 * (sp + 1) + 512]
                    for c in range(CC):
                        w = []
                        if c == 0:
                            w = [(asem, dma_done), ("sv", wconv_done_v),
                                 w_ready,
                                 ("sv", evac_cnt.get(sp, 0)),
                                 ("sd", sd_consts)]
                        S.emit("tensor", (lambda e, coff=col, c=c, ps_m=ps_m,
                                          M=M, tw=tw: e.matmul(
                            ps_m,
                            w_bf[:, FW * c + coff:FW * c + coff + M],
                            lnst[:, (tw % 2) * CC * 512 + 512 * c:
                                 (tw % 2) * CC * 512 + 512 * (c + 1)],
                            start=(c == 0), stop=(c == CC - 1))),
                            waits=w,
                            inc=("st", 1) if c == CC - 1 else None)
                    mm_done = S.get("st")
                    win_mm[tw] = mm_done

                    cosap = cs_sb[0:M, (tw % 2) * 1024:(tw % 2) * 1024 + 512]
                    sinap = cs_sb[0:M, (tw % 2) * 1024 + 512:(tw % 2) * 1024 + 1024]
                    if kind in ("q", "k"):
                        dst = q_sb if kind == "q" else k_sb
                        base = (h * Tt + 512 * tw)
                        qr = qraw[0:M, 512 * qpar:512 * qpar + 512]
                        if layer == 1:
                            S.emit("vector", (lambda e, ps_m=ps_m, qr=qr, mi=mi:
                                   e.tensor_scalar(qr, ps_m, b_sb[:, mi:mi + 1],
                                                   None, ALU.add)),
                                   waits=[("st", mm_done), ("sd", sd_consts),
                                          ("st", rot_done.get(qpar, 0))],
                                   inc=("sv", 1))
                        else:
                            S.emit("vector", (lambda e, ps_m=ps_m, qr=qr:
                                   e.tensor_copy(qr, ps_m)),
                                   waits=[("st", mm_done), ("sd", sd_consts),
                                          ("st", rot_done.get(qpar, 0))],
                                   inc=("sv", 1))
                        qr_done = S.get("sv")
                        S.emit("tensor", (lambda e, ps_r=ps_r, qr=qr, M=M:
                               e.matmul(ps_r, pm_sb[0:M, 0:M], qr,
                                        start=True, stop=True)),
                               waits=[("sv", qr_done), ("sd", sd_consts),
                                      ("sv", evac_cnt.get(sp, 0))],
                               inc=("st", 1))
                        rot_done[qpar] = S.get("st")
                        if layer == 1:
                            S.emit("vector", (lambda e, ps_m=ps_m, cosap=cosap,
                                              mi=mi, M=M: e.scalar_tensor_tensor(
                                   ropeA[0:M, :], ps_m, b_sb[:, mi:mi + 1], cosap,
                                   ALU.add, ALU.mult)),
                                   waits=[(asem, dma_done)])
                        else:
                            S.emit("vector", (lambda e, ps_m=ps_m, cosap=cosap,
                                              M=M: e.tensor_tensor(
                                   ropeA[0:M, :], ps_m, cosap, ALU.mult)),
                                   waits=[(asem, dma_done)])
                        S.emit("vector", (lambda e, ps_r=ps_r, sinap=sinap,
                                          M=M: e.tensor_tensor(
                               ropeB[0:M, :], ps_r, sinap, ALU.mult)),
                               waits=[("st", rot_done[qpar])])
                        S.emit("vector", (lambda e, dst=dst, base=base, M=M:
                               e.tensor_tensor(dst[0:M, base:base + 512],
                                               ropeA[0:M, :], ropeB[0:M, :],
                                               ALU.add)), inc=("sv", 1))
                        evac_cnt[sp] = S.get("sv")
                    else:
                        vw = [("st", mm_done), ("st", vstg_free.get(h, 0)),
                              ("sd", sd_consts)]
                        if layer == 1:
                            S.emit("vector", (lambda e, ps_m=ps_m, M=M, mi=mi:
                                   e.tensor_scalar(vstg[0:M, :], ps_m,
                                                   b_sb[:, mi:mi + 1], None,
                                                   ALU.add)),
                                   waits=vw, inc=("sv", 1))
                        else:
                            S.emit("vector", (lambda e, ps_m=ps_m, M=M:
                                   e.tensor_copy(vstg[0:M, :], ps_m)),
                                   waits=vw, inc=("sv", 1))
                        evac_cnt[sp] = S.get("sv")
                        vdone = S.get("sv")
                        cp_prev = 0
                        for j in range(4):
                            kt = 4 * tw + j
                            S.emit("tensor", (lambda e, j=j, M=M:
                                   e.transpose(ps_tr[0:128, 0:M],
                                               vstg[0:M, 128 * j:128 * (j + 1)],
                                               idn_sb[0:M, 0:M])),
                                   waits=[("sv", vdone), ("sv", cp_prev),
                                          ("sd", sd_consts)],
                                   inc=("st", 1))
                            trd = S.get("st")
                            vb = h * KT * VW
                            if layer == 0:
                                S.emit("vector", (lambda e, kt=kt, vb=vb:
                                       e.tensor_copy(
                                           v_sb[:, vb + VW * kt:vb + VW * kt + 64],
                                           ps_tr[0:128, 0:64])),
                                       waits=[("st", trd)], inc=("sv", 1))
                            else:
                                S.emit("vector", (lambda e, kt=kt, vb=vb:
                                       e.tensor_copy(
                                           v_sb[:, vb + VW * kt:vb + VW * kt + 64],
                                           ps_tr[0:128, 0:64])),
                                       waits=[("st", trd)])
                                S.emit("vector", (lambda e, kt=kt, vb=vb:
                                       e.tensor_copy(
                                           v_sb[:, vb + VW * kt + 65:
                                                vb + VW * kt + 129],
                                           ps_tr[0:128, 64:128])),
                                       inc=("sv", 1))
                            cp_prev = S.get("sv")
                        vstg_free[h] = S.get("st")
            S.flush(blk, sems)

        qkv_done_v = S.get("sv")
        with nc.psum_tensor(pref + "ps_n", [1, 512], F32) as ps_n, \
             nc.psum_tensor(pref + "ps_b1", [128, 512], F32) as ps_b1:
            units = [("q", h) for h in range(NH)] + [("k", h) for h in range(NKV)]
            for ui, (kind, h) in enumerate(units):
                src_sb = q_sb if kind == "q" else k_sb
                for tc in range(NW):
                    base = h * Tt + 512 * tc
                    S.emit("vector", (lambda e, src_sb=src_sb, base=base:
                           e.tensor_tensor(vstg[0:QP, :],
                                           src_sb[0:QP, base:base + 512],
                                           src_sb[0:QP, base:base + 512],
                                           ALU.mult)),
                           waits=[("sv", qkv_done_v)], inc=("sv", 1))
                    sq_done = S.get("sv")
                    S.emit("tensor", (lambda e: e.matmul(
                           ps_n[0:1, :], oc_sb[0:QP, 0:1], vstg[0:QP, :],
                           start=True, stop=True)),
                           waits=[("sv", sq_done), ("sd", sd_consts)],
                           inc=("st", 1))
                    nm_done = S.get("st")
                    S.emit("vector", (lambda e, tc=tc: e.tensor_reduce(
                           nrow[0:1, tc:tc + 1], ps_n[0:1, :], axis=AX.X,
                           op=ALU.max)),
                           waits=[("st", nm_done)], inc=("sv", 1))
                S.emit("vector", (lambda e, ui=ui: e.tensor_reduce(
                       nsc[0:1, ui:ui + 1], nrow[0:1, 0:NW], axis=AX.X,
                       op=ALU.max)), inc=("sv", 1))
            bcol0 = 12
            for h in range(NH):
                kvi = NH + (0 if layer == 0 else h)
                S.emit("vector", (lambda e, h=h, kvi=kvi: e.tensor_tensor(
                       nsc[0:1, 8 + h:9 + h], nsc[0:1, h:h + 1],
                       nsc[0:1, kvi:kvi + 1], ALU.mult)), inc=("sv", 1))
                sv_m = S.get("sv")
                S.emit("scalar", (lambda e, h=h: e.activation(
                       nsc[0:1, 16 + h:17 + h], nsc[0:1, 8 + h:9 + h], AF.Sqrt)),
                       waits=[("sv", sv_m)], inc=("sc", 1))
                sc_s = S.get("sc")
                S.emit("vector", (lambda e, h=h, scale=scale: e.tensor_scalar(
                       nsc[0:1, 24 + h:25 + h], nsc[0:1, 16 + h:17 + h],
                       -scale, None, ALU.mult)),
                       waits=[("sc", sc_s)])
                S.emit("vector", (lambda e, h=h: e.tensor_copy(
                       nbf[0:1, h:h + 1], nsc[0:1, 24 + h:25 + h])), inc=("sv", 1))
                sv_nb = S.get("sv")
                S.emit("tensor", (lambda e, h=h: e.matmul(
                       ps_b1[:, 0:1], orw_sb[0:1, :], nbf[0:1, h:h + 1],
                       start=True, stop=True)),
                       waits=[("sv", sv_nb), ("sd", sd_consts)], inc=("st", 1))
                st_b = S.get("st")
                S.emit("vector", (lambda e, h=h, bcol0=bcol0: e.tensor_copy(
                       b_sb[:, bcol0 + h:bcol0 + h + 1], ps_b1[:, 0:1])),
                       waits=[("st", st_b)], inc=("sv", 1))
            S.flush(blk, sems)
        bias_done_v = S.get("sv")

        with nc.psum_tensor(pref + "ps_s", [128, 4 * 512], F32) as ps_s, \
             nc.psum_tensor(pref + "ps_pva", [65, 512], F32) as ps_pva, \
             nc.psum_tensor(pref + "ps_pvb", [64, 512], F32) as ps_pvb, \
             nc.psum_tensor(pref + "ps_bc2", [128, 512], F32) as ps_bc2:
            bcol0 = 12
            pvdone_prev = 0
            otst_free = {}
            for h in range(NH):
                kh = 0 if layer == 0 else h
                for qc in range(NW):
                    ktmax = 4 * qc + 4
                    scnt = {}
                    exp_cnt = {}
                    for kt in range(ktmax):
                        w = []
                        if kt < 4:
                            w.append(("sv", pvdone_prev))
                            w.append(("sv", bias_done_v))
                        if kt >= 4:
                            w.append(("sc", exp_cnt.get(kt - 4 + 1 - ((kt - 4) % 2), 0)))
                        S.emit("tensor", (lambda e, kt=kt, h=h, qc=qc, kh=kh:
                               e.matmul(
                                   ps_s[:, (kt % 4) * 512:(kt % 4) * 512 + 512],
                                   k_sb[0:QP, kh * Tt + 128 * kt:
                                        kh * Tt + 128 * (kt + 1)],
                                   q_sb[0:QP, h * Tt + 512 * qc:
                                        h * Tt + 512 * (qc + 1)],
                                   start=True, stop=True)),
                               waits=w, inc=("st", 1))
                        scnt[("S", kt)] = S.get("st")
                        pair_hi = (kt % 2 == 1) or (kt == ktmax - 1)
                        if pair_hi:
                            k0 = kt - (kt % 2)
                            span = (kt % 2) + 1
                            we = [("st", scnt[("S", kt)])]
                            if k0 >= 4:
                                we.append(("st", scnt.get(("PV", k0 - 4 + span - 1), 0)))
                            if k0 >= 2:
                                we.append(("st", scnt.get(("PV", k0 - 2 + span - 1),
                                                          scnt.get(("PV", k0 - 2), 0))))
                            q4 = (k0 % 4)
                            S.emit("scalar", (lambda e, q4=q4, span=span, h=h:
                                   e.activation(
                                   p_sb[:, q4 * 512:q4 * 512 + 512 * span],
                                   ps_s[:, q4 * 512:q4 * 512 + 512 * span],
                                   AF.Exp, bias=b_sb[:, bcol0 + h:bcol0 + h + 1],
                                   scale=float(scale))),
                                   waits=we, inc=("sc", 1))
                            for kk in range(k0, k0 + span):
                                exp_cnt[kk] = S.get("sc")
                            pready0 = [("sc", S.get("sc"))]
                            for kk in range(k0, k0 + span):
                                if kk >= 4 * qc:
                                    mi = kk - 4 * qc
                                    S.emit("vector", (lambda e, kk=kk, mi=mi:
                                           e.tensor_tensor(
                                           p_sb[:, (kk % 4) * 512:(kk % 4) * 512 + 512],
                                           p_sb[:, (kk % 4) * 512:(kk % 4) * 512 + 512],
                                           masks_sb[:, 512 * mi:512 * (mi + 1)],
                                           ALU.mult)),
                                           waits=[("sc", exp_cnt[kk]),
                                                  ("sd", sd_consts)], inc=("sv", 1))
                                    pready0 = [("sv", S.get("sv"))]
                            vb = kh * KT * VW
                            for j, kk in enumerate(range(k0, k0 + span)):
                                if layer == 0:
                                    vaps = [(ps_pva[0:65, :],
                                             (vb + VW * kk, vb + VW * kk + 65))]
                                else:
                                    vaps = [(ps_pva[0:65, :],
                                             (vb + VW * kk, vb + VW * kk + 65)),
                                            (ps_pvb[0:64, :],
                                             (vb + VW * kk + 65, vb + VW * kk + 129))]
                                for i, (o, (v0, v1)) in enumerate(vaps):
                                    S.emit("tensor", (lambda e, o=o, v0=v0, v1=v1,
                                                      kk=kk, ktmax=ktmax: e.matmul(
                                           o, v_sb[:, v0:v1],
                                           p_sb[:, (kk % 4) * 512:(kk % 4) * 512 + 512],
                                           start=(kk == 0), stop=(kk == ktmax - 1))),
                                           waits=pready0 if (i == 0 and j == 0) else [],
                                           inc=("st", 1))
                                scnt[("PV", kk)] = S.get("st")
                    pv_all = S.get("st")
                    osem = "sdo%d" % (qc % 2)
                    S.emit("scalar", (lambda e: e.activation(
                           rrow[0:1, :], ps_pva[64:65, :], AF.Ln)),
                           waits=[("st", pv_all),
                                  (osem, otst_free.get((h, qc), 0))])
                    S.emit("scalar", (lambda e: e.activation(
                           rbf[0:1, :], rrow[0:1, :], AF.Exp, scale=-1.0)),
                           inc=("sc", 1))
                    sv_r = S.get("sc")
                    S.emit("tensor", (lambda e: e.matmul(
                           ps_bc2[:, :], orw_sb[0:1, :], rbf[0:1, :],
                           start=True, stop=True)),
                           waits=[("sc", sv_r), ("sd", sd_consts)],
                           inc=("st", 1))
                    st_bc = S.get("st")
                    ob = (qc % 2) * 512
                    S.emit("vector", (lambda e: e.tensor_copy(
                           rbc[:, :], ps_bc2[:, :])), waits=[("st", st_bc)])
                    S.emit("vector", (lambda e, ob=ob: e.tensor_tensor(
                           otst[0:64, ob:ob + 512], ps_pva[0:64, :],
                           rbc[0:64, :], ALU.mult)), inc=("sv", 1))
                    if layer == 1:
                        S.emit("vector", (lambda e, ob=ob: e.tensor_tensor(
                               otst[64:128, ob:ob + 512], ps_pvb[0:64, :],
                               rbc[0:64, :], ALU.mult)), inc=("sv", 1))
                    pvdone_prev = S.get("sv")
                    nd = S.get("sv")
                    S.emit("gpsimd", (lambda e, h=h, qc=qc, ob=ob: e.dma_start(
                           out=ot_dst(h, qc),
                           in_=otst[0:DH, ob:ob + 512])),
                           waits=[("sv", nd)], inc=(osem, 16))
                    nqc = (h, qc + 2) if qc + 2 < NW else (h + 1, qc + 2 - NW)
                    otst_free[nqc] = S.get(osem)
            S.emit("gpsimd", lambda e: e.nop(),
                   waits=[("sdo0", S.get("sdo0")), ("sdo1", S.get("sdo1"))])
            S.flush(blk, sems)



def build_A():
    nc = bass.Bass("TRN2", target_bir_lowering=False)
    xin = nc.dram_tensor("xin", [D, TLOC], F32, kind="ExternalInput")
    ln1 = nc.dram_tensor("ln1", [D, TLOC], BF16, kind="ExternalOutput")
    oc_d = nc.dram_tensor("ones_col", [128, 1], BF16, kind="ExternalInput")
    orw_d = nc.dram_tensor("ones_row", [1, 128], BF16, kind="ExternalInput")

    with ExitStack() as ex:
        x_sb = ex.enter_context(nc.sbuf_tensor("x_sb", [128, CC * 1024], F32))
        o_sb = ex.enter_context(nc.sbuf_tensor("o_sb", [128, CC * 1024], BF16))
        lb = ln_bufs(nc, ex)
        oc_sb = ex.enter_context(nc.sbuf_tensor("oc_sb", [128, 1], BF16))
        orw_sb = ex.enter_context(nc.sbuf_tensor("orw_sb", [1, 128], BF16))
        ps_stats = ex.enter_context(nc.psum_tensor("ps_stats", [1, 2048], F32))
        ps_bc = ex.enter_context(nc.psum_tensor("ps_bc", [128, 2048], F32))
        sems = {n: ex.enter_context(nc.semaphore(n)) for n in
                ("sd", "sdo0", "st", "sv", "sc")}
        blk = ex.enter_context(nc.Block())
        S = Sched()
        S.emit("sync", lambda e: e.dma_start(
               out=x_sb[:, :].rearrange("p (c t) -> p c t", t=TLOC),
               in_=xin.ap().rearrange("(c p) t -> p c t", p=128)), inc=("sd", 16))
        S.emit("sync", lambda e: e.dma_start(out=oc_sb[:, :], in_=oc_d[:, :]),
               inc=("sd", 16))
        S.emit("sync", lambda e: e.dma_start(out=orw_sb[:, :], in_=orw_d[:, :]),
               inc=("sd", 16))
        sd_c = S.get("sd")
        vfin = ln_emit(S, x_sb, o_sb, lb["xsq"], lb["t1"], lb["srow"],
                       lb["rowbf"], lb["abc"], lb["bbc"], ps_stats, ps_bc,
                       oc_sb, orw_sb, sd_c, wx=[("sd", sd_c)])
        S.emit("sync", lambda e: e.dma_start(
               out=ln1.ap().rearrange("(c p) t -> p c t", p=128),
               in_=o_sb[:, :].rearrange("p (c t) -> p c t", t=TLOC)),
               waits=[("sv", vfin)], inc=("sdo0", 16))
        S.emit("sync", lambda e: e.nop(), waits=[("sdo0", 16)])
        S.flush(blk, {**sems, "swa": None, "swb": None})
    return nc



def build_proj(layer):
    nc = bass.Bass("TRN2", target_bir_lowering=False)
    xin = nc.dram_tensor("xin", [D, TLOC], F32, kind="ExternalInput")
    ofs = nc.dram_tensor("ofs", [D, TLOC], BF16, kind="ExternalInput")
    wo_d = nc.dram_tensor("wo", [D, D], F32, kind="ExternalInput")
    xo = nc.dram_tensor("xo", [D, TLOC], F32, kind="ExternalOutput")
    lno = nc.dram_tensor("lno", [D, TLOC], BF16, kind="ExternalOutput")
    oc_d = nc.dram_tensor("ones_col", [128, 1], BF16, kind="ExternalInput")
    orw_d = nc.dram_tensor("ones_row", [1, 128], BF16, kind="ExternalInput")
    bo_d = nc.dram_tensor("bo", [128, 16], F32, kind="ExternalInput") \
        if layer == 1 else None
    with ExitStack() as ex:
        sems = {n: ex.enter_context(nc.semaphore(n)) for n in
                ("sd", "sdx", "swa", "swb", "sdw0", "sdw1", "sdo0", "st", "sv",
                 "sc", "sp")}
        blk = ex.enter_context(nc.Block())
        S = Sched()
        o_load = (lambda dest: [lambda e: e.dma_start(
            out=dest[:, :].rearrange("p (c t) -> p c t", t=TLOC),
            in_=ofs.ap().rearrange("(c p) t -> p c t", p=128))])
        emit_proj(nc, S, blk, sems, layer, "p", xin, o_load, wo_d, xo, lno,
                  oc_d, orw_d, bo_d)
    return nc


def emit_proj(nc, S, blk, sems, layer, pref, xin, o_load, wo_d, xo, lno,
              oc_d, orw_d, bo_d, o_eng="sync", o_sem="sdx", o_wait=(), ws=None):
    wo3 = wo_d.ap().rearrange("(c p) f -> p c f", p=128)

    with ExitStack() as ex:
        x_sb = ex.enter_context(nc.sbuf_tensor(pref + "x_sb", [128, CC * 1024], F32))
        o_sb = ex.enter_context(nc.sbuf_tensor(pref + "o_sb", [128, CC * 1024], BF16))
        ln_sb = ex.enter_context(nc.sbuf_tensor(pref + "ln_sb", [128, CC * 1024], BF16))
        oc_sb = ex.enter_context(nc.sbuf_tensor(pref + "oc_sb", [128, 1], BF16))
        orw_sb = ex.enter_context(nc.sbuf_tensor(pref + "orw_sb", [1, 128], BF16))
        bo_sb = ex.enter_context(nc.sbuf_tensor(pref + "bo_sb", [128, 16], F32))
        ex1 = ex.enter_context(ExitStack())
        if ws is None:
            ws = WStream(S, nc, ex1, pref + "wo", CC, 128)

        S.emit("sync", lambda e: e.dma_start(
               out=x_sb[:, :].rearrange("p (c t) -> p c t", t=TLOC),
               in_=xin.ap().rearrange("(c p) t -> p c t", p=128)), inc=("sdx", 16))
        for fdma in o_load(o_sb):
            S.emit(o_eng, fdma, waits=list(o_wait), inc=(o_sem, 16))
        S.emit("sync", lambda e: e.dma_start(out=oc_sb[:, :], in_=oc_d[:, :]),
               inc=("sd", 16))
        S.emit("sync", lambda e: e.dma_start(out=orw_sb[:, :], in_=orw_d[:, :]),
               inc=("sd", 16))
        if layer == 1:
            S.emit("sync", lambda e: e.dma_start(out=bo_sb[:, :], in_=bo_d[:, :]),
                   inc=("sd", 16))
        sd_c = S.get("sd")
        sdx_c = S.get("sdx")
        o_c = S.get(o_sem)

        with nc.psum_tensor(pref + "ps", [128, 2 * 512], F32) as ps:
            pe_war = {0: 0, 1: 0}
            hcur = ws.load(wo3[:, :, 0:128], 128)
            for f in range(CC):
                hnext = ws.load(wo3[:, :, 128 * (f + 1):128 * (f + 2)], 128) \
                    if f + 1 < CC else None
                for t in range(2):
                    for c in range(CC):
                        w = []
                        if c == 0:
                            w = [ws.wready(hcur), ("sdx", sdx_c),
                                 (o_sem, o_c), ("sv", pe_war[t])]
                        S.emit("tensor", (lambda e, f=f, t=t, c=c, hcur=hcur:
                               e.matmul(ps[:, (t % 2) * 512:(t % 2) * 512 + 512],
                                        ws.tile(hcur, c, 0, 128),
                                        o_sb[:, 1024 * c + 512 * t:
                                             1024 * c + 512 * t + 512],
                                        start=(c == 0), stop=(c == CC - 1))),
                               waits=w, inc=("st", 1) if c == CC - 1 else None)
                    mm_d = S.get("st")
                    if layer == 1:
                        S.emit("vector", (lambda e, f=f, t=t: e.scalar_tensor_tensor(
                               x_sb[:, 1024 * f + 512 * t:1024 * f + 512 * t + 512],
                               ps[:, (t % 2) * 512:(t % 2) * 512 + 512],
                               bo_sb[:, f:f + 1],
                               x_sb[:, 1024 * f + 512 * t:1024 * f + 512 * t + 512],
                               ALU.add, ALU.add)),
                               waits=[("st", mm_d), ("sd", sd_c)], inc=("sv", 1))
                    else:
                        S.emit("vector", (lambda e, f=f, t=t: e.tensor_tensor(
                               x_sb[:, 1024 * f + 512 * t:1024 * f + 512 * t + 512],
                               ps[:, (t % 2) * 512:(t % 2) * 512 + 512],
                               x_sb[:, 1024 * f + 512 * t:1024 * f + 512 * t + 512],
                               ALU.add)),
                               waits=[("st", mm_d), ("sdx", sdx_c)], inc=("sv", 1))
                    pe_war[t] = S.get("sv")
                ws.done(hcur)
                hcur = hnext
            x_done = S.get("sv")
            S.flush(blk, sems)
        ex1.close()

        with nc.psum_tensor(pref + "ps_stats", [1, 2048], F32) as ps_stats, \
             nc.psum_tensor(pref + "ps_bc", [128, 2048], F32) as ps_bc, \
             ExitStack() as ex2:
            lb = ln_bufs(nc, ex2, pref)
            vfin = ln_emit(S, x_sb, ln_sb, lb["xsq"], lb["t1"], lb["srow"],
                           lb["rowbf"], lb["abc"], lb["bbc"], ps_stats, ps_bc,
                           oc_sb, orw_sb, sd_c, wx=[("sv", x_done)])
            S.emit("sync", lambda e: e.dma_start(
                   out=xo.ap().rearrange("(c p) t -> p c t", p=128),
                   in_=x_sb[:, :].rearrange("p (c t) -> p c t", t=TLOC)),
                   waits=[("sv", x_done)], inc=("sdo0", 16))
            S.emit("sync", lambda e: e.dma_start(
                   out=lno.ap().rearrange("(c p) t -> p c t", p=128),
                   in_=ln_sb[:, :].rearrange("p (c t) -> p c t", t=TLOC)),
                   waits=[("sv", vfin)], inc=("sdo0", 16))
            S.emit("sync", lambda e: e.nop(), waits=[("sdo0", S.get("sdo0"))])
            S.flush(blk, sems)



def _gu_phase(S, nc, ws, ps, nmp, hoff, ln_sb, h_sb, sgw4, silu_first,
              we_ap=None, pair_cols=256, hw_st=0, ln_wait=(), war_in=None):
    evac_war = dict(war_in) if war_in else {}
    prev_h = 0
    hcur = _ws_load4(ws, sgw4(0), pair_cols)
    for mp in range(nmp):
        hnext = _ws_load4(ws, sgw4(mp + 1), pair_cols) if mp + 1 < nmp else None
        for t in range(2):
            idx = mp * 2 + t
            sp = 2 * (idx % 2)
            for half in range(2):
                for c in range(CC):
                    w = []
                    if c == 0:
                        w = [ws.wready(hcur),
                             ("sv", evac_war.get(sp + half, 0))] + list(ln_wait)
                    S.emit("tensor", (lambda e, c=c, half=half, sp=sp, t=t,
                                      hcur=hcur: e.matmul(
                           ps[:, (sp + half) * 512:(sp + half) * 512 + 512],
                           ws.tile(hcur, c, 128 * half, 128),
                           ln_sb[:, 1024 * c + 512 * t:1024 * c + 512 * t + 512],
                           start=(c == 0), stop=(c == CC - 1))),
                           waits=w, inc=("st", 1) if c == CC - 1 else None)
            mm_d = S.get("st")
            gh = 0 if silu_first else 1
            uh = 1 - gh
            S.emit("scalar", (lambda e, sp=sp, gh=gh: e.activation(
                   vstg_g[0:128, :], ps[:, (sp + gh) * 512:(sp + gh) * 512 + 512],
                   AF.Silu)), waits=[("sv", prev_h), ("st", mm_d)], inc=("sc", 1))
            sc_d = S.get("sc")
            hw = [("sc", sc_d), ("st", hw_st)]
            if we_ap is None:
                S.emit("vector", (lambda e, sp=sp, uh=uh, mp=mp, t=t,
                                  hoff=hoff: e.tensor_tensor(
                       h_sb[:, hoff + 1024 * mp + 512 * t:
                            hoff + 1024 * mp + 512 * t + 512],
                       vstg_g[0:128, :],
                       ps[:, (sp + uh) * 512:(sp + uh) * 512 + 512], ALU.mult)),
                       waits=hw, inc=("sv", 1))
            else:
                S.emit("vector", (lambda e, sp=sp, uh=uh: e.tensor_tensor(
                       vstg_h[0:128, :], vstg_g[0:128, :],
                       ps[:, (sp + uh) * 512:(sp + uh) * 512 + 512], ALU.mult)),
                       waits=hw)
                S.emit("vector", (lambda e, mp=mp, t=t, hoff=hoff, we_ap=we_ap:
                       e.tensor_tensor(
                       h_sb[:, hoff + 1024 * mp + 512 * t:
                            hoff + 1024 * mp + 512 * t + 512],
                       vstg_h[0:128, :], we_ap[:, 512 * t:512 * t + 512],
                       ALU.mult)), inc=("sv", 1))
            evac_war[sp + 0] = S.get("sv")
            evac_war[sp + 1] = S.get("sv")
            prev_h = S.get("sv")
        ws.done(hcur)
        hcur = hnext
    return S.get("sv"), evac_war


def _down_phase(S, nc, ws, ps, x_sb, h_sb, hoff, nch, dn3, h_ready, x_wait=(),
                war_in=None):
    evac_war = dict(war_in) if war_in else {}
    hcur = ws.load(dn3(0), 128, rows_chunks=nch)
    for f2 in range(CC):
        hnext = ws.load(dn3(f2 + 1), 128, rows_chunks=nch) if f2 + 1 < CC else None
        for t in range(2):
            sp = (f2 * 2 + t) % 4
            for c in range(nch):
                w = []
                if c == 0:
                    w = [ws.wready(hcur), ("sv", h_ready),
                         ("sv", evac_war.get(sp, 0))]
                S.emit("tensor", (lambda e, c=c, sp=sp, t=t, hcur=hcur,
                                  hoff=hoff: e.matmul(
                       ps[:, sp * 512:sp * 512 + 512],
                       ws.tile(hcur, c, 0, 128),
                       h_sb[:, hoff + 1024 * c + 512 * t:
                            hoff + 1024 * c + 512 * t + 512],
                       start=(c == 0), stop=(c == nch - 1))),
                       waits=w, inc=("st", 1) if c == nch - 1 else None)
            mm_d = S.get("st")
            S.emit("vector", (lambda e, f2=f2, t=t, sp=sp: e.tensor_tensor(
                   x_sb[:, 1024 * f2 + 512 * t:1024 * f2 + 512 * t + 512],
                   ps[:, sp * 512:sp * 512 + 512],
                   x_sb[:, 1024 * f2 + 512 * t:1024 * f2 + 512 * t + 512],
                   ALU.add)), waits=[("st", mm_d)] + list(x_wait),
                   inc=("sv", 1))
            evac_war[sp] = S.get("sv")
        ws.done(hcur)
        hcur = hnext
    return S.get("st"), evac_war


def build_moe():
    nc = bass.Bass("TRN2", target_bir_lowering=False)
    xin = nc.dram_tensor("xin", [D, TLOC], F32, kind="ExternalInput")
    lnin = nc.dram_tensor("lnin", [D, TLOC], BF16, kind="ExternalInput")
    gw_d = nc.dram_tensor("gw", [D, 8], F32, kind="ExternalInput")
    gb_d = nc.dram_tensor("gbbc", [128, 8], F32, kind="ExternalInput")
    sgw_d = nc.dram_tensor("sgw", [D, 5632], F32, kind="ExternalInput")
    sdw_d = nc.dram_tensor("sdw", [2816, D], F32, kind="ExternalInput")
    upw_d = nc.dram_tensor("upw", [8 * D, 2816], F32, kind="ExternalInput")
    dnw_d = nc.dram_tensor("dnw", [8 * 1408, D], F32, kind="ExternalInput")
    xo = nc.dram_tensor("xo", [D, TLOC], F32, kind="ExternalOutput")
    lno = nc.dram_tensor("lno", [D, TLOC], BF16, kind="ExternalOutput")
    oc_d = nc.dram_tensor("ones_col", [128, 1], BF16, kind="ExternalInput")
    orw_d = nc.dram_tensor("ones_row", [1, 128], BF16, kind="ExternalInput")
    idf_d = nc.dram_tensor("idn_f32", [128, 128], F32, kind="ExternalInput")
    with ExitStack() as ex:
        sems = {n: ex.enter_context(nc.semaphore(n)) for n in
                ("sd", "sdx", "swa", "swb", "sdw0", "sdw1", "sdo0", "st", "sv",
                 "sc", "sp")}
        blk = ex.enter_context(nc.Block())
        S = Sched()
        emit_moe(nc, S, blk, sems, "m", xin, lnin, gw_d, gb_d, sgw_d, sdw_d,
                 upw_d, dnw_d, xo, lno, oc_d, orw_d, idf_d)
    return nc


def emit_moe(nc, S, blk, sems, pref, xin, lnin, gw_d, gb_d, sgw_d, sdw_d,
             upw_d, dnw_d, xo, lno, oc_d, orw_d, idf_d, ws=None):
    global vstg_g, vstg_h
    with ExitStack() as ex:
        x_sb = ex.enter_context(nc.sbuf_tensor(pref + "x_sb", [128, CC * 1024], F32))
        ln_sb = ex.enter_context(nc.sbuf_tensor(pref + "ln_sb", [128, CC * 1024], BF16))
        vstg_g = ex.enter_context(nc.sbuf_tensor(pref + "vstg_g", [128, 512], BF16))
        vstg_h = ex.enter_context(nc.sbuf_tensor(pref + "vstg_h", [128, 512], BF16))
        webc = ex.enter_context(nc.sbuf_tensor(pref + "webc", [128, 1024], F32))
        oc_sb = ex.enter_context(nc.sbuf_tensor(pref + "oc_sb", [128, 1], BF16))
        orw_sb = ex.enter_context(nc.sbuf_tensor(pref + "orw_sb", [1, 128], BF16))
        idf_sb = ex.enter_context(nc.sbuf_tensor(pref + "idf_sb", [128, 128], F32))
        gb_sb = ex.enter_context(nc.sbuf_tensor(pref + "gb_sb", [128, 8], F32))
        gwbf = ex.enter_context(nc.sbuf_tensor(pref + "gwbf", [128, CC * 8], BF16))
        sfm = ex.enter_context(nc.sbuf_tensor(pref + "sfm", [8, 1024], F32))
        stm = ex.enter_context(nc.sbuf_tensor(pref + "stm", [128, 64], F32))
        ztm = ex.enter_context(nc.sbuf_tensor(pref + "ztm", [128, 64], F32))
        mtm = ex.enter_context(nc.sbuf_tensor(pref + "mtm", [128, 64], F32))
        m2tm = ex.enter_context(nc.sbuf_tensor(pref + "m2tm", [128, 64], F32))
        wetm = ex.enter_context(nc.sbuf_tensor(pref + "wetm", [128, 64], F32))
        rcol = ex.enter_context(nc.sbuf_tensor(pref + "rcol", [128, 8], F32))
        exH = ex.enter_context(ExitStack())
        h_sb = exH.enter_context(nc.sbuf_tensor(pref + "h_sb", [128, 11 * 1024], BF16))
        werow = exH.enter_context(nc.sbuf_tensor(pref + "werow", [1, 8 * 1024], BF16))
        S.emit("sync", lambda e: e.dma_start(
               out=x_sb[:, :].rearrange("p (c t) -> p c t", t=TLOC),
               in_=xin.ap().rearrange("(c p) t -> p c t", p=128)), inc=("sdx", 16))
        S.emit("sync", lambda e: e.dma_start(
               out=ln_sb[:, :].rearrange("p (c t) -> p c t", t=TLOC),
               in_=lnin.ap().rearrange("(c p) t -> p c t", p=128)), inc=("sdx", 16))
        S.emit("sync", lambda e: e.dma_start(out=oc_sb[:, :], in_=oc_d[:, :]),
               inc=("sd", 16))
        S.emit("sync", lambda e: e.dma_start(out=orw_sb[:, :], in_=orw_d[:, :]),
               inc=("sd", 16))
        S.emit("sync", lambda e: e.dma_start(out=idf_sb[:, :], in_=idf_d[:, :]),
               inc=("sd", 16))
        S.emit("sync", lambda e: e.dma_start(out=gb_sb[:, :], in_=gb_d[:, :]),
               inc=("sd", 16))
        S.emit("gpsimd", lambda e: e.dma_start(
               out=gwbf[:, :].rearrange("p (c f) -> p c f", f=8),
               in_=gw_d.ap().rearrange("(c p) f -> p c f", p=128)),
               inc=("sdw0", 16))
        gw_ready = ("sdw0", S.get("sdw0"))
        sd_c = S.get("sd")
        sdx_c = S.get("sdx")

        with nc.psum_tensor(pref + "ps_g", [8, 1024], F32) as ps_g, \
             nc.psum_tensor(pref + "ps_tr2", [128, 8], F32) as ps_tr2, \
             nc.psum_tensor(pref + "ps_wt", [1, 128], F32) as ps_wt:
            for t in range(2):
                for c in range(CC):
                    S.emit("tensor", (lambda e, t=t, c=c: e.matmul(
                           ps_g[0:8, 512 * t:512 * t + 512],
                           gwbf[:, 8 * c:8 * (c + 1)],
                           ln_sb[:, 1024 * c + 512 * t:1024 * c + 512 * t + 512],
                           start=(c == 0), stop=(c == CC - 1))),
                           waits=[gw_ready, ("sdx", sdx_c)] if c == 0 else [],
                           inc=("st", 1) if c == CC - 1 else None)
            gmm = S.get("st")
            S.emit("scalar", (lambda e: e.activation(
                   sfm[0:8, :], ps_g[0:8, :], AF.Sigmoid)),
                   waits=[("st", gmm)], inc=("sc", 1))
            sig_d = S.get("sc")
            for ti in range(8):
                S.emit("tensor", (lambda e, ti=ti: e.transpose(
                       ps_tr2[:, 0:8], sfm[0:8, 128 * ti:128 * (ti + 1)],
                       idf_sb[0:8, 0:8])),
                       waits=[("sc", sig_d), ("sd", sd_c),
                              ("sv", S.get("sv"))], inc=("st", 1))
                trd = S.get("st")
                S.emit("vector", (lambda e, ti=ti: e.tensor_copy(
                       stm[:, 8 * ti:8 * ti + 8], ps_tr2[:, 0:8])),
                       waits=[("st", trd)], inc=("sv", 1))
            sv_s = S.get("sv")
            for ti in range(8):
                o = 8 * ti
                S.emit("vector", (lambda e, o=o: e.tensor_tensor(
                       ztm[:, o:o + 8], stm[:, o:o + 8], gb_sb[:, 0:8], ALU.add)),
                       waits=[("sv", sv_s), ("sd", sd_c)])
                S.emit("vector", (lambda e, o=o: e.tensor_reduce(
                       rcol[:, 0:1], ztm[:, o:o + 8], axis=AX.X, op=ALU.max)))
                S.emit("vector", (lambda e, o=o: e.tensor_scalar(
                       mtm[:, o:o + 8], ztm[:, o:o + 8], rcol[:, 0:1], None,
                       ALU.is_equal)))
                S.emit("vector", (lambda e, o=o: e.tensor_tensor(
                       m2tm[:, o:o + 8], mtm[:, o:o + 8], stm[:, o:o + 8],
                       ALU.mult)))
                S.emit("vector", (lambda e, o=o: e.tensor_reduce(
                       rcol[:, 1:2], m2tm[:, o:o + 8], axis=AX.X, op=ALU.add)))
                S.emit("vector", (lambda e, o=o: e.scalar_tensor_tensor(
                       ztm[:, o:o + 8], mtm[:, o:o + 8], -1e30, ztm[:, o:o + 8],
                       ALU.mult, ALU.add)))
                S.emit("vector", (lambda e, o=o: e.tensor_reduce(
                       rcol[:, 2:3], ztm[:, o:o + 8], axis=AX.X, op=ALU.max)))
                S.emit("vector", (lambda e, o=o: e.tensor_scalar(
                       m2tm[:, o:o + 8], ztm[:, o:o + 8], rcol[:, 2:3], None,
                       ALU.is_equal)))
                S.emit("vector", (lambda e, o=o: e.tensor_tensor(
                       ztm[:, o:o + 8], m2tm[:, o:o + 8], stm[:, o:o + 8],
                       ALU.mult)))
                S.emit("vector", (lambda e, o=o: e.tensor_reduce(
                       rcol[:, 3:4], ztm[:, o:o + 8], axis=AX.X, op=ALU.add)))
                S.emit("vector", (lambda e: e.tensor_tensor(
                       rcol[:, 4:5], rcol[:, 1:2], rcol[:, 3:4], ALU.add)))
                S.emit("vector", (lambda e: e.reciprocal(
                       rcol[:, 5:6], rcol[:, 4:5])))
                S.emit("vector", (lambda e, o=o: e.tensor_scalar(
                       wetm[:, o:o + 8], mtm[:, o:o + 8], rcol[:, 1:2], None,
                       ALU.mult)))
                S.emit("vector", (lambda e, o=o: e.tensor_scalar(
                       ztm[:, o:o + 8], m2tm[:, o:o + 8], rcol[:, 3:4], None,
                       ALU.mult)))
                S.emit("vector", (lambda e, o=o: e.tensor_tensor(
                       wetm[:, o:o + 8], wetm[:, o:o + 8], ztm[:, o:o + 8],
                       ALU.add)))
                S.emit("vector", (lambda e, o=o: e.tensor_scalar(
                       wetm[:, o:o + 8], wetm[:, o:o + 8], rcol[:, 5:6], None,
                       ALU.mult)), inc=("sv", 1))
            we_d = S.get("sv")
            for ti in range(8):
                for e_ in range(8):
                    S.emit("tensor", (lambda e, ti=ti, e_=e_: e.transpose(
                           ps_wt[0:1, :], wetm[:, 8 * ti + e_:8 * ti + e_ + 1],
                           idf_sb[0:128, 0:128])),
                           waits=[("sv", we_d), ("sv", S.get("sv"))],
                           inc=("st", 1))
                    wtd = S.get("st")
                    S.emit("vector", (lambda e, ti=ti, e_=e_: e.tensor_copy(
                           werow[0:1, 1024 * e_ + 128 * ti:
                                 1024 * e_ + 128 * (ti + 1)], ps_wt[0:1, :])),
                           waits=[("st", wtd)], inc=("sv", 1))
            S.flush(blk, sems)
        werow_d = S.get("sv")

        sgw3 = sgw_d.ap().rearrange("(c p) f -> p c f", p=128)
        sdw3 = sdw_d.ap().rearrange("(c p) f -> p c f", p=128)
        upw3 = upw_d.ap().rearrange("(e c p) f -> e p c f", p=128, e=8)
        dnw3 = dnw_d.ap().rearrange("(e c p) f -> e p c f", p=128, e=8)

        def sgw_pair(j):
            return (sgw3[:, :, 128 * j:128 * (j + 1)],
                    sgw3[:, :, 2816 + 128 * j:2816 + 128 * (j + 1)])

        def upw_pair(e_, j):
            return (upw3[e_, :, :, 128 * j:128 * (j + 1)],
                    upw3[e_, :, :, 1408 + 128 * j:1408 + 128 * (j + 1)])

        with ExitStack() as ex1:
            if ws is None:
                ws = WStream(S, nc, ex1, pref + "moe", CC, 256)
            with nc.psum_tensor(pref + "ps_m", [128, 4 * 512], F32) as ps_m, \
                 nc.psum_tensor(pref + "ps_bc3", [128, 1024], F32) as ps_bc3:
                hw_st = 0
                pwar = None
                for rnd in range(2):
                    h_done, pwar = _gu_phase(S, nc, ws, ps_m, 11, 0, ln_sb, h_sb,
                                       (lambda mp, rnd=rnd:
                                        sgw_pair(11 * rnd + mp)),
                                       silu_first=False, we_ap=None, hw_st=hw_st,
                                       ln_wait=[("sdx", sdx_c)], war_in=pwar)
                    hw_st, pwar = _down_phase(S, nc, ws, ps_m, x_sb, h_sb, 0, 11,
                                        (lambda f2, rnd=rnd: sdw3[:,
                                         11 * rnd:11 * (rnd + 1),
                                         128 * f2:128 * (f2 + 1)]),
                                        h_done, x_wait=[("sdx", sdx_c)],
                                        war_in=pwar)
                for e_ in range(8):
                    for t in range(2):
                        S.emit("tensor", (lambda e, t=t, e_=e_: e.matmul(
                               ps_bc3[:, 512 * t:512 * t + 512], orw_sb[0:1, :],
                               werow[0:1, 1024 * e_ + 512 * t:
                                     1024 * e_ + 512 * t + 512],
                               start=True, stop=True)),
                               waits=[("sv", werow_d), ("sd", sd_c)],
                               inc=("st", 1))
                    bc_d = S.get("st")
                    S.emit("vector", (lambda e: e.tensor_copy(
                           webc[:, :], ps_bc3[:, :])),
                           waits=[("st", bc_d)], inc=("sv", 1))
                    h_done, pwar = _gu_phase(S, nc, ws, ps_m, 11, 0, ln_sb, h_sb,
                                       (lambda mp, e_=e_: upw_pair(e_, mp)),
                                       silu_first=True, we_ap=webc,
                                       hw_st=hw_st, ln_wait=[("sdx", sdx_c)],
                                       war_in=pwar)
                    hw_st, pwar = _down_phase(S, nc, ws, ps_m, x_sb, h_sb, 0, 11,
                                        (lambda f2, e_=e_: dnw3[e_, :, :,
                                         128 * f2:128 * (f2 + 1)]),
                                        h_done, x_wait=[("sdx", sdx_c)],
                                        war_in=pwar)
                x_done = S.get("sv")
                S.flush(blk, sems)
        exH.close()

        with nc.psum_tensor(pref + "ps_stats", [1, 2048], F32) as ps_stats, \
             nc.psum_tensor(pref + "ps_bc", [128, 2048], F32) as ps_bc, \
             ExitStack() as ex2:
            lb = ln_bufs(nc, ex2, pref)
            vfin = ln_emit(S, x_sb, ln_sb, lb["xsq"], lb["t1"], lb["srow"],
                           lb["rowbf"], lb["abc"], lb["bbc"], ps_stats, ps_bc,
                           oc_sb, orw_sb, sd_c, wx=[("sv", x_done)])
            S.emit("sync", lambda e: e.dma_start(
                   out=xo.ap().rearrange("(c p) t -> p c t", p=128),
                   in_=x_sb[:, :].rearrange("p (c t) -> p c t", t=TLOC)),
                   waits=[("sv", x_done)], inc=("sdo0", 16))
            S.emit("sync", lambda e: e.dma_start(
                   out=lno.ap().rearrange("(c p) t -> p c t", p=128),
                   in_=ln_sb[:, :].rearrange("p (c t) -> p c t", t=TLOC)),
                   waits=[("sv", vfin)], inc=("sdo0", 16))
            S.emit("sync", lambda e: e.nop(), waits=[("sdo0", S.get("sdo0"))])
            S.flush(blk, sems)



def build_ffn1():
    nc = bass.Bass("TRN2", target_bir_lowering=False)
    xin = nc.dram_tensor("xin", [D, TLOC], F32, kind="ExternalInput")
    lnin = nc.dram_tensor("lnin", [D, TLOC], BF16, kind="ExternalInput")
    wgu_d = nc.dram_tensor("wgu1", [D, 16384], F32, kind="ExternalInput")
    wdn_d = nc.dram_tensor("wdn1", [8192, D], F32, kind="ExternalInput")
    xo = nc.dram_tensor("xo", [D, TLOC], F32, kind="ExternalOutput")
    with ExitStack() as ex:
        sems = {n: ex.enter_context(nc.semaphore(n)) for n in
                ("sd", "sdx", "swa", "swb", "sdw0", "sdw1", "sdo0", "st", "sv",
                 "sc", "sp")}
        blk = ex.enter_context(nc.Block())
        S = Sched()
        emit_ffn1(nc, S, blk, sems, "g", xin, lnin, wgu_d, wdn_d, xo)
    return nc


def emit_ffn1(nc, S, blk, sems, pref, xin, lnin, wgu_d, wdn_d, xo, ws=None):
    global vstg_g, vstg_h
    with ExitStack() as ex:
        x_sb = ex.enter_context(nc.sbuf_tensor(pref + "x_sb", [128, CC * 1024], F32))
        ln_sb = ex.enter_context(nc.sbuf_tensor(pref + "ln_sb", [128, CC * 1024], BF16))
        h_sb = ex.enter_context(nc.sbuf_tensor(pref + "h_sb", [128, 16 * 1024], BF16))
        vstg_g = ex.enter_context(nc.sbuf_tensor(pref + "vstg_g", [128, 512], BF16))
        vstg_h = ex.enter_context(nc.sbuf_tensor(pref + "vstg_h", [128, 512], BF16))
        S.emit("sync", lambda e: e.dma_start(
               out=x_sb[:, :].rearrange("p (c t) -> p c t", t=TLOC),
               in_=xin.ap().rearrange("(c p) t -> p c t", p=128)), inc=("sdx", 16))
        S.emit("sync", lambda e: e.dma_start(
               out=ln_sb[:, :].rearrange("p (c t) -> p c t", t=TLOC),
               in_=lnin.ap().rearrange("(c p) t -> p c t", p=128)), inc=("sdx", 16))
        sdx_c = S.get("sdx")
        wgu3 = wgu_d.ap().rearrange("(c p) f -> p c f", p=128)
        wdn3 = wdn_d.ap().rearrange("(g c p) f -> g p c f", p=128, g=4)

        def wgu_pair(j):
            return (wgu3[:, :, 128 * j:128 * (j + 1)],
                    wgu3[:, :, 8192 + 128 * j:8192 + 128 * (j + 1)])

        with ExitStack() as ex1:
            if ws is None:
                ws = WStream(S, nc, ex1, pref + "f1", CC, 256)
            with nc.psum_tensor(pref + "ps_m", [128, 4 * 512], F32) as ps_m:
                hw_st = 0
                pwar = None
                for grp in range(4):
                    h_done, pwar = _gu_phase(S, nc, ws, ps_m, 16, 0, ln_sb, h_sb,
                                       (lambda mp, grp=grp:
                                        wgu_pair(16 * grp + mp)),
                                       silu_first=True, we_ap=None, hw_st=hw_st,
                                       ln_wait=[("sdx", sdx_c)], war_in=pwar)
                    hw_st, pwar = _down_phase(S, nc, ws, ps_m, x_sb, h_sb, 0, 16,
                                        (lambda f2, grp=grp: wdn3[grp, :, :,
                                         128 * f2:128 * (f2 + 1)]),
                                        h_done, x_wait=[("sdx", sdx_c)],
                                        war_in=pwar)
                x_done = S.get("sv")
                S.emit("sync", lambda e: e.dma_start(
                       out=xo.ap().rearrange("(c p) t -> p c t", p=128),
                       in_=x_sb[:, :].rearrange("p (c t) -> p c t", t=TLOC)),
                       waits=[("sv", x_done)], inc=("sdo0", 16))
                S.emit("sync", lambda e: e.nop(),
                       waits=[("sdo0", S.get("sdo0"))])
                S.flush(blk, sems)



_cache = {}


def _get(name, builder):
    if name not in _cache:
        _cache[name] = builder()
    return _cache[name]


def _consts(d):
    return {"ones_col": d["ones_col"], "ones_row": d["ones_row"]}


def kernel(**inputs):
    per_core = prep_host(inputs)
    return kernel_fused(per_core, inputs)


def kernel_multilaunch(**inputs):
    per_core = prep_host(inputs)
    cores = list(range(NCORES))

    def run(name, builder, maps):
        ncp = _get(name, builder)
        return run_bass_kernel_spmd(ncp, maps, core_ids=cores)

    resA = run("A", build_A, [{"xin": d["xin"], **_consts(d)} for d in per_core])
    ln1f = np.concatenate([np.asarray(resA.results[r]["ln1"]) for r in cores],
                          axis=1)
    resB0 = run("B0", lambda: build_attn(0),
                [{"lnf": ln1f, "wall": d["wall0"], "cos": d["cos0"],
                  "sin": d["sin0"], "masks": d["masks"], "idn_bf": d["idn_bf"],
                  **_consts(d)} for d in per_core])
    o0 = np.concatenate([np.asarray(resB0.results[r]["ot"]) for r in cores],
                        axis=0)
    resC0 = run("C0", lambda: build_proj(0),
                [{"xin": d["xin"], "ofs": _cc(o0[:, TLOC * r:TLOC * (r + 1)]),
                  "wo": d["wo0"], **_consts(d)}
                 for r, d in zip(cores, per_core)])
    resD = run("D", build_moe,
               [{"xin": np.asarray(resC0.results[r]["xo"]),
                 "lnin": np.asarray(resC0.results[r]["lno"]),
                 "gw": d["gw"], "gbbc": d["gbbc"], "sgw": d["sgw"],
                 "sdw": d["sdw"], "upw": d["upw"].reshape(8 * D, 2816),
                 "dnw": d["dnw"].reshape(8 * 1408, D),
                 "idn_f32": d["idn_f32"], **_consts(d)}
                for r, d in zip(cores, per_core)])
    ln3f = np.concatenate([np.asarray(resD.results[r]["lno"]) for r in cores],
                          axis=1)
    resB1 = run("B1", lambda: build_attn(1),
                [{"lnf": ln3f, "wall": d["wall1"], "bqkv1": d["bqkv1"],
                  "cos": d["cos1"], "sin": d["sin1"], "masks": d["masks"],
                  "idn_bf": d["idn_bf"], **_consts(d)} for d in per_core])
    o1 = np.concatenate([np.asarray(resB1.results[r]["ot"]) for r in cores],
                        axis=0)
    resC1 = run("C1", lambda: build_proj(1),
                [{"xin": np.asarray(resD.results[r]["xo"]),
                  "ofs": _cc(o1[:, TLOC * r:TLOC * (r + 1)]),
                  "wo": d["wo1"], "bo": _cc(np.asarray(
                      inputs["L1_attn_b_out"]).reshape(CC, 128).T.astype(
                          np.float32)), **_consts(d)}
                 for r, d in zip(cores, per_core)])
    resG = run("G", build_ffn1,
               [{"xin": np.asarray(resC1.results[r]["xo"]),
                 "lnin": np.asarray(resC1.results[r]["lno"]),
                 "wgu1": d["wgu1"], "wdn1": d["wdn1"]}
                for r, d in zip(cores, per_core)])

    out = np.zeros((1, T, D), dtype=np.float32)
    for r in cores:
        out[0, TLOC * r:TLOC * (r + 1), :] = \
            np.asarray(resG.results[r]["xo"], dtype=np.float32).T
    return out





_SCOPES = []


def _scope(nc, name):
    if _SCOPES:
        n, sid = _SCOPES.pop()
        nc.leave_named_scope(n, sid, False)
    if name is not None:
        sid, _ = nc.enter_named_scope(name, False)
        _SCOPES.append((name, sid))

def emit_barrier(S, sems):
    for eng in ("vector", "scalar", "gpsimd"):
        S.emit(eng, lambda e: e.drain())
    for eng in Sched.ENGINES:
        S.emit(eng, lambda e: e.nop(), inc=("bar", 1))
    tgt = S.get("bar")
    for eng in Sched.ENGINES:
        S.emit(eng, lambda e: e.nop(), waits=[("bar", tgt)])


def build_fused():
    nc = bass.Bass("TRN2", target_bir_lowering=False)
    I32 = mybir.dt.int32
    xin = nc.dram_tensor("xin", [D, TLOC], F32, kind="ExternalInput")
    xo = nc.dram_tensor("xo", [D, TLOC], F32, kind="ExternalOutput")
    coff_d = nc.dram_tensor("coff", [1, 1], I32, kind="ExternalInput")
    wall0 = nc.dram_tensor("wall0", [D, 384], F32, kind="ExternalInput")
    wall1 = nc.dram_tensor("wall1", [D, 768], F32, kind="ExternalInput")
    pm0_d = nc.dram_tensor("pm0", [64, 64], BF16, kind="ExternalInput")
    pm1_d = nc.dram_tensor("pm1", [128, 128], BF16, kind="ExternalInput")
    bqkv_d = nc.dram_tensor("bqkv1", [128, 12], F32, kind="ExternalInput")
    wo0_d = nc.dram_tensor("wo0", [D, D], F32, kind="ExternalInput")
    wo1_d = nc.dram_tensor("wo1", [D, D], F32, kind="ExternalInput")
    bo_d = nc.dram_tensor("bo", [128, 16], F32, kind="ExternalInput")
    gw_d = nc.dram_tensor("gw", [D, 8], F32, kind="ExternalInput")
    gb_d = nc.dram_tensor("gbbc", [128, 8], F32, kind="ExternalInput")
    sgw_d = nc.dram_tensor("sgw", [D, 5632], F32, kind="ExternalInput")
    sdw_d = nc.dram_tensor("sdw", [2816, D], F32, kind="ExternalInput")
    upw_d = nc.dram_tensor("upw", [8 * D, 2816], F32, kind="ExternalInput")
    dnw_d = nc.dram_tensor("dnw", [8 * 1408, D], F32, kind="ExternalInput")
    wgu_d = nc.dram_tensor("wgu1", [D, 16384], F32, kind="ExternalInput")
    wdn_d = nc.dram_tensor("wdn1", [8192, D], F32, kind="ExternalInput")
    cos0_d = nc.dram_tensor("cos0", [64, T], BF16, kind="ExternalInput")
    sin0_d = nc.dram_tensor("sin0", [64, T], BF16, kind="ExternalInput")
    cos1_d = nc.dram_tensor("cos1", [128, T], BF16, kind="ExternalInput")
    sin1_d = nc.dram_tensor("sin1", [128, T], BF16, kind="ExternalInput")
    masks_d = nc.dram_tensor("masks", [4, 128, 512], BF16, kind="ExternalInput")
    idn_d = nc.dram_tensor("idn_bf", [128, 128], BF16, kind="ExternalInput")
    idf_d = nc.dram_tensor("idn_f32", [128, 128], F32, kind="ExternalInput")
    oc_d = nc.dram_tensor("ones_col", [128, 1], BF16, kind="ExternalInput")
    orw_d = nc.dram_tensor("ones_row", [1, 128], BF16, kind="ExternalInput")
    ln1b = nc.dram_tensor("ln1b", [D, TLOC], BF16)
    ag1 = nc.dram_tensor("ag1", [8 * D, TLOC], BF16, addr_space="Shared")
    ob0 = nc.dram_tensor("ob0", [D, TLOC], BF16)
    oag0 = nc.dram_tensor("oag0", [8 * D, TLOC], BF16, addr_space="Shared")
    x1d = nc.dram_tensor("x1d", [D, TLOC], F32)
    ln2d = nc.dram_tensor("ln2d", [D, TLOC], BF16)
    x2d = nc.dram_tensor("x2d", [D, TLOC], F32)
    ln3b = nc.dram_tensor("ln3b", [D, TLOC], BF16)
    ag3 = nc.dram_tensor("ag3", [8 * D, TLOC], BF16, addr_space="Shared")
    ob1 = nc.dram_tensor("ob1", [D, TLOC], BF16)
    oag1 = nc.dram_tensor("oag1", [8 * D, TLOC], BF16, addr_space="Shared")
    x3d = nc.dram_tensor("x3d", [D, TLOC], F32)
    ln4d = nc.dram_tensor("ln4d", [D, TLOC], BF16)

    RG = [list(range(NCORES))]

    def agtile(ag):
        agv = ag.ap().rearrange("(b c p) t -> p b c t", p=128, b=NCORES)
        def f(tw):
            return agv[:, tw // 2, :, 512 * (tw % 2):512 * (tw % 2) + 512]
        return f

    def obdst(ob, DH):
        def f(h, qc):
            r0 = 256 * (qc // 2) + DH * h
            c0 = 512 * (qc % 2)
            return ob[r0:r0 + DH, c0:c0 + 512]
        return f

    with ExitStack() as ex:
        names = ("sd", "sdx", "sdg", "sdw0", "sdw1", "sda0", "sda1",
                 "sdo0", "sdo1", "swa", "swb", "st", "sv", "sc", "sp", "bar",
                 "cc1", "cc2", "cc3", "cc4")
        sems = {n: ex.enter_context(nc.semaphore(n)) for n in names}
        coff_sb = ex.enter_context(nc.sbuf_tensor("coff_sb", [1, 1], I32))
        ro = ex.enter_context(nc.gpsimd.register("ro"))
        blk = ex.enter_context(nc.Block())
        S = Sched()

        ro2 = ex.enter_context(nc.gpsimd.register("ro2"))
        ws_g = WStream(S, nc, ex, "wsg", CC, 256)
        S.emit("gpsimd", lambda e: e.dma_start(out=coff_sb[:, :],
               in_=coff_d[:, :]), inc=("sdg", 16))
        S.emit("gpsimd", lambda e: e.reg_load(ro, coff_sb[0:1, 0:1]),
               waits=[("sdg", 16)])
        S.emit("gpsimd", lambda e: e.reg_add(ro2, ro, 128 * TLOC))

        with ExitStack() as exA, \
             nc.psum_tensor("A_ps_stats", [1, 2048], F32) as ps_stats, \
             nc.psum_tensor("A_ps_bc", [128, 2048], F32) as ps_bc:
            x_sb = exA.enter_context(nc.sbuf_tensor("A_x", [128, CC * 1024], F32))
            o_sb = exA.enter_context(nc.sbuf_tensor("A_o", [128, CC * 1024], BF16))
            oc_sb = exA.enter_context(nc.sbuf_tensor("A_oc", [128, 1], BF16))
            orw_sb = exA.enter_context(nc.sbuf_tensor("A_orw", [1, 128], BF16))
            lb = ln_bufs(nc, exA, "A_")
            S.emit("sync", lambda e: e.dma_start(
                   out=x_sb[:, :].rearrange("p (c t) -> p c t", t=TLOC),
                   in_=xin.ap().rearrange("(c p) t -> p c t", p=128)),
                   inc=("sd", 16))
            S.emit("sync", lambda e: e.dma_start(out=oc_sb[:, :], in_=oc_d[:, :]),
                   inc=("sd", 16))
            S.emit("sync", lambda e: e.dma_start(out=orw_sb[:, :], in_=orw_d[:, :]),
                   inc=("sd", 16))
            sd_c = S.get("sd")
            vfin = ln_emit(S, x_sb, o_sb, lb["xsq"], lb["t1"], lb["srow"],
                           lb["rowbf"], lb["abc"], lb["bbc"], ps_stats, ps_bc,
                           oc_sb, orw_sb, sd_c, wx=[("sd", sd_c)])
            S.emit("sync", lambda e: e.dma_start(
                   out=ln1b.ap().rearrange("(c p) t -> p c t", p=128),
                   in_=o_sb[:, :].rearrange("p (c t) -> p c t", t=TLOC)),
                   waits=[("sv", vfin)], inc=("sdo0", 16))
            S.emit("sync", lambda e: e.nop(), waits=[("sdo0", S.get("sdo0"))])
            S.flush(blk, sems)
        emit_barrier(S, sems)
        S.flush(blk, sems)

        S.emit("gpsimd", lambda e: e.collective_compute(
               "AllGather", ALU.bypass, replica_groups=RG,
               ins=[ln1b.ap().opt()], outs=[ag1.ap().opt()]), inc=("cc1", 1))

        emit_attn(nc, S, blk, sems, 0, T, "a0", agtile(ag1), obdst(ob0, 64),
                  cos0_d, sin0_d, masks_d, idn_d, orw_d, oc_d, wall0, None,
                  pm0_d, src_ready=[("cc1", 1)])
        S.emit("gpsimd", (lambda e: e.collective_compute(
               "AllGather", ALU.bypass, replica_groups=RG,
               ins=[ob0.ap().opt()], outs=[oag0.ap().opt()])),
               waits=[("sdo0", S.get("sdo0")), ("sdo1", S.get("sdo1"))],
               inc=("cc2", 1))
        S.flush(blk, sems)
        emit_barrier(S, sems)
        S.flush(blk, sems)

        def mk_o_load(oag):
            def f(dest):
                dmas = []
                for hh, reg in ((0, ro), (1, ro2)):
                    dmas.append((lambda e, hh=hh, reg=reg: e.dma_start(
                        out=bass.AP(dest, hh * TLOC,
                                    [[CC * TLOC, 128], [2 * TLOC, 8],
                                     [1, TLOC]]),
                        in_=bass.AP(oag, reg,
                                    [[TLOC, 128], [2048 * TLOC, 8],
                                     [1, TLOC]]))))
                return dmas
            return f
        o_load0 = mk_o_load(oag0)
        emit_proj(nc, S, blk, sems, 0, "p0", xin, o_load0, wo0_d, x1d, ln2d,
                  oc_d, orw_d, None, o_eng="gpsimd", o_sem="sdg",
                  o_wait=[("cc2", 1)], ws=ws_g)
        emit_barrier(S, sems)
        S.flush(blk, sems)

        emit_moe(nc, S, blk, sems, "m", x1d, ln2d, gw_d, gb_d, sgw_d, sdw_d,
                 upw_d, dnw_d, x2d, ln3b, oc_d, orw_d, idf_d)
        emit_barrier(S, sems)
        S.flush(blk, sems)

        S.emit("gpsimd", lambda e: e.collective_compute(
               "AllGather", ALU.bypass, replica_groups=RG,
               ins=[ln3b.ap().opt()], outs=[ag3.ap().opt()]), inc=("cc3", 1))

        emit_attn(nc, S, blk, sems, 1, T, "a1", agtile(ag3), obdst(ob1, 128),
                  cos1_d, sin1_d, masks_d, idn_d, orw_d, oc_d, wall1, bqkv_d,
                  pm1_d, src_ready=[("cc3", 1)])
        S.emit("gpsimd", (lambda e: e.collective_compute(
               "AllGather", ALU.bypass, replica_groups=RG,
               ins=[ob1.ap().opt()], outs=[oag1.ap().opt()])),
               waits=[("sdo0", S.get("sdo0")), ("sdo1", S.get("sdo1"))],
               inc=("cc4", 1))
        S.flush(blk, sems)
        emit_barrier(S, sems)
        S.flush(blk, sems)

        _scope(nc, "ph6_proj1")
        o_load1 = mk_o_load(oag1)
        emit_proj(nc, S, blk, sems, 1, "p1", x2d, o_load1, wo1_d, x3d, ln4d,
                  oc_d, orw_d, bo_d, o_eng="gpsimd", o_sem="sdg",
                  o_wait=[("cc4", 1)], ws=ws_g)
        emit_barrier(S, sems)
        S.flush(blk, sems)

        emit_ffn1(nc, S, blk, sems, "g", x3d, ln4d, wgu_d, wdn_d, xo, ws=ws_g)
        S.flush(blk, sems)
    return nc


def kernel_fused(per_core, inputs):
    cores = list(range(NCORES))
    ncf = _get("FUSED", build_fused)
    maps = []
    for r, d in enumerate(per_core):
        maps.append({
            "xin": d["xin"], "coff": np.array([[256 * r * TLOC]], np.int32),
            "wall0": d["wall0"], "wall1": d["wall1"], "bqkv1": d["bqkv1"],
            "pm0": d["pm0"], "pm1": d["pm1"],
            "wo0": d["wo0"], "wo1": d["wo1"],
            "bo": _cc(np.asarray(inputs["L1_attn_b_out"]).reshape(CC, 128).T
                      .astype(np.float32)),
            "gw": d["gw"], "gbbc": d["gbbc"], "sgw": d["sgw"], "sdw": d["sdw"],
            "upw": d["upw"].reshape(8 * D, 2816),
            "dnw": d["dnw"].reshape(8 * 1408, D),
            "wgu1": d["wgu1"], "wdn1": d["wdn1"],
            "cos0": d["cos0"], "sin0": d["sin0"],
            "cos1": d["cos1"], "sin1": d["sin1"],
            "masks": d["masks"], "idn_bf": d["idn_bf"], "idn_f32": d["idn_f32"],
            "ones_col": d["ones_col"], "ones_row": d["ones_row"]})
    res = run_bass_kernel_spmd(ncf, maps, core_ids=cores)
    out = np.zeros((1, T, D), dtype=np.float32)
    for r in cores:
        out[0, TLOC * r:TLOC * (r + 1), :] = \
            np.asarray(res.results[r]["xo"], dtype=np.float32).T
    return out
